# version 03
# baseline (speedup 1.0000x reference)
import numpy as np
from contextlib import ExitStack

import concourse.bass as bass
import concourse.mybir as mybir
from concourse.bass_utils import run_bass_kernel_spmd

F32 = mybir.dt.float32
BF16 = mybir.dt.bfloat16
U8 = mybir.dt.uint8
AF = mybir.ActivationFunctionType
ALU = mybir.AluOpType
AXX = mybir.AxisListType.X

NCORE = 8
B, S, C = 4, 2048, 4096
T = 1024
H, D, BS = 32, 128, 256
NB = T // BS
CCH = C // 128
TH = 512
FI = 11008
EPS = 1e-5


def _lattice_perm():
    l = np.arange(C)
    jb, rem = l // 256, l % 256
    par, p = rem // 128, rem % 128
    return jb * 256 + 2 * p + par


def _decay_consts():
    base = 1.0 / 2.0 ** (8.0 / H)
    expo = np.arange(H, dtype=np.float64) + 1.0
    factor = 1.0 - 1 / (31 + 1e-05) + 1e-05
    slope = base ** expo * factor
    r = np.arange(BS, dtype=np.float64) + 1.0
    qd = np.exp(-slope[:, None] * r[None, :])
    kd = np.exp(-slope[:, None] * (BS - r)[None, :])
    nm = r[:, None] - r[None, :]
    dg = np.exp(-slope[:, None, None] * nm[None])
    dg = np.where(nm[None] >= 0, dg, 0.0)
    bd = np.exp(-slope * BS)
    return (qd.astype(np.float32), kd.astype(np.float32),
            dg.astype(np.float32), bd.astype(np.float32))


def _u8_view(int32_arr):
    a = np.ascontiguousarray(int32_arr)
    return np.ascontiguousarray(a.view(np.uint8).reshape(a.shape + (4,))[..., 0])


def host_prep(inp):
    import ml_dtypes
    perm = _lattice_perm()
    shared = {}

    x = np.ascontiguousarray(inp["L0_attn_hidden_states"], dtype=np.float32)
    x_lat = np.ascontiguousarray(x[:, :, perm])
    per_core = []
    zeros_prev = np.zeros((T, C), np.float32)
    for c in range(NCORE):
        i, j = c // 2, c % 2
        d = {"xp": np.ascontiguousarray(x_lat[i, j * T:(j + 1) * T])}
        d["xq"] = np.ascontiguousarray(x_lat[i, 0:T]) if j == 1 else zeros_prev
        per_core.append(d)

    qkv_w = inp["L0_attn_qkv_w"]
    wt = np.ascontiguousarray(qkv_w.T[perm, :])
    wt = wt.reshape(CCH, 128, H, 3, 128)
    shared["qkvw"] = np.ascontiguousarray(wt.transpose(2, 3, 1, 0, 4)).reshape(H, 3, 128, CCH * 128)

    gt = np.ascontiguousarray(inp["L0_attn_gate_w"].T[perm, :])
    gt = gt.reshape(CCH, 128, CCH, 128)
    shared["gatew"] = np.ascontiguousarray(gt.transpose(2, 1, 0, 3)).reshape(CCH, 128, CCH * 128)

    ot = np.ascontiguousarray(inp["L0_attn_out_w"].T[:, perm])
    ot = ot.reshape(CCH, 128, CCH, 128)
    shared["outw"] = np.ascontiguousarray(ot.transpose(2, 1, 0, 3)).reshape(CCH, 128, CCH * 128)

    shared["normw"] = np.ascontiguousarray(inp["L0_attn_norm_w"].reshape(CCH, 128).T)

    qd, kd, dg, bd = _decay_consts()
    shared["qd"] = np.ascontiguousarray(
        np.broadcast_to(qd[:, None, :], (H, 128, BS)).transpose(1, 0, 2).reshape(128, H * BS)
    ).astype(ml_dtypes.bfloat16)
    shared["kd"] = np.ascontiguousarray(kd.reshape(H, 2, 128).transpose(2, 0, 1).reshape(128, H * 2))
    dgt = dg.transpose(0, 2, 1).reshape(H, 2, 128, BS).transpose(2, 0, 1, 3)
    shared["dt"] = np.ascontiguousarray(dgt.reshape(128, H * 2 * BS)).astype(ml_dtypes.bfloat16)
    shared["ident"] = np.eye(128, dtype=ml_dtypes.bfloat16)
    shared["ones1"] = np.ones((128, 1), dtype=ml_dtypes.bfloat16)
    shared["onesr"] = np.ones((1, 128), dtype=np.float32)

    for nm, key in (("gq", "L0_ffn_gate_Wq"), ("uq", "L0_ffn_up_Wq")):
        wq = _u8_view(inp[key])
        wqt = np.ascontiguousarray(wq.T)
        r = wqt.reshape(16, 128, FI // 256, 256)
        shared[nm] = np.ascontiguousarray(r.transpose(2, 1, 0, 3))
    dq = _u8_view(inp["L0_ffn_down_Wq"])
    dqt = np.ascontiguousarray(dq.T[:, perm])
    r = dqt.reshape(43, 128, 16, 256)
    shared["dq"] = np.ascontiguousarray(r.transpose(2, 0, 1, 3))

    def s_re(sc, nchunks, ncols, colperm=None):
        a = np.ascontiguousarray(sc.T)
        if colperm is not None:
            a = np.ascontiguousarray(a[:, colperm])
        re = a.reshape(nchunks, 4, ncols).transpose(1, 0, 2)
        return np.ascontiguousarray(re).reshape(4, -1), a
    shared["gs_re"], shared["gsT"] = s_re(inp["L0_ffn_gate_scale"], 16, FI)
    _, shared["gzT"] = s_re(inp["L0_ffn_gate_zero"], 16, FI)
    shared["us_re"], shared["usT"] = s_re(inp["L0_ffn_up_scale"], 16, FI)
    _, shared["uzT"] = s_re(inp["L0_ffn_up_zero"], 16, FI)
    shared["ds_re"], shared["dsT"] = s_re(inp["L0_ffn_down_scale"], 43, C, perm)
    _, shared["dzT"] = s_re(inp["L0_ffn_down_zero"], 43, C, perm)

    band = np.zeros((128, 304), dtype=ml_dtypes.bfloat16)
    for p in range(128):
        band[p, 172 + p // 32] = 1
    shared["band"] = band
    pat4 = np.zeros((4, 128), dtype=ml_dtypes.bfloat16)
    for p in range(128):
        pat4[p // 32, p] = 1
    shared["pat4"] = pat4
    return shared, per_core



class Bld:
    ENG = ("t", "v", "s", "g", "y")

    def __init__(self):
        self.q = {e: [] for e in self.ENG}
        self.cnt = {}

    def sem(self, *names):
        for n in names:
            self.cnt[n] = 0

    def emit(self, eng, f, inc=None, k=1):
        if inc is not None:
            self.cnt[inc] += k

            def fn(e, S, f=f, inc=inc, k=k):
                f(e, S).then_inc(S[inc], k)
            self.q[eng].append(fn)
            return self.cnt[inc]
        self.q[eng].append(lambda e, S, f=f: f(e, S))
        return None

    def wait(self, eng, sem, val):
        if val > 0:
            self.q[eng].append(lambda e, S, sem=sem, val=val: e.wait_ge(S[sem], val))

    def drain(self, eng):
        self.q[eng].append(lambda e, S: e.drain())

    def dma(self, eng, sem, f):
        self.wait(eng, sem, self.cnt[sem])
        return self.emit(eng, f, inc=sem, k=16)


def build_nc(dbg=(), n_heads=H, do_p05=True, do_p1=True):
    nc = bass.Bass("TRN2", target_bir_lowering=False)
    dbg = set(dbg)
    _, _, _, bd_all = _decay_consts()

    xp = nc.dram_tensor("xp", [T, C], F32, kind="ExternalInput")
    xq = nc.dram_tensor("xq", [T, C], F32, kind="ExternalInput")
    qkvw = nc.dram_tensor("qkvw", [H, 3, 128, CCH * 128], F32, kind="ExternalInput")
    gatew = nc.dram_tensor("gatew", [CCH, 128, CCH * 128], F32, kind="ExternalInput")
    outw = nc.dram_tensor("outw", [CCH, 128, CCH * 128], F32, kind="ExternalInput")
    normw = nc.dram_tensor("normw", [128, CCH], F32, kind="ExternalInput")
    qd_t = nc.dram_tensor("qd", [128, H * BS], BF16, kind="ExternalInput")
    kd_t = nc.dram_tensor("kd", [128, H * 2], F32, kind="ExternalInput")
    dt_t = nc.dram_tensor("dt", [128, H * 2 * BS], BF16, kind="ExternalInput")
    ident_t = nc.dram_tensor("ident", [128, 128], BF16, kind="ExternalInput")
    ones1_t = nc.dram_tensor("ones1", [128, 1], BF16, kind="ExternalInput")
    onesr_t = nc.dram_tensor("onesr", [1, 128], F32, kind="ExternalInput")
    gq_t = nc.dram_tensor("gq", [43, 128, 16, 256], U8, kind="ExternalInput")
    uq_t = nc.dram_tensor("uq", [43, 128, 16, 256], U8, kind="ExternalInput")
    dq_t = nc.dram_tensor("dq", [16, 43, 128, 256], U8, kind="ExternalInput")
    sc_in = {}
    for nm, shp in (("gs_re", [4, 16 * FI]), ("us_re", [4, 16 * FI]), ("ds_re", [4, 43 * C]),
                    ("gsT", [64, FI]), ("gzT", [64, FI]), ("usT", [64, FI]), ("uzT", [64, FI]),
                    ("dsT", [172, C]), ("dzT", [172, C])):
        sc_in[nm] = nc.dram_tensor(nm, shp, F32, kind="ExternalInput")
    band_t = nc.dram_tensor("band", [128, 304], BF16, kind="ExternalInput")
    pat4_t = nc.dram_tensor("pat4", [4, 128], BF16, kind="ExternalInput")

    out_t = nc.dram_tensor("out", [T, C], F32, kind="ExternalOutput")

    xbf = nc.dram_tensor("xbf", [T, C], BF16, kind="Internal")
    xqbf = nc.dram_tensor("xqbf", [T, C], BF16, kind="Internal")
    a2d = nc.dram_tensor("a2d", [C, T], BF16, kind="Internal")
    dpd = nc.dram_tensor("dpd", [C, T], BF16, kind="Internal")
    zs_g = nc.dram_tensor("zs_g", [64, FI], BF16, kind="Internal")
    zs_u = nc.dram_tensor("zs_u", [64, FI], BF16, kind="Internal")
    zs_d = nc.dram_tensor("zs_d", [172, C], BF16, kind="Internal")

    dbg_t = {}
    for name, shp, dt in (("attn", [C, T], F32), ("r1", [C, T], F32),
                          ("kvin", [128, H * 128], F32)):
        if name in dbg:
            dbg_t[name] = nc.dram_tensor("dbg_" + name, shp, dt, kind="ExternalOutput")

    b = Bld()
    b.sem("dc", "db", "dx", "ds0", "ds1", "ds2", "dd0", "dd1",
          "dw0", "dw1", "dfb", "dfs", "dfz", "dzt", "ddp", "dsa", "dtm", "dfo",
          "pe", "dv", "ac", "po", "fin")

    with ExitStack() as st:
        def sb(name, shape, dtype):
            return st.enter_context(nc.sbuf_tensor(name, shape, dtype))

        x_sb = sb("x_sb", [128, CCH * T], BF16)
        at_sb = sb("at_sb", [128, CCH * T], BF16)
        wsl = sb("wsl", [128, 3 * 4096], BF16)
        qkvt = sb("qkvt", [128, 3 * T], BF16)
        q2 = sb("q2", [128, T], BF16)
        ssb = sb("ssb", [128, 512], BF16)
        ktm = sb("ktm", [128, 256], BF16)
        vtm = sb("vtm", [128, 256], BF16)
        stf = sb("stf", [128, 2 * 128], F32)
        kvbf = sb("kvbf", [128, 128], BF16)
        gaux = sb("gaux", [128, 14848], BF16)
        kvin = gaux[:, 9216:9216 + H * 128]
        qdsb = gaux[:, 0:H * BS]
        kdsb = sb("kdsb", [128, H * 2], F32)
        dtsb = gaux[:, 8192:8192 + 1024]
        identb = sb("identb", [128, 128], BF16)
        ones1b = sb("ones1b", [128, 1], BF16)
        onesrb = sb("onesrb", [1, 128], F32)
        normwb = sb("normwb", [128, CCH], F32)
        fout = sb("fout", [128, 3072], F32)
        rows = fout[0:1, 0:2048]
        rwork = fout[0:1, 2048:3072]
        bandb = sb("bandb", [128, 304], BF16)
        pat4b = sb("pat4b", [4, 128], BF16)

        pbk = [st.enter_context(nc.psum_tensor(f"pb{i}", [128, 512], F32)) for i in range(7)]
        pbf = st.enter_context(nc.psum_tensor("pbf", [128, 1024], BF16))

        sems = {n: st.enter_context(nc.semaphore(n)) for n in b.cnt}

        b.emit("g", lambda e, S: e.dma_start(out=xbf[:, :], in_=xp[:, :]), inc="db", k=16)
        b.emit("g", lambda e, S: e.dma_start(out=xqbf[:, :], in_=xq[:, :]), inc="db", k=16)
        b.emit("g", lambda e, S: e.dma_start(out=qdsb[:, :], in_=qd_t[:, :]), inc="dc", k=16)
        b.emit("g", lambda e, S: e.dma_start(out=kdsb[:, :], in_=kd_t[:, :]), inc="dc", k=16)
        b.emit("g", lambda e, S: e.dma_start(out=identb[:, :], in_=ident_t[:, :]), inc="dc", k=16)
        b.emit("g", lambda e, S: e.dma_start(out=ones1b[:, :], in_=ones1_t[:, :]), inc="dc", k=16)
        b.emit("g", lambda e, S: e.dma_start(out=onesrb[:, :], in_=onesr_t[:, :]), inc="dc", k=16)
        b.emit("g", lambda e, S: e.dma_start(out=normwb[:, :], in_=normw[:, :]), inc="dc", k=16)
        b.emit("g", lambda e, S: e.dma_start(out=bandb[:, :], in_=band_t[:, :]), inc="dc", k=16)
        b.emit("g", lambda e, S: e.dma_start(out=pat4b[:, :], in_=pat4_t[:, :]), inc="dc", k=16)
        consts_done = b.cnt["dc"]

        b.wait("y", "db", 32)
        for cc in range(CCH):
            b.emit("y", lambda e, S, cc=cc: e.dma_start_transpose(
                x_sb[:, cc * T:(cc + 1) * T], xbf[:, cc * 128:(cc + 1) * 128]), inc="dx", k=16)
        for cc in range(CCH):
            b.emit("y", lambda e, S, cc=cc: e.dma_start_transpose(
                at_sb[:, cc * T:(cc + 1) * T], xqbf[:, cc * 128:(cc + 1) * 128]), inc="dx", k=16)
        x_loaded = b.cnt["dx"]

        NSLOT = 3
        slot_of = {}
        slab_ready = {}
        slab_guard = [0] * NSLOT
        next_slot = [0]

        def load_slab(h, z):
            sl = next_slot[0] % NSLOT
            next_slot[0] += 1
            slot_of[(h, z)] = sl
            b.wait("g", "pe", slab_guard[sl])
            sem = f"ds{sl}"
            m = b.dma("g", sem, lambda e, S, h=h, z=z, sl=sl: e.dma_start(
                out=wsl[:, sl * 4096:(sl + 1) * 4096], in_=qkvw[h, z, :, :]))
            slab_ready[(h, z)] = (sem, m)

        def qkv_matmuls(h, z, comp_idx, src):
            sl = slot_of[(h, z)]
            sem, m = slab_ready[(h, z)]
            b.wait("t", sem, m)
            b.wait("t", "dv", b.cnt["dv"])
            for th in range(2):
                bank = pbk[2 * comp_idx + th]
                for cc in range(CCH):
                    b.emit("t", lambda e, S, bank=bank, sl=sl, cc=cc, th=th, src=src:
                           e.matmul(bank[:, :],
                                    wsl[:, sl * 4096 + cc * 128: sl * 4096 + (cc + 1) * 128],
                                    src[:, cc * T + th * TH: cc * T + th * TH + TH],
                                    start=(cc == 0), stop=(cc == CCH - 1)),
                           inc="pe" if cc == CCH - 1 else None)
            pe_m = b.cnt["pe"]
            b.wait("s", "pe", pe_m)
            for th in range(2):
                bank = pbk[2 * comp_idx + th]
                b.emit("s", lambda e, S, bank=bank, comp_idx=comp_idx, th=th:
                       e.activation(qkvt[:, comp_idx * T + th * TH: comp_idx * T + th * TH + TH],
                                    bank[:, :], AF.Silu), inc="ac")
            slab_guard[sl] = pe_m
            return b.cnt["ac"]

        def scan_block(h, n, with_outputs, phase1):
            bd = float(bd_all[h])
            par = n % 2
            kcol = T + n * BS
            vcol = 2 * T + n * BS
            kvin_col = h * 128
            b.wait("t", "dv", b.cnt["dv"])
            for sub in range(2):
                b.emit("t", lambda e, S, kcol=kcol, sub=sub:
                       e.transpose(pbf[:, sub * 128:(sub + 1) * 128],
                                   qkvt[:, kcol + sub * 128: kcol + (sub + 1) * 128], identb[:, :]))
            for sub in range(2):
                b.emit("t", lambda e, S, vcol=vcol, sub=sub:
                       e.transpose(pbf[:, 256 + sub * 128: 256 + (sub + 1) * 128],
                                   qkvt[:, vcol + sub * 128: vcol + (sub + 1) * 128], identb[:, :]),
                       inc="pe" if sub == 1 else None)
            tr_done = b.cnt["pe"]
            if with_outputs:
                qcol = n * BS
                for sub in range(2):
                    b.emit("t", lambda e, S, kcol=kcol, qcol=qcol, sub=sub:
                           e.matmul(pbk[6][:, sub * 256:(sub + 1) * 256],
                                    qkvt[:, kcol + sub * 128: kcol + (sub + 1) * 128],
                                    qkvt[:, qcol: qcol + 256], start=True, stop=True),
                           inc="pe" if sub == 1 else None)
                s_done = b.cnt["pe"]
                b.wait("v", "pe", s_done)
                dslot = (h % 2) * 512
                for sub in range(2):
                    b.emit("v", lambda e, S, sub=sub, dslot=dslot:
                           e.tensor_mul(ssb[:, sub * 256:(sub + 1) * 256],
                                        pbk[6][:, sub * 256:(sub + 1) * 256],
                                        dtsb[:, dslot + sub * 256: dslot + (sub + 1) * 256]),
                           inc="dv" if sub == 1 else None)
            b.wait("v", "pe", tr_done)
            for sub in range(2):
                b.emit("v", lambda e, S, sub=sub, h=h:
                       e.tensor_scalar(ktm[:, sub * 128:(sub + 1) * 128],
                                       pbf[:, sub * 128:(sub + 1) * 128],
                                       kdsb[:, h * 2 + sub: h * 2 + sub + 1], None, ALU.mult))
            b.emit("v", lambda e, S: e.tensor_copy(vtm[:, :], pbf[:, 256:512]), inc="dv")
            ev_done = b.cnt["dv"]
            b.wait("t", "dv", ev_done)
            if with_outputs:
                qcol = n * BS
                for sub in range(2):
                    b.emit("t", lambda e, S, sub=sub:
                           e.matmul(pbk[1][:, 0:256], vtm[:, sub * 128:(sub + 1) * 128],
                                    ssb[:, sub * 256:(sub + 1) * 256],
                                    start=(sub == 0), stop=False))
                kv_ap = (kvin[:, kvin_col:kvin_col + 128] if n == 0 else kvbf[:, :])
                b.emit("t", lambda e, S, kv_ap=kv_ap, qcol=qcol:
                       e.matmul(pbk[1][:, 0:256], kv_ap, q2[:, qcol:qcol + 256],
                                start=False, stop=True), inc="pe")
                att_done = b.cnt["pe"]
                b.wait("v", "pe", att_done)
                b.emit("v", lambda e, S, h=h, n=n:
                       e.tensor_copy(at_sb[:, h * T + n * BS: h * T + (n + 1) * BS],
                                     pbk[1][:, 0:256]), inc="dv")
            for sub in range(2):
                b.emit("t", lambda e, S, sub=sub:
                       e.matmul(pbk[0][:, 0:128], ktm[:, sub * 128:(sub + 1) * 128],
                                vtm[:, sub * 128:(sub + 1) * 128],
                                start=(sub == 0), stop=(sub == 1)),
                       inc="pe" if sub == 1 else None)
            c_done = b.cnt["pe"]
            b.wait("v", "pe", c_done)
            b.drain("v")
            src_st = stf[:, par * 128:(par + 1) * 128]
            dst_st = stf[:, (1 - par) * 128:(2 - par) * 128]
            if n == 0:
                if phase1:
                    b.emit("v", lambda e, S, dst=dst_st, bd=bd, kvin_col=kvin_col:
                           e.scalar_tensor_tensor(dst, kvin[:, kvin_col:kvin_col + 128], bd,
                                                  pbk[0][:, 0:128], ALU.mult, ALU.add))
                else:
                    b.emit("v", lambda e, S, dst=dst_st: e.tensor_copy(dst, pbk[0][:, 0:128]))
            else:
                b.emit("v", lambda e, S, src=src_st, dst=dst_st, bd=bd:
                       e.scalar_tensor_tensor(dst, src, bd, pbk[0][:, 0:128], ALU.mult, ALU.add))
            b.drain("v")
            b.emit("v", lambda e, S, dst=dst_st: e.tensor_copy(kvbf[:, :], dst), inc="dv")

        for sT_n, zT_n, dst in (("gsT", "gzT", zs_g), ("usT", "uzT", zs_u)):
            for c0, c1 in ((0, 5504), (5504, FI)):
                w = c1 - c0
                b.wait("g", "dfb", b.cnt["dfb"])
                b.dma("g", "dfz", lambda e, S, sT_n=sT_n, c0=c0, c1=c1, w=w: e.dma_start(
                    out=wsl[0:64, 0:w], in_=sc_in[sT_n][:, c0:c1]))
                lm = b.dma("g", "dfz", lambda e, S, zT_n=zT_n, c0=c0, c1=c1, w=w: e.dma_start(
                    out=wsl[0:64, 5504:5504 + w], in_=sc_in[zT_n][:, c0:c1]))
                b.wait("g", "dfz", lm)
                b.emit("g", lambda e, S, w=w: e.tensor_mul(
                    wsl[0:64, 5504:5504 + w], wsl[0:64, 5504:5504 + w], wsl[0:64, 0:w]),
                    inc="po")
                b.wait("g", "po", b.cnt["po"])
                b.dma("g", "dfb", lambda e, S, dst=dst, c0=c0, c1=c1, w=w: e.dma_start(
                    out=dst[:, c0:c1], in_=wsl[0:64, 5504:5504 + w]))
        for r0_, r1_ in ((0, 128), (128, 172)):
            nrw = r1_ - r0_
            b.wait("g", "dfb", b.cnt["dfb"])
            b.dma("g", "dfz", lambda e, S, r0_=r0_, nrw=nrw: e.dma_start(
                out=wsl[0:nrw, 0:C], in_=sc_in["dsT"][r0_:r0_ + nrw, :]))
            lm = b.dma("g", "dfz", lambda e, S, r0_=r0_, nrw=nrw: e.dma_start(
                out=wsl[0:nrw, 4096:4096 + C], in_=sc_in["dzT"][r0_:r0_ + nrw, :]))
            b.wait("g", "dfz", lm)
            b.emit("g", lambda e, S, nrw=nrw: e.tensor_mul(
                wsl[0:nrw, 8192:8192 + C], wsl[0:nrw, 4096:4096 + C],
                wsl[0:nrw, 0:C]), inc="po")
            b.wait("g", "po", b.cnt["po"])
            b.dma("g", "dfb", lambda e, S, r0_=r0_, nrw=nrw: e.dma_start(
                out=zs_d[r0_:r0_ + nrw, :], in_=wsl[0:nrw, 8192:8192 + C]))
        b.wait("g", "dfb", b.cnt["dfb"])

        b.emit("g", lambda e, S: e.memset(kvin[:, :], 0), inc="po")
        b.wait("t", "dx", x_loaded)
        b.wait("t", "dc", consts_done)
        b.wait("v", "dc", consts_done)
        b.wait("v", "dx", x_loaded)
        for h in range(n_heads if do_p05 else 0):
            load_slab(h, 1)
            load_slab(h, 2)
            qkv_matmuls(h, 1, 1, at_sb)
            qkv_matmuls(h, 2, 2, at_sb)
            b.wait("t", "ac", b.cnt["ac"])
            for n in range(NB):
                scan_block(h, n, with_outputs=False, phase1=False)
            b.drain("v")
            fp = NB % 2
            b.emit("v", lambda e, S, h=h, fp=fp:
                   e.tensor_copy(kvin[:, h * 128:(h + 1) * 128], stf[:, fp * 128:(fp + 1) * 128]),
                   inc="dv")

        if "kvin" in dbg:
            b.wait("g", "dv", b.cnt["dv"])
            b.emit("g", lambda e, S: e.dma_start(out=dbg_t["kvin"][:, :], in_=kvin[:, :]),
                   inc="fin", k=16)

        for h in range(n_heads if do_p1 else 0):
            load_slab(h, 0)
            load_slab(h, 1)
            load_slab(h, 2)
            dslot = (h % 2) * 512
            dsem = f"dd{h % 2}"
            b.wait("y", "dv", b.cnt["dv"])
            d_ready = b.dma("y", dsem, lambda e, S, h=h, dslot=dslot: e.dma_start(
                out=dtsb[:, dslot:dslot + 512], in_=dt_t[:, h * 512:(h + 1) * 512]))
            qkv_matmuls(h, 0, 0, x_sb)
            qkv_matmuls(h, 1, 1, x_sb)
            qkv_matmuls(h, 2, 2, x_sb)
            ac_m = b.cnt["ac"]
            b.wait("v", "ac", ac_m)
            b.wait("v", dsem, d_ready)
            for n in range(NB):
                b.emit("v", lambda e, S, h=h, n=n:
                       e.tensor_mul(q2[:, n * BS:(n + 1) * BS],
                                    qkvt[:, n * BS:(n + 1) * BS],
                                    qdsb[:, h * BS:(h + 1) * BS]),
                       inc="dv" if n == NB - 1 else None)
            b.wait("t", "dv", b.cnt["dv"])
            for n in range(NB):
                scan_block(h, n, with_outputs=True, phase1=True)

        if "attn" in dbg:
            b.wait("g", "dv", b.cnt["dv"])
            for cc in range(CCH):
                b.emit("g", lambda e, S, cc=cc: e.dma_start(
                    out=dbg_t["attn"][cc * 128:(cc + 1) * 128, :],
                    in_=at_sb[:, cc * T:(cc + 1) * T]), inc="fin", k=16)

        SQ0, SQ1 = 8192, 9216
        GS0, GS1 = 8192, 9216
        RSTD_BC, MEAN_BC = 12288, 13312

        sq_guard = [0, 0]
        attn_done = b.cnt["dv"]
        b.wait("g", "dv", attn_done)
        for cc in range(CCH):
            sl = cc % 2
            b.wait("g", "pe", sq_guard[sl])
            b.emit("g", lambda e, S, cc=cc, sl=sl:
                   e.tensor_mul(wsl[:, (SQ0 if sl == 0 else SQ1):(SQ0 if sl == 0 else SQ1) + 1024],
                                at_sb[:, cc * T:(cc + 1) * T], at_sb[:, cc * T:(cc + 1) * T]),
                   inc="po")
            po_m = b.cnt["po"]
            b.wait("t", "po", po_m)
            for th in range(2):
                b.emit("t", lambda e, S, sl=sl, th=th, cc=cc:
                       e.matmul(pbk[4 + th][0:1, :], ones1b[:, :],
                                wsl[:, (SQ0 if sl == 0 else SQ1) + th * TH:
                                    (SQ0 if sl == 0 else SQ1) + th * TH + TH],
                                start=(cc == 0), stop=(cc == CCH - 1)),
                       inc="pe" if th == 1 else None)
            sq_guard[sl] = b.cnt["pe"]
        rms_mm = b.cnt["pe"]
        b.wait("v", "pe", rms_mm)
        b.emit("v", lambda e, S: e.tensor_copy(rows[0:1, 0:512], pbk[4][0:1, :]))
        b.emit("v", lambda e, S: e.tensor_copy(rows[0:1, 512:1024], pbk[5][0:1, :]))
        b.drain("v")
        b.emit("v", lambda e, S: e.tensor_scalar(rows[0:1, 0:1024], rows[0:1, 0:1024],
                                                 1.0 / C, EPS, ALU.mult, ALU.add), inc="dv")
        b.wait("s", "dv", b.cnt["dv"])
        b.emit("s", lambda e, S: e.activation(rows[0:1, 1024:2048], rows[0:1, 0:1024], AF.Sqrt),
               inc="ac")
        b.wait("v", "ac", b.cnt["ac"])
        b.emit("v", lambda e, S: e.reciprocal(rows[0:1, 0:1024], rows[0:1, 1024:2048]), inc="dv")
        b.wait("t", "dv", b.cnt["dv"])
        for th in range(2):
            b.emit("t", lambda e, S, th=th:
                   e.matmul(pbk[4 + th][:, :], onesrb[:, :], rows[0:1, th * TH:(th + 1) * TH],
                            start=True, stop=True), inc="pe" if th == 1 else None)
        b.wait("s", "pe", b.cnt["pe"])
        for th in range(2):
            b.emit("s", lambda e, S, th=th:
                   e.activation(gaux[:, RSTD_BC + th * TH: RSTD_BC + th * TH + TH],
                                pbk[4 + th][:, :], AF.Copy), inc="ac")
        rstd_bc_done = b.cnt["ac"]

        wslot_guard = [0, 0]
        gpsum_guard = [0, 0]
        gstage_guard = [0, 0]
        z_first = True
        for oc in range(CCH):
            sl = oc % 2
            wsem = f"dw{sl}"
            b.wait("g", "pe", wslot_guard[sl])
            wm = b.dma("g", wsem, lambda e, S, oc=oc, sl=sl: e.dma_start(
                out=wsl[:, sl * 4096:(sl + 1) * 4096], in_=gatew[oc, :, :]))
            b.wait("t", wsem, wm)
            b.wait("t", "ac", gpsum_guard[sl])
            for th in range(2):
                bank = pbk[2 * sl + th]
                for cc in range(CCH):
                    b.emit("t", lambda e, S, bank=bank, sl=sl, cc=cc, th=th:
                           e.matmul(bank[:, :],
                                    wsl[:, sl * 4096 + cc * 128: sl * 4096 + (cc + 1) * 128],
                                    x_sb[:, cc * T + th * TH: cc * T + th * TH + TH],
                                    start=(cc == 0), stop=(cc == CCH - 1)),
                           inc="pe" if cc == CCH - 1 else None)
            wslot_guard[sl] = b.cnt["pe"]
            b.wait("s", "pe", b.cnt["pe"])
            if z_first:
                b.wait("s", "ac", rstd_bc_done)
            b.wait("s", "dv", gstage_guard[sl])
            for th in range(2):
                b.emit("s", lambda e, S, sl=sl, th=th:
                       e.activation(wsl[:, (GS0 if sl == 0 else GS1) + th * TH:
                                        (GS0 if sl == 0 else GS1) + th * TH + TH],
                                    pbk[2 * sl + th][:, :], AF.Sigmoid), inc="ac")
            gpsum_guard[sl] = b.cnt["ac"]
            b.wait("v", "ac", b.cnt["ac"])
            if z_first:
                b.wait("v", "ac", rstd_bc_done)
                z_first = False
            b.drain("v")
            b.emit("v", lambda e, S, oc=oc:
                   e.tensor_mul(q2[:, :], at_sb[:, oc * T:(oc + 1) * T],
                                gaux[:, RSTD_BC:RSTD_BC + 1024]))
            b.drain("v")
            b.emit("v", lambda e, S, oc=oc, sl=sl:
                   e.scalar_tensor_tensor(at_sb[:, oc * T:(oc + 1) * T], q2[:, :],
                                          normwb[:, oc:oc + 1],
                                          wsl[:, (GS0 if sl == 0 else GS1):(GS0 if sl == 0 else GS1) + 1024],
                                          ALU.mult, ALU.mult), inc="dv")
            gstage_guard[sl] = b.cnt["dv"]

        a_guard = 0
        b.wait("t", "ac", b.cnt["ac"])
        b.wait("t", "dv", b.cnt["dv"])
        for oc in range(CCH):
            sl = oc % 2
            wsem = f"dw{sl}"
            b.wait("g", "pe", wslot_guard[sl])
            wm = b.dma("g", wsem, lambda e, S, oc=oc, sl=sl: e.dma_start(
                out=wsl[:, sl * 4096:(sl + 1) * 4096], in_=outw[oc, :, :]))
            b.wait("t", wsem, wm)
            b.wait("t", "dv", a_guard)
            for th in range(2):
                for cc in range(CCH):
                    b.emit("t", lambda e, S, sl=sl, cc=cc, th=th:
                           e.matmul(pbk[th][:, :],
                                    wsl[:, sl * 4096 + cc * 128: sl * 4096 + (cc + 1) * 128],
                                    at_sb[:, cc * T + th * TH: cc * T + th * TH + TH],
                                    start=(cc == 0), stop=(cc == CCH - 1)),
                           inc="pe" if cc == CCH - 1 else None)
            wslot_guard[sl] = b.cnt["pe"]
            b.wait("v", "pe", b.cnt["pe"])
            for th in range(2):
                b.emit("v", lambda e, S, oc=oc, th=th:
                       e.tensor_add(x_sb[:, oc * T + th * TH: oc * T + th * TH + TH],
                                    x_sb[:, oc * T + th * TH: oc * T + th * TH + TH],
                                    pbk[th][:, :]), inc="dv" if th == 1 else None)
            a_guard = b.cnt["dv"]
            b.wait("g", "dv", a_guard)
            b.wait("g", "pe", sq_guard[sl])
            b.emit("g", lambda e, S, oc=oc, sl=sl:
                   e.tensor_mul(gaux[:, sl * 1024:(sl + 1) * 1024],
                                x_sb[:, oc * T:(oc + 1) * T], x_sb[:, oc * T:(oc + 1) * T]),
                   inc="po")
            b.wait("t", "po", b.cnt["po"])
            for th in range(2):
                b.emit("t", lambda e, S, oc=oc, th=th:
                       e.matmul(pbk[4 + th][0:1, :], ones1b[:, :],
                                x_sb[:, oc * T + th * TH: oc * T + th * TH + TH],
                                start=(oc == 0), stop=(oc == CCH - 1)))
            for th in range(2):
                b.emit("t", lambda e, S, oc=oc, sl=sl, th=th:
                       e.matmul(pbk[2 + th][0:1, :], ones1b[:, :],
                                gaux[:, (0 if sl == 0 else 1024) + th * TH:
                                     (0 if sl == 0 else 1024) + th * TH + TH],
                                start=(oc == 0), stop=(oc == CCH - 1)),
                       inc="pe" if th == 1 else None)
            sq_guard[sl] = b.cnt["pe"]
        ln1_mm = b.cnt["pe"]
        b.wait("v", "pe", ln1_mm)
        b.emit("v", lambda e, S: e.tensor_copy(rows[0:1, 0:512], pbk[4][0:1, :]))
        b.emit("v", lambda e, S: e.tensor_copy(rows[0:1, 512:1024], pbk[5][0:1, :]))
        b.emit("v", lambda e, S: e.tensor_copy(rows[0:1, 1024:1536], pbk[2][0:1, :]))
        b.emit("v", lambda e, S: e.tensor_copy(rows[0:1, 1536:2048], pbk[3][0:1, :]))
        b.drain("v")
        b.emit("v", lambda e, S: e.tensor_scalar(rows[0:1, 0:1024], rows[0:1, 0:1024],
                                                 1.0 / C, None, ALU.mult))
        b.drain("v")
        b.emit("v", lambda e, S: e.tensor_mul(rwork, rows[0:1, 0:1024],
                                              rows[0:1, 0:1024]))
        b.drain("v")
        b.emit("v", lambda e, S: e.scalar_tensor_tensor(rows[0:1, 1024:2048], rows[0:1, 1024:2048],
                                                        1.0 / C, rwork,
                                                        ALU.mult, ALU.subtract))
        b.drain("v")
        b.emit("v", lambda e, S: e.tensor_scalar(rows[0:1, 1024:2048], rows[0:1, 1024:2048],
                                                 EPS, None, ALU.add), inc="dv")
        b.wait("s", "dv", b.cnt["dv"])
        b.emit("s", lambda e, S: e.activation(rwork, rows[0:1, 1024:2048], AF.Sqrt),
               inc="ac")
        b.wait("v", "ac", b.cnt["ac"])
        b.emit("v", lambda e, S: e.reciprocal(rows[0:1, 1024:2048], rwork), inc="dv")
        b.wait("t", "dv", b.cnt["dv"])
        for th in range(2):
            b.emit("t", lambda e, S, th=th:
                   e.matmul(pbk[4 + th][:, :], onesrb[:, :], rows[0:1, th * TH:(th + 1) * TH],
                            start=True, stop=True))
        for th in range(2):
            b.emit("t", lambda e, S, th=th:
                   e.matmul(pbk[2 + th][:, :], onesrb[:, :], rows[0:1, 1024 + th * TH:1024 + (th + 1) * TH],
                            start=True, stop=True), inc="pe" if th == 1 else None)
        b.wait("s", "pe", b.cnt["pe"])
        for th in range(2):
            b.emit("s", lambda e, S, th=th:
                   e.activation(gaux[:, MEAN_BC + th * TH: MEAN_BC + th * TH + TH],
                                pbk[4 + th][:, :], AF.Copy))
        for th in range(2):
            b.emit("s", lambda e, S, th=th:
                   e.activation(gaux[:, RSTD_BC + th * TH: RSTD_BC + th * TH + TH],
                                pbk[2 + th][:, :], AF.Copy), inc="ac")
        ln1_bc = b.cnt["ac"]
        b.wait("v", "ac", ln1_bc)
        for cc in range(CCH):
            b.drain("v")
            b.emit("v", lambda e, S, cc=cc:
                   e.tensor_sub(q2[:, :], x_sb[:, cc * T:(cc + 1) * T],
                                gaux[:, MEAN_BC:MEAN_BC + 1024]))
            b.drain("v")
            b.emit("v", lambda e, S, cc=cc:
                   e.tensor_mul(x_sb[:, cc * T:(cc + 1) * T], q2[:, :],
                                gaux[:, RSTD_BC:RSTD_BC + 1024]), inc="dv")
        r1_done = b.cnt["dv"]

        if "r1" in dbg:
            b.wait("g", "dv", r1_done)
            for cc in range(CCH):
                b.emit("g", lambda e, S, cc=cc: e.dma_start(
                    out=dbg_t["r1"][cc * 128:(cc + 1) * 128, :],
                    in_=x_sb[:, cc * T:(cc + 1) * T]), inc="fin", k=16)

        HI_T, LO_T, S_SL = 0, 4096, 8192
        XSUM, HSUM = 12288, 13312
        ZSD_T, ZSG_T = 14336, 14592
        dpo = qkvt[:, 2048:3072]

        b.wait("t", "dv", r1_done)
        for cc in range(CCH):
            jc = cc // 2
            for th in range(2):
                b.emit("t", lambda e, S, cc=cc, jc=jc, th=th:
                       e.matmul(pbk[th][:, :], bandb[:, 172 - 4 * jc: 300 - 4 * jc],
                                x_sb[:, cc * T + th * TH: cc * T + th * TH + TH],
                                start=(cc == 0), stop=(cc == CCH - 1)),
                       inc="pe" if (cc == CCH - 1 and th == 1) else None)
        b.wait("v", "pe", b.cnt["pe"])
        b.emit("v", lambda e, S: e.tensor_scalar(gaux[0:64, XSUM:XSUM + 512],
                                                 pbk[0][0:64, :], -1.0, None, ALU.mult))
        b.emit("v", lambda e, S: e.tensor_scalar(gaux[0:64, XSUM + 512:XSUM + 1024],
                                                 pbk[1][0:64, :], -1.0, None, ALU.mult),
               inc="dv")
        xsum_done = b.cnt["dv"]

        PH_R = [(0, 15), (15, 29), (29, 43)]
        H135 = 135.53125
        tiles_guard = 0
        tiles_guard_po = 0
        bank_guard_ac = 0
        bank_guard_dv = xsum_done
        for ph, (rr0, rr1) in enumerate(PH_R):
            nr = rr1 - rr0
            for r in range(rr0, rr1):
                for X in ("g", "u"):
                    wq_src = gq_t if X == "g" else uq_t
                    sre = sc_in["gs_re"] if X == "g" else sc_in["us_re"]
                    zsd = zs_g if X == "g" else zs_u
                    b.wait("g", "dv", b.cnt["dv"])
                    bm = b.dma("g", "dfb", lambda e, S, wq_src=wq_src, r=r: e.dma_start(
                        out=wsl[:, 0:4096],
                        in_=wq_src[r].rearrange("p cb i -> p (cb i)")))
                    b.wait("g", "dv", tiles_guard)
                    b.wait("g", "po", tiles_guard_po)
                    b.wait("g", "dfs", b.cnt["dfs"])
                    for gs4 in range(4):
                        b.emit("g", lambda e, S, sre=sre, r=r, gs4=gs4: e.dma_start(
                            out=gaux[gs4 * 32:(gs4 + 1) * 32, S_SL:S_SL + 4096],
                            in_=bass.AP(sre, gs4 * 16 * FI + 256 * r,
                                        [[0, 32], [FI, 16], [1, 256]])), inc="dfs", k=16)
                    sm = b.cnt["dfs"]
                    b.wait("y", "pe", b.cnt["pe"])
                    zm = b.dma("y", "dzt", lambda e, S, zsd=zsd, r=r: e.dma_start(
                        out=gaux[0:64, ZSG_T:ZSG_T + 256], in_=zsd[0:64, 256 * r:256 * r + 256]))
                    b.wait("s", "dfb", bm)
                    b.wait("s", "dv", tiles_guard)
                    b.emit("s", lambda e, S: e.activation(wsl[:, 4096:8192], wsl[:, 0:4096],
                                                          AF.Copy, bias=H135, scale=0.0625),
                           inc="ac")
                    b.wait("v", "ac", b.cnt["ac"])
                    b.emit("v", lambda e, S: e.tensor_scalar(wsl[:, 8192:12288], wsl[:, 4096:8192],
                                                             136.0, None, ALU.subtract), inc="dv")
                    b.drain("v")
                    b.emit("v", lambda e, S: e.scalar_tensor_tensor(wsl[:, 4096:8192],
                                                                    wsl[:, 8192:12288], -16.0,
                                                                    wsl[:, 0:4096],
                                                                    ALU.mult, ALU.add))
                    b.wait("v", "dfs", sm)
                    b.drain("v")
                    b.emit("v", lambda e, S: e.tensor_mul(gaux[:, LO_T:LO_T + 4096],
                                                          wsl[:, 4096:8192],
                                                          gaux[:, S_SL:S_SL + 4096]), inc="dv")
                    b.wait("g", "dv", b.cnt["dv"] - 1)
                    b.wait("g", "dfs", sm)
                    b.emit("g", lambda e, S: e.tensor_mul(gaux[:, HI_T:HI_T + 4096],
                                                          wsl[:, 8192:12288],
                                                          gaux[:, S_SL:S_SL + 4096]), inc="po")
                    tiles_guard = b.cnt["dv"]
                    tiles_guard_po = b.cnt["po"]
                    b.wait("t", "dv", tiles_guard)
                    b.wait("t", "po", tiles_guard_po)
                    b.wait("t", "dzt", zm)
                    if X == "g":
                        b.wait("t", "dv", bank_guard_dv)
                    else:
                        b.wait("t", "ac", bank_guard_ac)
                    for par in range(2):
                        for th in range(2):
                            bank = pbk[2 * par + th]
                            b.emit("t", lambda e, S, bank=bank, par=par, th=th:
                                   e.matmul(bank[:, :],
                                            gaux[0:64, ZSG_T + par:ZSG_T + 256:2],
                                            gaux[0:64, XSUM + th * TH:XSUM + th * TH + TH],
                                            start=True, stop=False))
                            for cb in range(16):
                                for cpar in range(2):
                                    src_t = HI_T if cpar == 0 else LO_T
                                    cc = 2 * cb + cpar
                                    last = (cb == 15 and cpar == 1)
                                    b.emit("t", lambda e, S, bank=bank, src_t=src_t, cb=cb,
                                           par=par, cc=cc, th=th, last=last:
                                           e.matmul(bank[:, :],
                                                    gaux[:, src_t + cb * 256 + par: src_t + (cb + 1) * 256: 2],
                                                    x_sb[:, cc * T + th * TH: cc * T + th * TH + TH],
                                                    start=False, stop=last),
                                           inc="pe" if last else None)
                    if X == "g":
                        b.wait("s", "pe", b.cnt["pe"])
                        for par in range(2):
                            for th in range(2):
                                b.emit("s", lambda e, S, par=par, th=th:
                                       e.activation(qkvt[:, par * 1024 + th * TH: par * 1024 + th * TH + TH],
                                                    pbk[2 * par + th][:, :], AF.Silu), inc="ac")
                        bank_guard_ac = b.cnt["ac"]
                    else:
                        b.wait("v", "pe", b.cnt["pe"])
                        for par in range(2):
                            idx = 2 * (r - rr0) + par
                            for th in range(2):
                                b.emit("v", lambda e, S, par=par, th=th, idx=idx:
                                       e.tensor_mul(at_sb[:, idx * T + th * TH: idx * T + th * TH + TH],
                                                    qkvt[:, par * 1024 + th * TH: par * 1024 + th * TH + TH],
                                                    pbk[2 * par + th][:, :]),
                                       inc="dv" if th == 1 else None)
                        bank_guard_dv = b.cnt["dv"]
                        b.wait("t", "dv", bank_guard_dv)
                        for par in range(2):
                            idx = 2 * (r - rr0) + par
                            lr = r - rr0
                            for th in range(2):
                                b.emit("t", lambda e, S, idx=idx, lr=lr, th=th, nr=nr:
                                       e.matmul(pbk[4 + th][:, :],
                                                bandb[:, 172 - 4 * lr: 300 - 4 * lr],
                                                at_sb[:, idx * T + th * TH: idx * T + th * TH + TH],
                                                start=(idx == 0), stop=(idx == 2 * nr - 1)),
                                       inc="pe" if th == 1 else None)
            b.wait("v", "pe", b.cnt["pe"])
            b.emit("v", lambda e, S, nr=nr: e.tensor_scalar(gaux[0:4 * nr, HSUM:HSUM + 512],
                                                            pbk[4][0:4 * nr, :], -1.0, None, ALU.mult))
            b.emit("v", lambda e, S, nr=nr: e.tensor_scalar(gaux[0:4 * nr, HSUM + 512:HSUM + 1024],
                                                            pbk[5][0:4 * nr, :], -1.0, None, ALU.mult),
               inc="dv")
            hsum_done = b.cnt["dv"]

            dpp_guard = [0, 0]
            for how in range(16):
                b.wait("g", "dv", b.cnt["dv"])
                bm = b.dma("g", "dfb", lambda e, S, how=how, rr0=rr0, nr=nr: e.dma_start(
                    out=wsl[:, 0:nr * 256].rearrange("p (ib i) -> p ib i", ib=nr),
                    in_=dq_t[how, rr0:rr0 + nr].rearrange("ib p i -> p ib i")))
                b.wait("g", "dv", tiles_guard)
                b.wait("g", "po", tiles_guard_po)
                b.wait("g", "dfs", b.cnt["dfs"])
                for gs4 in range(4):
                    b.emit("g", lambda e, S, how=how, rr0=rr0, nr=nr, gs4=gs4: e.dma_start(
                        out=gaux[gs4 * 32:(gs4 + 1) * 32, S_SL:S_SL + nr * 256],
                        in_=bass.AP(sc_in["ds_re"], gs4 * 43 * C + rr0 * C + 256 * how,
                                    [[0, 32], [C, nr], [1, 256]])), inc="dfs", k=16)
                sm = b.cnt["dfs"]
                b.wait("y", "pe", b.cnt["pe"])
                zm = b.dma("y", "dzt", lambda e, S, how=how, rr0=rr0, nr=nr: e.dma_start(
                    out=gaux[0:4 * nr, ZSD_T:ZSD_T + 256],
                    in_=zs_d[4 * rr0:4 * rr0 + 4 * nr, 256 * how:256 * how + 256]))
                b.wait("s", "dfb", bm)
                b.wait("s", "dv", tiles_guard)
                b.emit("s", lambda e, S, nr=nr: e.activation(wsl[:, 4096:4096 + nr * 256],
                                                             wsl[:, 0:nr * 256],
                                                             AF.Copy, bias=H135, scale=0.0625),
                       inc="ac")
                b.wait("v", "ac", b.cnt["ac"])
                b.emit("v", lambda e, S, nr=nr: e.tensor_scalar(wsl[:, 8192:8192 + nr * 256],
                                                                wsl[:, 4096:4096 + nr * 256],
                                                                136.0, None, ALU.subtract), inc="dv")
                b.drain("v")
                b.emit("v", lambda e, S, nr=nr: e.scalar_tensor_tensor(
                    wsl[:, 4096:4096 + nr * 256], wsl[:, 8192:8192 + nr * 256], -16.0,
                    wsl[:, 0:nr * 256], ALU.mult, ALU.add))
                b.wait("v", "dfs", sm)
                b.drain("v")
                b.emit("v", lambda e, S, nr=nr: e.tensor_mul(gaux[:, LO_T:LO_T + nr * 256],
                                                             wsl[:, 4096:4096 + nr * 256],
                                                             gaux[:, S_SL:S_SL + nr * 256]), inc="dv")
                b.wait("g", "dv", b.cnt["dv"] - 1)
                b.wait("g", "dfs", sm)
                b.emit("g", lambda e, S, nr=nr: e.tensor_mul(gaux[:, HI_T:HI_T + nr * 256],
                                                             wsl[:, 8192:8192 + nr * 256],
                                                             gaux[:, S_SL:S_SL + nr * 256]), inc="po")
                tiles_guard = b.cnt["dv"]
                tiles_guard_po = b.cnt["po"]
                for hsub in range(2):
                    cc_o = 2 * how + hsub
                    pp = (2 * how + hsub) % 2
                    banks = (pbk[0], pbk[1]) if pp == 0 else (pbk[2], pbk[3])
                    if ph > 0:
                        b.wait("y", "dv", dpp_guard[hsub])
                        dm = b.dma("y", "ddp", lambda e, S, cc_o=cc_o, hsub=hsub: e.dma_start(
                            out=qkvt[:, hsub * 1024:(hsub + 1) * 1024],
                            in_=dpd[cc_o * 128:(cc_o + 1) * 128, :]))
                    b.wait("t", "dv", tiles_guard)
                    b.wait("t", "po", tiles_guard_po)
                    b.wait("t", "dzt", zm)
                    b.wait("t", "dv", bank_guard_dv)
                    for th in range(2):
                        bank = banks[th]
                        b.emit("t", lambda e, S, bank=bank, hsub=hsub, th=th, nr=nr:
                               e.matmul(bank[:, :], gaux[0:4 * nr, ZSD_T + hsub * 128:ZSD_T + (hsub + 1) * 128],
                                        gaux[0:4 * nr, HSUM + th * TH:HSUM + th * TH + TH],
                                        start=True, stop=False))
                        for k in range(nr):
                            for par in range(2):
                                src_t = HI_T if par == 0 else LO_T
                                idx2 = 2 * k + par
                                b.emit("t", lambda e, S, bank=bank, src_t=src_t, k=k, hsub=hsub,
                                       idx2=idx2, th=th, last=(k == nr - 1 and par == 1):
                                       e.matmul(bank[:, :],
                                                gaux[:, src_t + k * 256 + hsub * 128: src_t + k * 256 + hsub * 128 + 128],
                                                at_sb[:, idx2 * T + th * TH: idx2 * T + th * TH + TH],
                                                start=False, stop=last),
                                       inc="pe" if (k == nr - 1 and par == 1) else None)
                    pe_down = b.cnt["pe"]
                    b.wait("v", "pe", pe_down)
                    b.wait("v", "dsa", b.cnt["dsa"])
                    if ph > 0:
                        b.wait("v", "ddp", dm)
                    for th in range(2):
                        last = (ph != 2 and th == 1)
                        if ph == 0:
                            b.emit("v", lambda e, S, th=th, banks=banks:
                                   e.tensor_copy(dpo[:, th * TH: th * TH + TH],
                                                 banks[th][:, :]), inc="dv" if last else None)
                        else:
                            b.emit("v", lambda e, S, th=th, hsub=hsub, banks=banks:
                                   e.tensor_add(dpo[:, th * TH: th * TH + TH],
                                                qkvt[:, hsub * 1024 + th * TH: hsub * 1024 + th * TH + TH],
                                                banks[th][:, :]), inc="dv" if last else None)
                    if ph == 2:
                        b.drain("v")
                        for th in range(2):
                            b.emit("v", lambda e, S, th=th, cc_o=cc_o:
                                   e.tensor_add(dpo[:, th * TH: th * TH + TH],
                                                dpo[:, th * TH: th * TH + TH],
                                                x_sb[:, cc_o * T + th * TH: cc_o * T + th * TH + TH]),
                                   inc="dv" if th == 1 else None)
                    bank_guard_dv = b.cnt["dv"]
                    dpp_guard[hsub] = b.cnt["dv"]
                    b.wait("g", "dv", b.cnt["dv"])
                    dst_d = a2d if ph == 2 else dpd
                    b.dma("g", "dsa", lambda e, S, dst_d=dst_d, cc_o=cc_o: e.dma_start(
                        out=dst_d[cc_o * 128:(cc_o + 1) * 128, :],
                        in_=dpo[:, 0:1024]))

        a2_stores = b.cnt["dsa"]
        b.wait("y", "dsa", a2_stores)
        tm_guard = [0, 0]
        sq_fin_guard = 0
        for tc in range(8):
            sl = tc % 2
            b.wait("y", "dv", tm_guard[sl])
            tm_m = b.dma("y", "dtm", lambda e, S, tc=tc, sl=sl: e.dma_start_transpose(
                gaux[:, sl * 4096:(sl + 1) * 4096], a2d[:, tc * 128:(tc + 1) * 128]))
            ro = 8 * sl
            b.wait("v", "dtm", tm_m)
            b.emit("v", lambda e, S, sl=sl, ro=ro: e.tensor_reduce(
                stf[:, ro:ro + 1], gaux[:, sl * 4096:(sl + 1) * 4096], AXX, ALU.add))
            b.wait("s", "dtm", tm_m)
            b.wait("s", "dv", sq_fin_guard)
            b.emit("s", lambda e, S, sl=sl: e.activation(
                gaux[:, 8192:12288], gaux[:, sl * 4096:(sl + 1) * 4096], AF.Square), inc="ac")
            b.wait("v", "ac", b.cnt["ac"])
            b.emit("v", lambda e, S, ro=ro: e.tensor_reduce(
                stf[:, ro + 1:ro + 2], gaux[:, 8192:12288], AXX, ALU.add), inc="dv")
            sq_fin_guard = b.cnt["dv"]
            b.drain("v")
            b.emit("v", lambda e, S, ro=ro: e.tensor_scalar(
                stf[:, ro + 2:ro + 3], stf[:, ro:ro + 1], 1.0 / C, None, ALU.mult))
            b.drain("v")
            b.emit("v", lambda e, S, ro=ro: e.tensor_mul(stf[:, ro + 3:ro + 4],
                                                         stf[:, ro + 2:ro + 3], stf[:, ro + 2:ro + 3]))
            b.drain("v")
            b.emit("v", lambda e, S, ro=ro: e.scalar_tensor_tensor(
                stf[:, ro + 4:ro + 5], stf[:, ro + 1:ro + 2], 1.0 / C, stf[:, ro + 3:ro + 4],
                ALU.mult, ALU.subtract))
            b.drain("v")
            b.emit("v", lambda e, S, ro=ro: e.tensor_scalar(
                stf[:, ro + 4:ro + 5], stf[:, ro + 4:ro + 5], EPS, None, ALU.add), inc="dv")
            b.wait("s", "dv", b.cnt["dv"])
            b.emit("s", lambda e, S, ro=ro: e.activation(stf[:, ro + 5:ro + 6],
                                                         stf[:, ro + 4:ro + 5], AF.Sqrt), inc="ac")
            b.wait("v", "ac", b.cnt["ac"])
            b.emit("v", lambda e, S, ro=ro: e.reciprocal(stf[:, ro + 6:ro + 7],
                                                         stf[:, ro + 5:ro + 6]))
            b.drain("v")
            for qv in range(4):
                fsl = qv % 2
                b.wait("v", "dfo", max(0, b.cnt["dfo"] - 16))
                src_ap = gaux[:, sl * 4096 + qv * 1024: sl * 4096 + (qv + 1) * 1024]
                b.emit("v", lambda e, S, src_ap=src_ap, ro=ro, fsl=fsl:
                       e.tensor_scalar(
                           fout[:, fsl * 1024:(fsl + 1) * 1024].rearrange(
                               "t (jb p par) -> t jb par p", jb=4, p=128, par=2),
                           src_ap.rearrange("t (jb par p) -> t jb par p", jb=4, par=2, p=128),
                           stf[:, ro + 2:ro + 3], stf[:, ro + 6:ro + 7],
                           ALU.subtract, ALU.mult), inc="dv")
                vm = b.cnt["dv"]
                b.wait("y", "dv", vm)
                b.dma("y", "dfo", lambda e, S, tc=tc, qv=qv, fsl=fsl: e.dma_start(
                    out=out_t[tc * 128:(tc + 1) * 128, qv * 1024:(qv + 1) * 1024],
                    in_=fout[:, fsl * 1024:(fsl + 1) * 1024]))
            tm_guard[sl] = b.cnt["dv"]

        b.wait("g", "dfo", b.cnt["dfo"])
        b.wait("g", "fin", b.cnt["fin"])

        with nc.Block() as blk:
            @blk.tensor
            def _(t):
                for fn in b.q["t"]:
                    fn(t, sems)

            @blk.vector
            def _(v):
                for fn in b.q["v"]:
                    fn(v, sems)

            @blk.scalar
            def _(s):
                for fn in b.q["s"]:
                    fn(s, sems)

            @blk.gpsimd
            def _(g):
                for fn in b.q["g"]:
                    fn(g, sems)

            @blk.sync
            def _(y):
                for fn in b.q["y"]:
                    fn(y, sems)

    return nc


def kernel(**inputs):
    shared, per_core = host_prep(inputs)
    nc = build_nc()
    in_maps = [dict(shared, **pc) for pc in per_core]
    res = run_bass_kernel_spmd(nc, in_maps, core_ids=list(range(NCORE)))
    out = np.empty((B, S, C), np.float32)
    for c in range(NCORE):
        i, j = c // 2, c % 2
        out[i, j * T:(j + 1) * T, :] = res.results[c]["out"]
    return out


# version 06
# speedup vs baseline: 1.2973x; 1.2973x over previous
import numpy as np
from contextlib import ExitStack

import concourse.bass as bass
import concourse.mybir as mybir
from concourse.bass_utils import run_bass_kernel_spmd

F32 = mybir.dt.float32
BF16 = mybir.dt.bfloat16
U8 = mybir.dt.uint8
AF = mybir.ActivationFunctionType
ALU = mybir.AluOpType
AXX = mybir.AxisListType.X

NCORE = 8
B, S, C = 4, 2048, 4096
T = 1024
H, D, BS = 32, 128, 256
NB = T // BS
CCH = C // 128
TH = 512
FI = 11008
EPS = 1e-5


def _lattice_perm():
    l = np.arange(C)
    jb, rem = l // 256, l % 256
    par, p = rem // 128, rem % 128
    return jb * 256 + 2 * p + par


def _decay_consts():
    base = 1.0 / 2.0 ** (8.0 / H)
    expo = np.arange(H, dtype=np.float64) + 1.0
    factor = 1.0 - 1 / (31 + 1e-05) + 1e-05
    slope = base ** expo * factor
    r = np.arange(BS, dtype=np.float64) + 1.0
    qd = np.exp(-slope[:, None] * r[None, :])
    kd = np.exp(-slope[:, None] * (BS - r)[None, :])
    nm = r[:, None] - r[None, :]
    dg = np.exp(-slope[:, None, None] * nm[None])
    dg = np.where(nm[None] >= 0, dg, 0.0)
    bd = np.exp(-slope * BS)
    return (qd.astype(np.float32), kd.astype(np.float32),
            dg.astype(np.float32), bd.astype(np.float32))


def _u8_view(int32_arr):
    a = np.ascontiguousarray(int32_arr)
    return np.ascontiguousarray(a.view(np.uint8).reshape(a.shape + (4,))[..., 0])


def host_prep(inp):
    import ml_dtypes
    perm = _lattice_perm()
    shared = {}

    x = np.ascontiguousarray(inp["L0_attn_hidden_states"], dtype=np.float32)
    x_lat = np.ascontiguousarray(x[:, :, perm])
    per_core = []
    zeros_prev = np.zeros((T, C), np.float32)
    for c in range(NCORE):
        i, j = c // 2, c % 2
        d = {"xp": np.ascontiguousarray(x_lat[i, j * T:(j + 1) * T])}
        d["xq"] = np.ascontiguousarray(x_lat[i, 0:T]) if j == 1 else zeros_prev
        per_core.append(d)

    qkv_w = inp["L0_attn_qkv_w"]
    wt = np.ascontiguousarray(qkv_w.T[perm, :])
    wt = wt.reshape(CCH, 128, H, 3, 128)
    shared["qkvw"] = np.ascontiguousarray(wt.transpose(2, 3, 1, 0, 4)).reshape(H, 3, 128, CCH * 128)

    gt = np.ascontiguousarray(inp["L0_attn_gate_w"].T[perm, :])
    gt = gt.reshape(CCH, 128, CCH, 128)
    shared["gatew"] = np.ascontiguousarray(gt.transpose(2, 1, 0, 3)).reshape(CCH, 128, CCH * 128)

    ot = np.ascontiguousarray(inp["L0_attn_out_w"].T[:, perm])
    ot = ot.reshape(CCH, 128, CCH, 128)
    shared["outw"] = np.ascontiguousarray(ot.transpose(2, 1, 0, 3)).reshape(CCH, 128, CCH * 128)

    shared["normw"] = np.ascontiguousarray(inp["L0_attn_norm_w"].reshape(CCH, 128).T)

    qd, kd, dg, bd = _decay_consts()
    shared["qd"] = np.ascontiguousarray(
        np.broadcast_to(qd[:, None, :], (H, 128, BS)).transpose(1, 0, 2).reshape(128, H * BS)
    ).astype(ml_dtypes.bfloat16)
    shared["kd"] = np.ascontiguousarray(kd.reshape(H, 2, 128).transpose(2, 0, 1).reshape(128, H * 2))
    dgt = dg.transpose(0, 2, 1).reshape(H, 2, 128, BS).transpose(2, 0, 1, 3)
    shared["dt"] = np.ascontiguousarray(dgt.reshape(128, H * 2 * BS)).astype(ml_dtypes.bfloat16)
    shared["ident"] = np.eye(128, dtype=ml_dtypes.bfloat16)
    shared["ones1"] = np.ones((128, 1), dtype=ml_dtypes.bfloat16)
    shared["onesr"] = np.ones((1, 128), dtype=np.float32)

    for nm, key in (("gq", "L0_ffn_gate_Wq"), ("uq", "L0_ffn_up_Wq")):
        wq = _u8_view(inp[key])
        wqt = np.ascontiguousarray(wq.T)
        r = wqt.reshape(16, 128, FI // 256, 256)
        shared[nm] = np.ascontiguousarray(r.transpose(2, 1, 0, 3))
    dq = _u8_view(inp["L0_ffn_down_Wq"])
    dqt = np.ascontiguousarray(dq.T[:, perm])
    r = dqt.reshape(43, 128, 16, 256)
    shared["dq"] = np.ascontiguousarray(r.transpose(2, 0, 1, 3))

    def s_re(sc, nchunks, ncols, colperm=None):
        a = np.ascontiguousarray(sc.T)
        if colperm is not None:
            a = np.ascontiguousarray(a[:, colperm])
        re = a.reshape(nchunks, 4, ncols).transpose(1, 0, 2)
        return np.ascontiguousarray(re).reshape(4, -1), a
    shared["gs_re"], shared["gsT"] = s_re(inp["L0_ffn_gate_scale"], 16, FI)
    _, shared["gzT"] = s_re(inp["L0_ffn_gate_zero"], 16, FI)
    shared["us_re"], shared["usT"] = s_re(inp["L0_ffn_up_scale"], 16, FI)
    _, shared["uzT"] = s_re(inp["L0_ffn_up_zero"], 16, FI)
    shared["ds_re"], shared["dsT"] = s_re(inp["L0_ffn_down_scale"], 43, C, perm)
    _, shared["dzT"] = s_re(inp["L0_ffn_down_zero"], 43, C, perm)

    band = np.zeros((128, 304), dtype=ml_dtypes.bfloat16)
    for p in range(128):
        band[p, 172 + p // 32] = 1
    shared["band"] = band
    pat4 = np.zeros((4, 128), dtype=ml_dtypes.bfloat16)
    for p in range(128):
        pat4[p // 32, p] = 1
    shared["pat4"] = pat4
    return shared, per_core



class Bld:
    ENG = ("t", "v", "s", "g", "y")

    def __init__(self):
        self.q = {e: [] for e in self.ENG}
        self.cnt = {}

    def sem(self, *names):
        for n in names:
            self.cnt[n] = 0

    def emit(self, eng, f, inc=None, k=1):
        if inc is not None:
            self.cnt[inc] += k

            def fn(e, S, f=f, inc=inc, k=k):
                f(e, S).then_inc(S[inc], k)
            self.q[eng].append(fn)
            return self.cnt[inc]
        self.q[eng].append(lambda e, S, f=f: f(e, S))
        return None

    def wait(self, eng, sem, val):
        if val > 0:
            self.q[eng].append(lambda e, S, sem=sem, val=val: e.wait_ge(S[sem], val))

    def drain(self, eng):
        self.q[eng].append(lambda e, S: e.drain())

    def dma(self, eng, sem, f):
        self.wait(eng, sem, self.cnt[sem])
        return self.emit(eng, f, inc=sem, k=16)


def build_nc(dbg=(), n_heads=H, do_p05=True, do_p1=True):
    nc = bass.Bass("TRN2", target_bir_lowering=False)
    dbg = set(dbg)
    _, _, _, bd_all = _decay_consts()

    xp = nc.dram_tensor("xp", [T, C], F32, kind="ExternalInput")
    xq = nc.dram_tensor("xq", [T, C], F32, kind="ExternalInput")
    qkvw = nc.dram_tensor("qkvw", [H, 3, 128, CCH * 128], F32, kind="ExternalInput")
    gatew = nc.dram_tensor("gatew", [CCH, 128, CCH * 128], F32, kind="ExternalInput")
    outw = nc.dram_tensor("outw", [CCH, 128, CCH * 128], F32, kind="ExternalInput")
    normw = nc.dram_tensor("normw", [128, CCH], F32, kind="ExternalInput")
    qd_t = nc.dram_tensor("qd", [128, H * BS], BF16, kind="ExternalInput")
    kd_t = nc.dram_tensor("kd", [128, H * 2], F32, kind="ExternalInput")
    dt_t = nc.dram_tensor("dt", [128, H * 2 * BS], BF16, kind="ExternalInput")
    ident_t = nc.dram_tensor("ident", [128, 128], BF16, kind="ExternalInput")
    ones1_t = nc.dram_tensor("ones1", [128, 1], BF16, kind="ExternalInput")
    onesr_t = nc.dram_tensor("onesr", [1, 128], F32, kind="ExternalInput")
    gq_t = nc.dram_tensor("gq", [43, 128, 16, 256], U8, kind="ExternalInput")
    uq_t = nc.dram_tensor("uq", [43, 128, 16, 256], U8, kind="ExternalInput")
    dq_t = nc.dram_tensor("dq", [16, 43, 128, 256], U8, kind="ExternalInput")
    sc_in = {}
    for nm, shp in (("gs_re", [4, 16 * FI]), ("us_re", [4, 16 * FI]), ("ds_re", [4, 43 * C]),
                    ("gsT", [64, FI]), ("gzT", [64, FI]), ("usT", [64, FI]), ("uzT", [64, FI]),
                    ("dsT", [172, C]), ("dzT", [172, C])):
        sc_in[nm] = nc.dram_tensor(nm, shp, F32, kind="ExternalInput")
    band_t = nc.dram_tensor("band", [128, 304], BF16, kind="ExternalInput")
    pat4_t = nc.dram_tensor("pat4", [4, 128], BF16, kind="ExternalInput")

    out_t = nc.dram_tensor("out", [T, C], F32, kind="ExternalOutput")

    xbf = nc.dram_tensor("xbf", [T, C], BF16, kind="Internal")
    xqbf = nc.dram_tensor("xqbf", [T, C], BF16, kind="Internal")
    a2d = nc.dram_tensor("a2d", [C, T], BF16, kind="Internal")
    dpd = nc.dram_tensor("dpd", [C, T], BF16, kind="Internal")
    zs_g = nc.dram_tensor("zs_g", [64, FI], BF16, kind="Internal")
    zs_u = nc.dram_tensor("zs_u", [64, FI], BF16, kind="Internal")
    zs_d = nc.dram_tensor("zs_d", [172, C], BF16, kind="Internal")
    s1_g = nc.dram_tensor("s1_g", [4, 16 * FI], BF16, kind="Internal")
    s1_u = nc.dram_tensor("s1_u", [4, 16 * FI], BF16, kind="Internal")
    s1_d = nc.dram_tensor("s1_d", [4, 43 * C], BF16, kind="Internal")

    dbg_t = {}
    for name, shp, dt in (("attn", [C, T], F32), ("r1", [C, T], F32),
                          ("kvin", [128, H * 128], F32)):
        if name in dbg:
            dbg_t[name] = nc.dram_tensor("dbg_" + name, shp, dt, kind="ExternalOutput")

    b = Bld()
    b.sem("dc", "db", "dx", "ds0", "ds1", "ds2", "dd0", "dd1",
          "dw0", "dw1", "dfb", "dfs", "dfz", "dzt", "ddp", "dsa", "dtm", "dfo", "ds1c",
          "pe", "dv", "ac", "po", "fin")

    with ExitStack() as st:
        def sb(name, shape, dtype):
            return st.enter_context(nc.sbuf_tensor(name, shape, dtype))

        x_sb = sb("x_sb", [128, CCH * T], BF16)
        at_sb = sb("at_sb", [128, CCH * T], BF16)
        wsl = sb("wsl", [128, 3 * 4096], BF16)
        qkvt = sb("qkvt", [128, 3 * T], BF16)
        q2 = sb("q2", [128, T], BF16)
        ssb = sb("ssb", [128, 512], BF16)
        ktm = sb("ktm", [128, 256], BF16)
        vtm = sb("vtm", [128, 256], BF16)
        stf = sb("stf", [128, 2 * 128], F32)
        kvbf = sb("kvbf", [128, 128], BF16)
        gaux = sb("gaux", [128, 14848], BF16)
        kvin = gaux[:, 9216:9216 + H * 128]
        qdsb = gaux[:, 0:H * BS]
        kdsb = sb("kdsb", [128, H * 2], F32)
        dtsb = gaux[:, 8192:8192 + 1024]
        identb = sb("identb", [128, 128], BF16)
        ones1b = sb("ones1b", [128, 1], BF16)
        onesrb = sb("onesrb", [1, 128], F32)
        normwb = sb("normwb", [128, CCH], F32)
        fout = sb("fout", [128, 3072], F32)
        rows = fout[0:1, 0:2048]
        rwork = fout[0:1, 2048:3072]
        bandb = sb("bandb", [128, 304], BF16)
        pat4b = sb("pat4b", [4, 128], BF16)

        pbk = [st.enter_context(nc.psum_tensor(f"pb{i}", [128, 512], F32)) for i in range(7)]
        pbf = st.enter_context(nc.psum_tensor("pbf", [128, 1024], BF16))

        sems = {n: st.enter_context(nc.semaphore(n)) for n in b.cnt}

        b.emit("g", lambda e, S: e.dma_start(out=xbf[:, :], in_=xp[:, :]), inc="db", k=16)
        b.emit("g", lambda e, S: e.dma_start(out=xqbf[:, :], in_=xq[:, :]), inc="db", k=16)
        b.emit("g", lambda e, S: e.dma_start(out=qdsb[:, :], in_=qd_t[:, :]), inc="dc", k=16)
        b.emit("g", lambda e, S: e.dma_start(out=kdsb[:, :], in_=kd_t[:, :]), inc="dc", k=16)
        b.emit("g", lambda e, S: e.dma_start(out=identb[:, :], in_=ident_t[:, :]), inc="dc", k=16)
        b.emit("g", lambda e, S: e.dma_start(out=ones1b[:, :], in_=ones1_t[:, :]), inc="dc", k=16)
        b.emit("g", lambda e, S: e.dma_start(out=onesrb[:, :], in_=onesr_t[:, :]), inc="dc", k=16)
        b.emit("g", lambda e, S: e.dma_start(out=normwb[:, :], in_=normw[:, :]), inc="dc", k=16)
        b.emit("g", lambda e, S: e.dma_start(out=bandb[:, :], in_=band_t[:, :]), inc="dc", k=16)
        b.emit("g", lambda e, S: e.dma_start(out=pat4b[:, :], in_=pat4_t[:, :]), inc="dc", k=16)
        consts_done = b.cnt["dc"]

        b.wait("y", "db", 32)
        for cc in range(CCH):
            b.emit("y", lambda e, S, cc=cc: e.dma_start_transpose(
                x_sb[:, cc * T:(cc + 1) * T], xbf[:, cc * 128:(cc + 1) * 128]), inc="dx", k=16)
        for cc in range(CCH):
            b.emit("y", lambda e, S, cc=cc: e.dma_start_transpose(
                at_sb[:, cc * T:(cc + 1) * T], xqbf[:, cc * 128:(cc + 1) * 128]), inc="dx", k=16)
        x_loaded = b.cnt["dx"]

        NSLOT = 3
        slot_of = {}
        slab_ready = {}
        slab_guard = [0] * NSLOT
        next_slot = [0]

        def load_slab(h, z):
            sl = next_slot[0] % NSLOT
            next_slot[0] += 1
            slot_of[(h, z)] = sl
            b.wait("g", "pe", slab_guard[sl])
            sem = f"ds{sl}"
            m = b.dma("g", sem, lambda e, S, h=h, z=z, sl=sl: e.dma_start(
                out=wsl[:, sl * 4096:(sl + 1) * 4096], in_=qkvw[h, z, :, :]))
            slab_ready[(h, z)] = (sem, m)

        def qkv_matmuls(h, z, comp_idx, src):
            sl = slot_of[(h, z)]
            sem, m = slab_ready[(h, z)]
            b.wait("t", sem, m)
            b.wait("t", "dv", b.cnt["dv"])
            for th in range(2):
                bank = pbk[2 * comp_idx + th]
                for cc in range(CCH):
                    b.emit("t", lambda e, S, bank=bank, sl=sl, cc=cc, th=th, src=src:
                           e.matmul(bank[:, :],
                                    wsl[:, sl * 4096 + cc * 128: sl * 4096 + (cc + 1) * 128],
                                    src[:, cc * T + th * TH: cc * T + th * TH + TH],
                                    start=(cc == 0), stop=(cc == CCH - 1)),
                           inc="pe" if cc == CCH - 1 else None)
            pe_m = b.cnt["pe"]
            b.wait("s", "pe", pe_m)
            for th in range(2):
                bank = pbk[2 * comp_idx + th]
                b.emit("s", lambda e, S, bank=bank, comp_idx=comp_idx, th=th:
                       e.activation(qkvt[:, comp_idx * T + th * TH: comp_idx * T + th * TH + TH],
                                    bank[:, :], AF.Silu), inc="ac")
            slab_guard[sl] = pe_m
            return b.cnt["ac"]

        def scan_block(h, n, with_outputs, phase1):
            bd = float(bd_all[h])
            par = n % 2
            kcol = T + n * BS
            vcol = 2 * T + n * BS
            kvin_col = h * 128
            b.wait("t", "dv", b.cnt["dv"])
            for sub in range(2):
                b.emit("t", lambda e, S, kcol=kcol, sub=sub:
                       e.transpose(pbf[:, sub * 128:(sub + 1) * 128],
                                   qkvt[:, kcol + sub * 128: kcol + (sub + 1) * 128], identb[:, :]))
            for sub in range(2):
                b.emit("t", lambda e, S, vcol=vcol, sub=sub:
                       e.transpose(pbf[:, 256 + sub * 128: 256 + (sub + 1) * 128],
                                   qkvt[:, vcol + sub * 128: vcol + (sub + 1) * 128], identb[:, :]),
                       inc="pe" if sub == 1 else None)
            tr_done = b.cnt["pe"]
            if with_outputs:
                qcol = n * BS
                for sub in range(2):
                    b.emit("t", lambda e, S, kcol=kcol, qcol=qcol, sub=sub:
                           e.matmul(pbk[6][:, sub * 256:(sub + 1) * 256],
                                    qkvt[:, kcol + sub * 128: kcol + (sub + 1) * 128],
                                    qkvt[:, qcol: qcol + 256], start=True, stop=True),
                           inc="pe" if sub == 1 else None)
                s_done = b.cnt["pe"]
                b.wait("v", "pe", s_done)
                dslot = (h % 2) * 512
                for sub in range(2):
                    b.emit("v", lambda e, S, sub=sub, dslot=dslot:
                           e.tensor_mul(ssb[:, sub * 256:(sub + 1) * 256],
                                        pbk[6][:, sub * 256:(sub + 1) * 256],
                                        dtsb[:, dslot + sub * 256: dslot + (sub + 1) * 256]),
                           inc="dv" if sub == 1 else None)
            b.wait("v", "pe", tr_done)
            for sub in range(2):
                b.emit("v", lambda e, S, sub=sub, h=h:
                       e.tensor_scalar(ktm[:, sub * 128:(sub + 1) * 128],
                                       pbf[:, sub * 128:(sub + 1) * 128],
                                       kdsb[:, h * 2 + sub: h * 2 + sub + 1], None, ALU.mult))
            b.emit("v", lambda e, S: e.tensor_copy(vtm[:, :], pbf[:, 256:512]), inc="dv")
            ev_done = b.cnt["dv"]
            b.wait("t", "dv", ev_done)
            if with_outputs:
                qcol = n * BS
                for sub in range(2):
                    b.emit("t", lambda e, S, sub=sub:
                           e.matmul(pbk[1][:, 0:256], vtm[:, sub * 128:(sub + 1) * 128],
                                    ssb[:, sub * 256:(sub + 1) * 256],
                                    start=(sub == 0), stop=False))
                kv_ap = (kvin[:, kvin_col:kvin_col + 128] if n == 0 else kvbf[:, :])
                b.emit("t", lambda e, S, kv_ap=kv_ap, qcol=qcol:
                       e.matmul(pbk[1][:, 0:256], kv_ap, q2[:, qcol:qcol + 256],
                                start=False, stop=True), inc="pe")
                att_done = b.cnt["pe"]
                b.wait("v", "pe", att_done)
                b.emit("v", lambda e, S, h=h, n=n:
                       e.tensor_copy(at_sb[:, h * T + n * BS: h * T + (n + 1) * BS],
                                     pbk[1][:, 0:256]), inc="dv")
            for sub in range(2):
                b.emit("t", lambda e, S, sub=sub:
                       e.matmul(pbk[0][:, 0:128], ktm[:, sub * 128:(sub + 1) * 128],
                                vtm[:, sub * 128:(sub + 1) * 128],
                                start=(sub == 0), stop=(sub == 1)),
                       inc="pe" if sub == 1 else None)
            c_done = b.cnt["pe"]
            b.wait("v", "pe", c_done)
            b.drain("v")
            src_st = stf[:, par * 128:(par + 1) * 128]
            dst_st = stf[:, (1 - par) * 128:(2 - par) * 128]
            if n == 0:
                if phase1:
                    b.emit("v", lambda e, S, dst=dst_st, bd=bd, kvin_col=kvin_col:
                           e.scalar_tensor_tensor(dst, kvin[:, kvin_col:kvin_col + 128], bd,
                                                  pbk[0][:, 0:128], ALU.mult, ALU.add))
                else:
                    b.emit("v", lambda e, S, dst=dst_st: e.tensor_copy(dst, pbk[0][:, 0:128]))
            else:
                b.emit("v", lambda e, S, src=src_st, dst=dst_st, bd=bd:
                       e.scalar_tensor_tensor(dst, src, bd, pbk[0][:, 0:128], ALU.mult, ALU.add))
            b.drain("v")
            b.emit("v", lambda e, S, dst=dst_st: e.tensor_copy(kvbf[:, :], dst), inc="dv")

        b.emit("g", lambda e, S: e.dma_start(out=s1_g[:, :], in_=sc_in["gs_re"][:, :]), inc="ds1c", k=16)
        b.emit("g", lambda e, S: e.dma_start(out=s1_u[:, :], in_=sc_in["us_re"][:, :]), inc="ds1c", k=16)
        b.emit("g", lambda e, S: e.dma_start(out=s1_d[:, :], in_=sc_in["ds_re"][:, :]), inc="ds1c", k=16)
        s1_done = b.cnt["ds1c"]
        for sT_n, zT_n, dst in (("gsT", "gzT", zs_g), ("usT", "uzT", zs_u)):
            for c0, c1 in ((0, 5504), (5504, FI)):
                w = c1 - c0
                b.wait("g", "dfb", b.cnt["dfb"])
                b.dma("g", "dfz", lambda e, S, sT_n=sT_n, c0=c0, c1=c1, w=w: e.dma_start(
                    out=wsl[0:64, 0:w], in_=sc_in[sT_n][:, c0:c1]))
                lm = b.dma("g", "dfz", lambda e, S, zT_n=zT_n, c0=c0, c1=c1, w=w: e.dma_start(
                    out=wsl[0:64, 5504:5504 + w], in_=sc_in[zT_n][:, c0:c1]))
                b.wait("g", "dfz", lm)
                b.emit("g", lambda e, S, w=w: e.tensor_mul(
                    wsl[0:64, 5504:5504 + w], wsl[0:64, 5504:5504 + w], wsl[0:64, 0:w]),
                    inc="po")
                b.wait("g", "po", b.cnt["po"])
                b.dma("g", "dfb", lambda e, S, dst=dst, c0=c0, c1=c1, w=w: e.dma_start(
                    out=dst[:, c0:c1], in_=wsl[0:64, 5504:5504 + w]))
        for r0_, r1_ in ((0, 128), (128, 172)):
            nrw = r1_ - r0_
            b.wait("g", "dfb", b.cnt["dfb"])
            b.dma("g", "dfz", lambda e, S, r0_=r0_, nrw=nrw: e.dma_start(
                out=wsl[0:nrw, 0:C], in_=sc_in["dsT"][r0_:r0_ + nrw, :]))
            lm = b.dma("g", "dfz", lambda e, S, r0_=r0_, nrw=nrw: e.dma_start(
                out=wsl[0:nrw, 4096:4096 + C], in_=sc_in["dzT"][r0_:r0_ + nrw, :]))
            b.wait("g", "dfz", lm)
            b.emit("g", lambda e, S, nrw=nrw: e.tensor_mul(
                wsl[0:nrw, 8192:8192 + C], wsl[0:nrw, 4096:4096 + C],
                wsl[0:nrw, 0:C]), inc="po")
            b.wait("g", "po", b.cnt["po"])
            b.dma("g", "dfb", lambda e, S, r0_=r0_, nrw=nrw: e.dma_start(
                out=zs_d[r0_:r0_ + nrw, :], in_=wsl[0:nrw, 8192:8192 + C]))
        b.wait("g", "dfb", b.cnt["dfb"])

        b.emit("g", lambda e, S: e.memset(kvin[:, :], 0), inc="po")
        b.wait("t", "dx", x_loaded)
        b.wait("t", "dc", consts_done)
        b.wait("v", "dc", consts_done)
        b.wait("v", "dx", x_loaded)
        for h in range(n_heads if do_p05 else 0):
            load_slab(h, 1)
            load_slab(h, 2)
            qkv_matmuls(h, 1, 1, at_sb)
            qkv_matmuls(h, 2, 2, at_sb)
            b.wait("t", "ac", b.cnt["ac"])
            for n in range(NB):
                scan_block(h, n, with_outputs=False, phase1=False)
            b.drain("v")
            fp = NB % 2
            b.emit("v", lambda e, S, h=h, fp=fp:
                   e.tensor_copy(kvin[:, h * 128:(h + 1) * 128], stf[:, fp * 128:(fp + 1) * 128]),
                   inc="dv")

        if "kvin" in dbg:
            b.wait("g", "dv", b.cnt["dv"])
            b.emit("g", lambda e, S: e.dma_start(out=dbg_t["kvin"][:, :], in_=kvin[:, :]),
                   inc="fin", k=16)

        for h in range(n_heads if do_p1 else 0):
            load_slab(h, 0)
            load_slab(h, 1)
            load_slab(h, 2)
            dslot = (h % 2) * 512
            dsem = f"dd{h % 2}"
            b.wait("y", "dv", b.cnt["dv"])
            d_ready = b.dma("y", dsem, lambda e, S, h=h, dslot=dslot: e.dma_start(
                out=dtsb[:, dslot:dslot + 512], in_=dt_t[:, h * 512:(h + 1) * 512]))
            qkv_matmuls(h, 0, 0, x_sb)
            qkv_matmuls(h, 1, 1, x_sb)
            qkv_matmuls(h, 2, 2, x_sb)
            ac_m = b.cnt["ac"]
            b.wait("v", "ac", ac_m)
            b.wait("v", dsem, d_ready)
            for n in range(NB):
                b.emit("v", lambda e, S, h=h, n=n:
                       e.tensor_mul(q2[:, n * BS:(n + 1) * BS],
                                    qkvt[:, n * BS:(n + 1) * BS],
                                    qdsb[:, h * BS:(h + 1) * BS]),
                       inc="dv" if n == NB - 1 else None)
            b.wait("t", "dv", b.cnt["dv"])
            for n in range(NB):
                scan_block(h, n, with_outputs=True, phase1=True)

        if "attn" in dbg:
            b.wait("g", "dv", b.cnt["dv"])
            for cc in range(CCH):
                b.emit("g", lambda e, S, cc=cc: e.dma_start(
                    out=dbg_t["attn"][cc * 128:(cc + 1) * 128, :],
                    in_=at_sb[:, cc * T:(cc + 1) * T]), inc="fin", k=16)

        SQ0, SQ1 = 8192, 9216
        GS0, GS1 = 8192, 9216
        RSTD_BC, MEAN_BC = 12288, 13312

        sq_guard = [0, 0]
        attn_done = b.cnt["dv"]
        for cc in range(CCH):
            sl = cc % 2
            b.wait("v", "pe", sq_guard[sl])
            b.drain("v")
            b.emit("v", lambda e, S, cc=cc, sl=sl:
                   e.tensor_mul(gaux[:, sl * 1024:(sl + 1) * 1024],
                                at_sb[:, cc * T:(cc + 1) * T], at_sb[:, cc * T:(cc + 1) * T]),
                   inc="dv")
            dv_m = b.cnt["dv"]
            b.wait("t", "dv", dv_m)
            for th in range(2):
                b.emit("t", lambda e, S, sl=sl, th=th, cc=cc:
                       e.matmul(pbk[4 + th][0:1, :], ones1b[:, :],
                                gaux[:, sl * 1024 + th * TH: sl * 1024 + th * TH + TH],
                                start=(cc == 0), stop=(cc == CCH - 1)),
                       inc="pe" if th == 1 else None)
            sq_guard[sl] = b.cnt["pe"]
        rms_mm = b.cnt["pe"]
        b.wait("v", "pe", rms_mm)
        b.drain("v")
        b.emit("v", lambda e, S: e.tensor_copy(rows[0:1, 0:512], pbk[4][0:1, :]))
        b.emit("v", lambda e, S: e.tensor_copy(rows[0:1, 512:1024], pbk[5][0:1, :]))
        b.drain("v")
        b.emit("v", lambda e, S: e.tensor_scalar(rows[0:1, 0:1024], rows[0:1, 0:1024],
                                                 1.0 / C, EPS, ALU.mult, ALU.add), inc="dv")
        b.wait("s", "dv", b.cnt["dv"])
        b.emit("s", lambda e, S: e.activation(rows[0:1, 1024:2048], rows[0:1, 0:1024], AF.Sqrt),
               inc="ac")
        b.wait("v", "ac", b.cnt["ac"])
        b.emit("v", lambda e, S: e.reciprocal(rows[0:1, 0:1024], rows[0:1, 1024:2048]), inc="dv")
        b.wait("t", "dv", b.cnt["dv"])
        for th in range(2):
            b.emit("t", lambda e, S, th=th:
                   e.matmul(pbk[4 + th][:, :], onesrb[:, :], rows[0:1, th * TH:(th + 1) * TH],
                            start=True, stop=True), inc="pe" if th == 1 else None)
        b.wait("s", "pe", b.cnt["pe"])
        for th in range(2):
            b.emit("s", lambda e, S, th=th:
                   e.activation(gaux[:, RSTD_BC + th * TH: RSTD_BC + th * TH + TH],
                                pbk[4 + th][:, :], AF.Copy), inc="ac")
        rstd_bc_done = b.cnt["ac"]

        for dsn in ("ds0", "ds1", "ds2"):
            b.wait("g", dsn, b.cnt[dsn])
        wslot_guard = [b.cnt["pe"], b.cnt["pe"]]
        gpsum_guard = [0, 0]
        gstage_guard = [0, 0]
        z_first = True
        for oc in range(CCH):
            sl = oc % 2
            wsem = f"dw{sl}"
            b.wait("g", "pe", wslot_guard[sl])
            wm = b.dma("g", wsem, lambda e, S, oc=oc, sl=sl: e.dma_start(
                out=wsl[:, sl * 4096:(sl + 1) * 4096], in_=gatew[oc, :, :]))
            b.wait("t", wsem, wm)
            b.wait("t", "ac", gpsum_guard[sl])
            for th in range(2):
                bank = pbk[2 * sl + th]
                for cc in range(CCH):
                    b.emit("t", lambda e, S, bank=bank, sl=sl, cc=cc, th=th:
                           e.matmul(bank[:, :],
                                    wsl[:, sl * 4096 + cc * 128: sl * 4096 + (cc + 1) * 128],
                                    x_sb[:, cc * T + th * TH: cc * T + th * TH + TH],
                                    start=(cc == 0), stop=(cc == CCH - 1)),
                           inc="pe" if cc == CCH - 1 else None)
            wslot_guard[sl] = b.cnt["pe"]
            b.wait("s", "pe", b.cnt["pe"])
            if z_first:
                b.wait("s", "ac", rstd_bc_done)
            b.wait("s", "dv", gstage_guard[sl])
            for th in range(2):
                b.emit("s", lambda e, S, sl=sl, th=th:
                       e.activation(wsl[:, (GS0 if sl == 0 else GS1) + th * TH:
                                        (GS0 if sl == 0 else GS1) + th * TH + TH],
                                    pbk[2 * sl + th][:, :], AF.Sigmoid), inc="ac")
            gpsum_guard[sl] = b.cnt["ac"]
            b.wait("v", "ac", b.cnt["ac"])
            if z_first:
                b.wait("v", "ac", rstd_bc_done)
                z_first = False
            b.drain("v")
            b.emit("v", lambda e, S, oc=oc:
                   e.tensor_mul(q2[:, :], at_sb[:, oc * T:(oc + 1) * T],
                                gaux[:, RSTD_BC:RSTD_BC + 1024]))
            b.drain("v")
            b.emit("v", lambda e, S, oc=oc, sl=sl:
                   e.scalar_tensor_tensor(at_sb[:, oc * T:(oc + 1) * T], q2[:, :],
                                          normwb[:, oc:oc + 1],
                                          wsl[:, (GS0 if sl == 0 else GS1):(GS0 if sl == 0 else GS1) + 1024],
                                          ALU.mult, ALU.mult), inc="dv")
            gstage_guard[sl] = b.cnt["dv"]

        a_guard = 0
        b.wait("t", "ac", b.cnt["ac"])
        b.wait("t", "dv", b.cnt["dv"])
        for oc in range(CCH):
            sl = oc % 2
            wsem = f"dw{sl}"
            b.wait("g", "pe", wslot_guard[sl])
            wm = b.dma("g", wsem, lambda e, S, oc=oc, sl=sl: e.dma_start(
                out=wsl[:, sl * 4096:(sl + 1) * 4096], in_=outw[oc, :, :]))
            b.wait("t", wsem, wm)
            b.wait("t", "dv", a_guard)
            for th in range(2):
                for cc in range(CCH):
                    b.emit("t", lambda e, S, sl=sl, cc=cc, th=th:
                           e.matmul(pbk[th][:, :],
                                    wsl[:, sl * 4096 + cc * 128: sl * 4096 + (cc + 1) * 128],
                                    at_sb[:, cc * T + th * TH: cc * T + th * TH + TH],
                                    start=(cc == 0), stop=(cc == CCH - 1)),
                           inc="pe" if cc == CCH - 1 else None)
            wslot_guard[sl] = b.cnt["pe"]
            b.wait("v", "pe", b.cnt["pe"])
            for th in range(2):
                b.emit("v", lambda e, S, oc=oc, th=th:
                       e.tensor_add(x_sb[:, oc * T + th * TH: oc * T + th * TH + TH],
                                    x_sb[:, oc * T + th * TH: oc * T + th * TH + TH],
                                    pbk[th][:, :]), inc="dv" if th == 1 else None)
            a_guard = b.cnt["dv"]
            b.wait("v", "pe", sq_guard[sl])
            b.drain("v")
            b.emit("v", lambda e, S, oc=oc, sl=sl:
                   e.tensor_mul(gaux[:, sl * 1024:(sl + 1) * 1024],
                                x_sb[:, oc * T:(oc + 1) * T], x_sb[:, oc * T:(oc + 1) * T]),
                   inc="dv")
            b.wait("t", "dv", b.cnt["dv"])
            for th in range(2):
                b.emit("t", lambda e, S, oc=oc, th=th:
                       e.matmul(pbk[4 + th][0:1, :], ones1b[:, :],
                                x_sb[:, oc * T + th * TH: oc * T + th * TH + TH],
                                start=(oc == 0), stop=(oc == CCH - 1)))
            for th in range(2):
                b.emit("t", lambda e, S, oc=oc, sl=sl, th=th:
                       e.matmul(pbk[2 + th][0:1, :], ones1b[:, :],
                                gaux[:, (0 if sl == 0 else 1024) + th * TH:
                                     (0 if sl == 0 else 1024) + th * TH + TH],
                                start=(oc == 0), stop=(oc == CCH - 1)),
                       inc="pe" if th == 1 else None)
            sq_guard[sl] = b.cnt["pe"]
        ln1_mm = b.cnt["pe"]
        b.wait("v", "pe", ln1_mm)
        b.emit("v", lambda e, S: e.tensor_copy(rows[0:1, 0:512], pbk[4][0:1, :]))
        b.emit("v", lambda e, S: e.tensor_copy(rows[0:1, 512:1024], pbk[5][0:1, :]))
        b.emit("v", lambda e, S: e.tensor_copy(rows[0:1, 1024:1536], pbk[2][0:1, :]))
        b.emit("v", lambda e, S: e.tensor_copy(rows[0:1, 1536:2048], pbk[3][0:1, :]))
        b.drain("v")
        b.emit("v", lambda e, S: e.tensor_scalar(rows[0:1, 0:1024], rows[0:1, 0:1024],
                                                 1.0 / C, None, ALU.mult))
        b.drain("v")
        b.emit("v", lambda e, S: e.tensor_mul(rwork, rows[0:1, 0:1024],
                                              rows[0:1, 0:1024]))
        b.drain("v")
        b.emit("v", lambda e, S: e.scalar_tensor_tensor(rows[0:1, 1024:2048], rows[0:1, 1024:2048],
                                                        1.0 / C, rwork,
                                                        ALU.mult, ALU.subtract))
        b.drain("v")
        b.emit("v", lambda e, S: e.tensor_scalar(rows[0:1, 1024:2048], rows[0:1, 1024:2048],
                                                 EPS, None, ALU.add), inc="dv")
        b.wait("s", "dv", b.cnt["dv"])
        b.emit("s", lambda e, S: e.activation(rwork, rows[0:1, 1024:2048], AF.Sqrt),
               inc="ac")
        b.wait("v", "ac", b.cnt["ac"])
        b.emit("v", lambda e, S: e.reciprocal(rows[0:1, 1024:2048], rwork), inc="dv")
        b.wait("t", "dv", b.cnt["dv"])
        for th in range(2):
            b.emit("t", lambda e, S, th=th:
                   e.matmul(pbk[4 + th][:, :], onesrb[:, :], rows[0:1, th * TH:(th + 1) * TH],
                            start=True, stop=True))
        for th in range(2):
            b.emit("t", lambda e, S, th=th:
                   e.matmul(pbk[2 + th][:, :], onesrb[:, :], rows[0:1, 1024 + th * TH:1024 + (th + 1) * TH],
                            start=True, stop=True), inc="pe" if th == 1 else None)
        b.wait("s", "pe", b.cnt["pe"])
        for th in range(2):
            b.emit("s", lambda e, S, th=th:
                   e.activation(gaux[:, MEAN_BC + th * TH: MEAN_BC + th * TH + TH],
                                pbk[4 + th][:, :], AF.Copy))
        for th in range(2):
            b.emit("s", lambda e, S, th=th:
                   e.activation(gaux[:, RSTD_BC + th * TH: RSTD_BC + th * TH + TH],
                                pbk[2 + th][:, :], AF.Copy), inc="ac")
        ln1_bc = b.cnt["ac"]
        b.wait("v", "ac", ln1_bc)
        for cc in range(CCH):
            b.drain("v")
            b.emit("v", lambda e, S, cc=cc:
                   e.tensor_sub(q2[:, :], x_sb[:, cc * T:(cc + 1) * T],
                                gaux[:, MEAN_BC:MEAN_BC + 1024]))
            b.drain("v")
            b.emit("v", lambda e, S, cc=cc:
                   e.tensor_mul(x_sb[:, cc * T:(cc + 1) * T], q2[:, :],
                                gaux[:, RSTD_BC:RSTD_BC + 1024]), inc="dv")
        r1_done = b.cnt["dv"]

        if "r1" in dbg:
            b.wait("g", "dv", r1_done)
            for cc in range(CCH):
                b.emit("g", lambda e, S, cc=cc: e.dma_start(
                    out=dbg_t["r1"][cc * 128:(cc + 1) * 128, :],
                    in_=x_sb[:, cc * T:(cc + 1) * T]), inc="fin", k=16)

        HI_T, LO_T, S_SL = 0, 4096, 8192
        XSUM, HSUM = 12288, 13312
        ZSD_T, ZSG_T = 14336, 14592
        dpo = qkvt[:, 2048:3072]

        b.wait("t", "dv", r1_done)
        for cc in range(CCH):
            jc = cc // 2
            for th in range(2):
                b.emit("t", lambda e, S, cc=cc, jc=jc, th=th:
                       e.matmul(pbk[th][:, :], bandb[:, 172 - 4 * jc: 300 - 4 * jc],
                                x_sb[:, cc * T + th * TH: cc * T + th * TH + TH],
                                start=(cc == 0), stop=(cc == CCH - 1)),
                       inc="pe" if (cc == CCH - 1 and th == 1) else None)
        pe_ffn0 = b.cnt["pe"]
        b.wait("v", "pe", pe_ffn0)
        b.emit("v", lambda e, S: e.tensor_scalar(gaux[0:64, XSUM:XSUM + 512],
                                                 pbk[0][0:64, :], -1.0, None, ALU.mult))
        b.emit("v", lambda e, S: e.tensor_scalar(gaux[0:64, XSUM + 512:XSUM + 1024],
                                                 pbk[1][0:64, :], -1.0, None, ALU.mult),
               inc="dv")
        xsum_done = b.cnt["dv"]

        b.wait("y", "ds1c", s1_done)
        b.wait("y", "dd0", b.cnt["dd0"])
        b.wait("y", "dd1", b.cnt["dd1"])
        b.wait("y", "dv", r1_done)
        b.wait("g", "dw0", b.cnt["dw0"])
        b.wait("g", "dw1", b.cnt["dw1"])
        PH_R = [(0, 15), (15, 29), (29, 43)]
        H135 = 135.53125
        HB = 2048
        ZSL = (ZSD_T, ZSG_T)

        units = []
        for ph, (rr0, rr1) in enumerate(PH_R):
            for r in range(rr0, rr1):
                for X in ("g", "u"):
                    units.append(("gu", ph, r, X, 0))
                    units.append(("gu", ph, r, X, 1))
            for how in range(16):
                units.append(("dn", ph, 0, how, 0))
                units.append(("dn", ph, 0, how, 1))

        st_deq_dv = {}
        st_pe_free = {}
        init_dv = r1_done
        init_pe = pe_ffn0
        zs_free = [init_pe, init_pe]
        zs_mark = {}
        tile_ctr = [0]
        ddp_m = {}
        ms = {
            "g_bank_dv": xsum_done,
            "silu_ac": 0,
            "h_evac_dv": 0,
            "pending_hsum": None,
            "hsum_done": 0,
            "dn_evac_dv": [0, 0],
            "dpp_guard": [0, 0],
        }

        def deq_unit(i):
            kind = units[i][0]
            pp = i % 2
            g2 = st_deq_dv.get(i - 2, init_dv)
            p2 = st_pe_free.get(i - 2, init_pe)
            if kind == "gu":
                _, ph, r, X, h = units[i]
                w = HB
                wq_src = gq_t if X == "g" else uq_t
                s1w = s1_g if X == "g" else s1_u
                b.wait("g", "dv", g2)
                bm = b.dma("g", "dfb", lambda e, S, wq_src=wq_src, r=r, h=h, pp=pp: e.dma_start(
                    out=wsl[:, pp * HB: pp * HB + HB],
                    in_=wq_src[r, :, 8 * h:8 * h + 8, :].rearrange("p cb i -> p (cb i)")))
                b.wait("y", "dv", g2)
                b.wait("y", "dfs", b.cnt["dfs"])
                for gs4 in range(4):
                    b.emit("y", lambda e, S, s1w=s1w, r=r, h=h, gs4=gs4, pp=pp: e.dma_start(
                        out=gaux[gs4 * 32:(gs4 + 1) * 32, S_SL + pp * HB:S_SL + pp * HB + HB],
                        in_=bass.AP(s1w, gs4 * 16 * FI + 8 * h * FI + 256 * r,
                                    [[0, 32], [FI, 8], [1, 256]])), inc="dfs", k=16)
                sm = b.cnt["dfs"]
                if h == 0:
                    zsd = zs_g if X == "g" else zs_u
                    sl = tile_ctr[0] % 2
                    tile_ctr[0] += 1
                    b.wait("y", "pe", zs_free[sl])
                    zs_mark[("gu", r, X)] = (sl, b.dma(
                        "y", "dzt", lambda e, S, zsd=zsd, r=r, sl=sl: e.dma_start(
                            out=gaux[0:64, ZSL[sl]:ZSL[sl] + 256],
                            in_=zsd[0:64, 256 * r:256 * r + 256])))
            else:
                _, ph, _, how, h = units[i]
                rr0, rr1 = PH_R[ph]
                nr = rr1 - rr0
                k0, nh = (0, 8) if h == 0 else (8, nr - 8)
                w = nh * 256
                b.wait("g", "dv", g2)
                bm = b.dma("g", "dfb", lambda e, S, how=how, rr0=rr0, k0=k0, nh=nh, pp=pp: e.dma_start(
                    out=wsl[:, pp * HB: pp * HB + nh * 256].rearrange("p (ib i) -> p ib i", ib=nh),
                    in_=dq_t[how, rr0 + k0:rr0 + k0 + nh].rearrange("ib p i -> p ib i")))
                b.wait("y", "dv", g2)
                b.wait("y", "dfs", b.cnt["dfs"])
                for gs4 in range(4):
                    b.emit("y", lambda e, S, how=how, rr0=rr0, k0=k0, nh=nh, gs4=gs4, pp=pp: e.dma_start(
                        out=gaux[gs4 * 32:(gs4 + 1) * 32, S_SL + pp * HB:S_SL + pp * HB + nh * 256],
                        in_=bass.AP(s1_d, gs4 * 43 * C + (rr0 + k0) * C + 256 * how,
                                    [[0, 32], [C, nh], [1, 256]])), inc="dfs", k=16)
                sm = b.cnt["dfs"]
                if h == 0:
                    sl = tile_ctr[0] % 2
                    tile_ctr[0] += 1
                    b.wait("y", "pe", zs_free[sl])
                    zs_mark[("dn", ph, how)] = (sl, b.dma(
                        "y", "dzt", lambda e, S, rr0=rr0, nr=nr, how=how, sl=sl: e.dma_start(
                            out=gaux[0:4 * nr, ZSL[sl]:ZSL[sl] + 256],
                            in_=zs_d[4 * rr0:4 * rr0 + 4 * nr, 256 * how:256 * how + 256])))
            b.wait("s", "dfb", bm)
            b.wait("s", "dv", g2)
            b.emit("s", lambda e, S, pp=pp, w=w: e.activation(
                wsl[:, 4096 + pp * HB:4096 + pp * HB + w], wsl[:, pp * HB:pp * HB + w],
                AF.Copy, bias=H135, scale=0.0625), inc="ac")
            b.wait("v", "ac", b.cnt["ac"])
            b.emit("v", lambda e, S, pp=pp, w=w: e.tensor_scalar(
                wsl[:, 8192 + pp * HB:8192 + pp * HB + w],
                wsl[:, 4096 + pp * HB:4096 + pp * HB + w], 136.0, None, ALU.subtract))
            b.drain("v")
            b.emit("v", lambda e, S, pp=pp, w=w: e.scalar_tensor_tensor(
                wsl[:, 4096 + pp * HB:4096 + pp * HB + w],
                wsl[:, 8192 + pp * HB:8192 + pp * HB + w], -16.0,
                wsl[:, pp * HB:pp * HB + w], ALU.mult, ALU.add))
            b.wait("v", "dfs", sm)
            b.wait("v", "pe", p2)
            b.drain("v")
            b.emit("v", lambda e, S, pp=pp, w=w: e.tensor_mul(
                gaux[:, LO_T + pp * HB:LO_T + pp * HB + w],
                wsl[:, 4096 + pp * HB:4096 + pp * HB + w],
                gaux[:, S_SL + pp * HB:S_SL + pp * HB + w]))
            b.emit("v", lambda e, S, pp=pp, w=w: e.tensor_mul(
                gaux[:, HI_T + pp * HB:HI_T + pp * HB + w],
                wsl[:, 8192 + pp * HB:8192 + pp * HB + w],
                gaux[:, S_SL + pp * HB:S_SL + pp * HB + w]), inc="dv")
            st_deq_dv[i] = b.cnt["dv"]

        def emit_hsum(ph, r, guard_dv):
            rr0, rr1 = PH_R[ph]
            nr = rr1 - rr0
            b.wait("t", "dv", guard_dv)
            for par in range(2):
                idx = 2 * (r - rr0) + par
                lr = r - rr0
                for th in range(2):
                    b.emit("t", lambda e, S, idx=idx, lr=lr, th=th, nr=nr:
                           e.matmul(pbk[4 + th][:, :],
                                    bandb[:, 172 - 4 * lr: 300 - 4 * lr],
                                    at_sb[:, idx * T + th * TH: idx * T + th * TH + TH],
                                    start=(idx == 0), stop=(idx == 2 * nr - 1)),
                           inc="pe" if (par == 1 and th == 1) else None)

        def emit_hsum_evac(ph):
            rr0, rr1 = PH_R[ph]
            nr = rr1 - rr0
            b.wait("v", "pe", b.cnt["pe"])
            b.emit("v", lambda e, S, nr=nr: e.tensor_scalar(
                gaux[0:4 * nr, HSUM:HSUM + 512], pbk[4][0:4 * nr, :], -1.0, None, ALU.mult))
            b.emit("v", lambda e, S, nr=nr: e.tensor_scalar(
                gaux[0:4 * nr, HSUM + 512:HSUM + 1024], pbk[5][0:4 * nr, :], -1.0, None, ALU.mult),
                inc="dv")
            ms["hsum_done"] = b.cnt["dv"]

        def grp_gu(i):
            _, ph, r, X, h = units[i]
            rr0, rr1 = PH_R[ph]
            pp = i % 2
            b.wait("t", "dv", st_deq_dv[i])
            if h == 0:
                sl, zm = zs_mark[("gu", r, X)]
                b.wait("t", "dzt", zm)
                if X == "g":
                    b.wait("t", "dv", ms["g_bank_dv"])
                else:
                    b.wait("t", "ac", ms["silu_ac"])
                for par in range(2):
                    for th in range(2):
                        bank = pbk[2 * par + th]
                        b.emit("t", lambda e, S, bank=bank, par=par, th=th, sl=sl:
                               e.matmul(bank[:, :],
                                        gaux[0:64, ZSL[sl] + par:ZSL[sl] + 256:2],
                                        gaux[0:64, XSUM + th * TH:XSUM + th * TH + TH],
                                        start=True, stop=False))
                        for cb in range(8):
                            for cpar in range(2):
                                src_t = HI_T if cpar == 0 else LO_T
                                cc = 2 * cb + cpar
                                lastu = (par == 1 and th == 1 and cb == 7 and cpar == 1)
                                b.emit("t", lambda e, S, bank=bank, src_t=src_t, cb=cb,
                                       pp=pp, cc=cc, th=th, par=par:
                                       e.matmul(bank[:, :],
                                                gaux[:, src_t + pp * HB + cb * 256 + par: src_t + pp * HB + (cb + 1) * 256: 2],
                                                x_sb[:, cc * T + th * TH: cc * T + th * TH + TH],
                                                start=False, stop=False),
                                       inc="pe" if lastu else None)
                zs_free[sl] = b.cnt["pe"]
                st_pe_free[i] = b.cnt["pe"]
            else:
                for par in range(2):
                    for th in range(2):
                        bank = pbk[2 * par + th]
                        for cb in range(8, 16):
                            for cpar in range(2):
                                src_t = HI_T if cpar == 0 else LO_T
                                cc = 2 * cb + cpar
                                last = (cb == 15 and cpar == 1)
                                lastu = (par == 1 and th == 1 and last)
                                b.emit("t", lambda e, S, bank=bank, src_t=src_t, cb=cb,
                                       pp=pp, cc=cc, th=th, last=last, par=par:
                                       e.matmul(bank[:, :],
                                                gaux[:, src_t + pp * HB + (cb - 8) * 256 + par: src_t + pp * HB + (cb - 7) * 256: 2],
                                                x_sb[:, cc * T + th * TH: cc * T + th * TH + TH],
                                                start=False, stop=last),
                                       inc="pe" if lastu else None)
                st_pe_free[i] = b.cnt["pe"]
                if X == "g":
                    b.wait("s", "pe", b.cnt["pe"])
                    b.wait("s", "dv", ms["g_bank_dv"])
                    for par in range(2):
                        for th in range(2):
                            b.emit("s", lambda e, S, par=par, th=th:
                                   e.activation(qkvt[:, par * 1024 + th * TH: par * 1024 + th * TH + TH],
                                                pbk[2 * par + th][:, :], AF.Silu), inc="ac")
                    ms["silu_ac"] = b.cnt["ac"]
                    if ms["pending_hsum"] is not None:
                        emit_hsum(*ms["pending_hsum"])
                        ms["pending_hsum"] = None
                else:
                    b.wait("v", "pe", b.cnt["pe"])
                    for par in range(2):
                        idx = 2 * (r - rr0) + par
                        for th in range(2):
                            b.emit("v", lambda e, S, par=par, th=th, idx=idx:
                                   e.tensor_mul(at_sb[:, idx * T + th * TH: idx * T + th * TH + TH],
                                                qkvt[:, par * 1024 + th * TH: par * 1024 + th * TH + TH],
                                                pbk[2 * par + th][:, :]),
                                   inc="dv" if (par == 1 and th == 1) else None)
                    ms["h_evac_dv"] = b.cnt["dv"]
                    ms["g_bank_dv"] = b.cnt["dv"]
                    if r == rr1 - 1:
                        emit_hsum(ph, r, ms["h_evac_dv"])
                        emit_hsum_evac(ph)
                    else:
                        ms["pending_hsum"] = (ph, r, ms["h_evac_dv"])

        def dn_group(ph, how, hsub, half, pp, sl=None):
            rr0, rr1 = PH_R[ph]
            nr = rr1 - rr0
            k0, nh = (0, 8) if half == 0 else (8, nr - 8)
            banks = (pbk[0], pbk[1]) if hsub == 0 else (pbk[2], pbk[3])
            for th in range(2):
                bank = banks[th]
                if half == 0:
                    b.emit("t", lambda e, S, bank=bank, hsub=hsub, th=th, nr=nr, sl=sl:
                           e.matmul(bank[:, :],
                                    gaux[0:4 * nr, ZSL[sl] + hsub * 128:ZSL[sl] + (hsub + 1) * 128],
                                    gaux[0:4 * nr, HSUM + th * TH:HSUM + th * TH + TH],
                                    start=True, stop=False))
                for k in range(k0, k0 + nh):
                    for par in range(2):
                        src_t = HI_T if par == 0 else LO_T
                        idx2 = 2 * k + par
                        last = (half == 1 and k == k0 + nh - 1 and par == 1)
                        lastu = (th == 1 and k == k0 + nh - 1 and par == 1)
                        b.emit("t", lambda e, S, bank=bank, src_t=src_t, k=k, k0=k0, hsub=hsub,
                               idx2=idx2, th=th, last=last, pp=pp:
                               e.matmul(bank[:, :],
                                        gaux[:, src_t + pp * HB + (k - k0) * 256 + hsub * 128: src_t + pp * HB + (k - k0) * 256 + hsub * 128 + 128],
                                        at_sb[:, idx2 * T + th * TH: idx2 * T + th * TH + TH],
                                        start=False, stop=last),
                               inc="pe" if lastu else None)

        def emit_dn_evac(ph, how, hsub, guard_pe):
            cc_o = 2 * how + hsub
            banks = (pbk[0], pbk[1]) if hsub == 0 else (pbk[2], pbk[3])
            b.wait("v", "pe", guard_pe)
            b.wait("v", "dsa", b.cnt["dsa"])
            if ph > 0:
                b.wait("v", "ddp", ddp_m[(ph, how, hsub)])
            for th in range(2):
                last = (ph != 2 and th == 1)
                if ph == 0:
                    b.emit("v", lambda e, S, th=th, banks=banks:
                           e.tensor_copy(dpo[:, th * TH: th * TH + TH],
                                         banks[th][:, :]), inc="dv" if last else None)
                else:
                    b.emit("v", lambda e, S, th=th, hsub=hsub, banks=banks:
                           e.tensor_add(dpo[:, th * TH: th * TH + TH],
                                        qkvt[:, hsub * 1024 + th * TH: hsub * 1024 + th * TH + TH],
                                        banks[th][:, :]), inc="dv" if last else None)
            if ph == 2:
                b.drain("v")
                for th in range(2):
                    b.emit("v", lambda e, S, th=th, cc_o=cc_o:
                           e.tensor_add(dpo[:, th * TH: th * TH + TH],
                                        dpo[:, th * TH: th * TH + TH],
                                        x_sb[:, cc_o * T + th * TH: cc_o * T + th * TH + TH]),
                           inc="dv" if th == 1 else None)
            ms["dn_evac_dv"][hsub] = b.cnt["dv"]
            ms["dpp_guard"][hsub] = b.cnt["dv"]
            b.wait("g", "dv", b.cnt["dv"])
            dst_d = a2d if ph == 2 else dpd
            b.dma("g", "dsa", lambda e, S, dst_d=dst_d, cc_o=cc_o: e.dma_start(
                out=dst_d[cc_o * 128:(cc_o + 1) * 128, :],
                in_=dpo[:, 0:1024]))

        def grp_dn(i):
            _, ph, _, how, h = units[i]
            pp = i % 2
            sl, zm = zs_mark[("dn", ph, how)]
            if h == 0:
                if ph > 0:
                    for hsub in range(2):
                        b.wait("y", "dv", ms["dpp_guard"][hsub])
                        b.wait("y", "dv", ms["h_evac_dv"])
                        ddp_m[(ph, how, hsub)] = b.dma(
                            "y", "ddp", lambda e, S, how=how, hsub=hsub: e.dma_start(
                                out=qkvt[:, hsub * 1024:(hsub + 1) * 1024],
                                in_=dpd[(2 * how + hsub) * 128:(2 * how + hsub + 1) * 128, :]))
                b.wait("t", "dv", st_deq_dv[i])
                b.wait("t", "dzt", zm)
                b.wait("t", "dv", ms["hsum_done"])
                b.wait("t", "dv", max(ms["dn_evac_dv"][0], ms["h_evac_dv"]))
                dn_group(ph, how, hsub=0, half=0, pp=pp, sl=sl)
                b.wait("t", "dv", st_deq_dv[i + 1])
                dn_group(ph, how, hsub=0, half=1, pp=(i + 1) % 2)
                emit_dn_evac(ph, how, hsub=0, guard_pe=b.cnt["pe"])
                b.wait("t", "dv", max(ms["dn_evac_dv"][1], ms["h_evac_dv"]))
                dn_group(ph, how, hsub=1, half=0, pp=pp, sl=sl)
                zs_free[sl] = b.cnt["pe"]
                st_pe_free[i] = b.cnt["pe"]
            else:
                b.wait("t", "dv", st_deq_dv[i])
                dn_group(ph, how, hsub=1, half=1, pp=pp)
                st_pe_free[i] = b.cnt["pe"]
                emit_dn_evac(ph, how, hsub=1, guard_pe=b.cnt["pe"])
                if how == 15:
                    ms["g_bank_dv"] = b.cnt["dv"]

        deq_unit(0)
        deq_unit(1)
        for i in range(len(units)):
            if units[i][0] == "gu":
                grp_gu(i)
            else:
                grp_dn(i)
            if i + 2 < len(units):
                deq_unit(i + 2)

        a2_stores = b.cnt["dsa"]
        b.wait("y", "dsa", a2_stores)
        tm_guard = [0, 0]
        sq_fin_guard = 0
        for tc in range(8):
            sl = tc % 2
            b.wait("y", "dv", tm_guard[sl])
            tm_m = b.dma("y", "dtm", lambda e, S, tc=tc, sl=sl: e.dma_start_transpose(
                gaux[:, sl * 4096:(sl + 1) * 4096], a2d[:, tc * 128:(tc + 1) * 128]))
            ro = 8 * sl
            b.wait("v", "dtm", tm_m)
            b.emit("v", lambda e, S, sl=sl, ro=ro: e.tensor_reduce(
                stf[:, ro:ro + 1], gaux[:, sl * 4096:(sl + 1) * 4096], AXX, ALU.add))
            b.wait("s", "dtm", tm_m)
            b.wait("s", "dv", sq_fin_guard)
            b.emit("s", lambda e, S, sl=sl: e.activation(
                gaux[:, 8192:12288], gaux[:, sl * 4096:(sl + 1) * 4096], AF.Square), inc="ac")
            b.wait("v", "ac", b.cnt["ac"])
            b.emit("v", lambda e, S, ro=ro: e.tensor_reduce(
                stf[:, ro + 1:ro + 2], gaux[:, 8192:12288], AXX, ALU.add), inc="dv")
            sq_fin_guard = b.cnt["dv"]
            b.drain("v")
            b.emit("v", lambda e, S, ro=ro: e.tensor_scalar(
                stf[:, ro + 2:ro + 3], stf[:, ro:ro + 1], 1.0 / C, None, ALU.mult))
            b.drain("v")
            b.emit("v", lambda e, S, ro=ro: e.tensor_mul(stf[:, ro + 3:ro + 4],
                                                         stf[:, ro + 2:ro + 3], stf[:, ro + 2:ro + 3]))
            b.drain("v")
            b.emit("v", lambda e, S, ro=ro: e.scalar_tensor_tensor(
                stf[:, ro + 4:ro + 5], stf[:, ro + 1:ro + 2], 1.0 / C, stf[:, ro + 3:ro + 4],
                ALU.mult, ALU.subtract))
            b.drain("v")
            b.emit("v", lambda e, S, ro=ro: e.tensor_scalar(
                stf[:, ro + 4:ro + 5], stf[:, ro + 4:ro + 5], EPS, None, ALU.add), inc="dv")
            b.wait("s", "dv", b.cnt["dv"])
            b.emit("s", lambda e, S, ro=ro: e.activation(stf[:, ro + 5:ro + 6],
                                                         stf[:, ro + 4:ro + 5], AF.Sqrt), inc="ac")
            b.wait("v", "ac", b.cnt["ac"])
            b.emit("v", lambda e, S, ro=ro: e.reciprocal(stf[:, ro + 6:ro + 7],
                                                         stf[:, ro + 5:ro + 6]))
            b.drain("v")
            for qv in range(4):
                fsl = qv % 2
                b.wait("v", "dfo", max(0, b.cnt["dfo"] - 16))
                src_ap = gaux[:, sl * 4096 + qv * 1024: sl * 4096 + (qv + 1) * 1024]
                b.emit("v", lambda e, S, src_ap=src_ap, ro=ro, fsl=fsl:
                       e.tensor_scalar(
                           fout[:, fsl * 1024:(fsl + 1) * 1024].rearrange(
                               "t (jb p par) -> t jb par p", jb=4, p=128, par=2),
                           src_ap.rearrange("t (jb par p) -> t jb par p", jb=4, par=2, p=128),
                           stf[:, ro + 2:ro + 3], stf[:, ro + 6:ro + 7],
                           ALU.subtract, ALU.mult), inc="dv")
                vm = b.cnt["dv"]
                b.wait("y", "dv", vm)
                b.dma("y", "dfo", lambda e, S, tc=tc, qv=qv, fsl=fsl: e.dma_start(
                    out=out_t[tc * 128:(tc + 1) * 128, qv * 1024:(qv + 1) * 1024],
                    in_=fout[:, fsl * 1024:(fsl + 1) * 1024]))
            tm_guard[sl] = b.cnt["dv"]

        b.wait("g", "dfo", b.cnt["dfo"])
        b.wait("g", "fin", b.cnt["fin"])

        with nc.Block() as blk:
            @blk.tensor
            def _(t):
                for fn in b.q["t"]:
                    fn(t, sems)

            @blk.vector
            def _(v):
                for fn in b.q["v"]:
                    fn(v, sems)

            @blk.scalar
            def _(s):
                for fn in b.q["s"]:
                    fn(s, sems)

            @blk.gpsimd
            def _(g):
                for fn in b.q["g"]:
                    fn(g, sems)

            @blk.sync
            def _(y):
                for fn in b.q["y"]:
                    fn(y, sems)

    return nc


def kernel(**inputs):
    shared, per_core = host_prep(inputs)
    nc = build_nc()
    in_maps = [dict(shared, **pc) for pc in per_core]
    res = run_bass_kernel_spmd(nc, in_maps, core_ids=list(range(NCORE)))
    out = np.empty((B, S, C), np.float32)
    for c in range(NCORE):
        i, j = c // 2, c % 2
        out[i, j * T:(j + 1) * T, :] = res.results[c]["out"]
    return out


# version 07
# speedup vs baseline: 1.3912x; 1.0724x over previous
import numpy as np
from contextlib import ExitStack

import concourse.bass as bass
import concourse.mybir as mybir
from concourse.bass_utils import run_bass_kernel_spmd

F32 = mybir.dt.float32
BF16 = mybir.dt.bfloat16
U8 = mybir.dt.uint8
AF = mybir.ActivationFunctionType
ALU = mybir.AluOpType
AXX = mybir.AxisListType.X

NCORE = 8
B, S, C = 4, 2048, 4096
T = 1024
H, D, BS = 32, 128, 256
NB = T // BS
CCH = C // 128
TH = 512
FI = 11008
EPS = 1e-5


def _lattice_perm():
    l = np.arange(C)
    jb, rem = l // 256, l % 256
    par, p = rem // 128, rem % 128
    return jb * 256 + 2 * p + par


def _decay_consts():
    base = 1.0 / 2.0 ** (8.0 / H)
    expo = np.arange(H, dtype=np.float64) + 1.0
    factor = 1.0 - 1 / (31 + 1e-05) + 1e-05
    slope = base ** expo * factor
    r = np.arange(BS, dtype=np.float64) + 1.0
    qd = np.exp(-slope[:, None] * r[None, :])
    kd = np.exp(-slope[:, None] * (BS - r)[None, :])
    nm = r[:, None] - r[None, :]
    dg = np.exp(-slope[:, None, None] * nm[None])
    dg = np.where(nm[None] >= 0, dg, 0.0)
    bd = np.exp(-slope * BS)
    return (qd.astype(np.float32), kd.astype(np.float32),
            dg.astype(np.float32), bd.astype(np.float32))


def _u8_view(int32_arr):
    a = np.ascontiguousarray(int32_arr)
    return np.ascontiguousarray(a.view(np.uint8).reshape(a.shape + (4,))[..., 0])


def host_prep(inp):
    import ml_dtypes
    perm = _lattice_perm()
    shared = {}

    x = np.ascontiguousarray(inp["L0_attn_hidden_states"], dtype=np.float32)
    x_lat = np.ascontiguousarray(x[:, :, perm])
    per_core = []
    zeros_prev = np.zeros((T, C), np.float32)
    for c in range(NCORE):
        i, j = c // 2, c % 2
        d = {"xp": np.ascontiguousarray(x_lat[i, j * T:(j + 1) * T])}
        d["xq"] = np.ascontiguousarray(x_lat[i, 0:T]) if j == 1 else zeros_prev
        per_core.append(d)

    qkv_w = inp["L0_attn_qkv_w"]
    wt = np.ascontiguousarray(qkv_w.T[perm, :])
    wt = wt.reshape(CCH, 128, H, 3, 128)
    shared["qkvw"] = np.ascontiguousarray(wt.transpose(2, 3, 1, 0, 4)).reshape(H, 3, 128, CCH * 128)

    gt = np.ascontiguousarray(inp["L0_attn_gate_w"].T[perm, :])
    gt = gt.reshape(CCH, 128, CCH, 128)
    shared["gatew"] = np.ascontiguousarray(gt.transpose(2, 1, 0, 3)).reshape(CCH, 128, CCH * 128)

    ot = np.ascontiguousarray(inp["L0_attn_out_w"].T[:, perm])
    ot = ot.reshape(CCH, 128, CCH, 128)
    shared["outw"] = np.ascontiguousarray(ot.transpose(2, 1, 0, 3)).reshape(CCH, 128, CCH * 128)

    shared["normw"] = np.ascontiguousarray(inp["L0_attn_norm_w"].reshape(CCH, 128).T)

    qd, kd, dg, bd = _decay_consts()
    shared["qd"] = np.ascontiguousarray(
        np.broadcast_to(qd[:, None, :], (H, 128, BS)).transpose(1, 0, 2).reshape(128, H * BS)
    ).astype(ml_dtypes.bfloat16)
    shared["kd"] = np.ascontiguousarray(kd.reshape(H, 2, 128).transpose(2, 0, 1).reshape(128, H * 2))
    dgt = dg.transpose(0, 2, 1).reshape(H, 2, 128, BS).transpose(2, 0, 1, 3)
    shared["dt"] = np.ascontiguousarray(dgt.reshape(128, H * 2 * BS)).astype(ml_dtypes.bfloat16)
    shared["ident"] = np.eye(128, dtype=ml_dtypes.bfloat16)
    shared["ones1"] = np.ones((128, 1), dtype=ml_dtypes.bfloat16)
    shared["onesr"] = np.ones((1, 128), dtype=np.float32)

    for nm, key in (("gq", "L0_ffn_gate_Wq"), ("uq", "L0_ffn_up_Wq")):
        wq = _u8_view(inp[key])
        wqt = np.ascontiguousarray(wq.T)
        r = wqt.reshape(16, 128, FI // 256, 256)
        shared[nm] = np.ascontiguousarray(r.transpose(2, 1, 0, 3))
    dq = _u8_view(inp["L0_ffn_down_Wq"])
    dqt = np.ascontiguousarray(dq.T[:, perm])
    r = dqt.reshape(43, 128, 16, 256)
    shared["dq"] = np.ascontiguousarray(r.transpose(2, 0, 1, 3))

    def s_re(sc, nchunks, ncols, colperm=None):
        a = np.ascontiguousarray(sc.T)
        if colperm is not None:
            a = np.ascontiguousarray(a[:, colperm])
        re = a.reshape(nchunks, 4, ncols).transpose(1, 0, 2)
        return np.ascontiguousarray(re).reshape(4, -1), a
    shared["gs_re"], shared["gsT"] = s_re(inp["L0_ffn_gate_scale"], 16, FI)
    _, shared["gzT"] = s_re(inp["L0_ffn_gate_zero"], 16, FI)
    shared["us_re"], shared["usT"] = s_re(inp["L0_ffn_up_scale"], 16, FI)
    _, shared["uzT"] = s_re(inp["L0_ffn_up_zero"], 16, FI)
    shared["ds_re"], shared["dsT"] = s_re(inp["L0_ffn_down_scale"], 43, C, perm)
    _, shared["dzT"] = s_re(inp["L0_ffn_down_zero"], 43, C, perm)

    band = np.zeros((128, 304), dtype=ml_dtypes.bfloat16)
    for p in range(128):
        band[p, 172 + p // 32] = 1
    shared["band"] = band
    pat4 = np.zeros((4, 128), dtype=ml_dtypes.bfloat16)
    for p in range(128):
        pat4[p // 32, p] = 1
    shared["pat4"] = pat4
    return shared, per_core



class Bld:
    ENG = ("t", "v", "s", "g", "y")

    def __init__(self):
        self.q = {e: [] for e in self.ENG}
        self.cnt = {}

    def sem(self, *names):
        for n in names:
            self.cnt[n] = 0

    def emit(self, eng, f, inc=None, k=1):
        if inc is not None:
            self.cnt[inc] += k

            def fn(e, S, f=f, inc=inc, k=k):
                f(e, S).then_inc(S[inc], k)
            self.q[eng].append(fn)
            return self.cnt[inc]
        self.q[eng].append(lambda e, S, f=f: f(e, S))
        return None

    def wait(self, eng, sem, val):
        if val > 0:
            self.q[eng].append(lambda e, S, sem=sem, val=val: e.wait_ge(S[sem], val))

    def drain(self, eng):
        self.q[eng].append(lambda e, S: e.drain())

    def dma(self, eng, sem, f):
        self.wait(eng, sem, self.cnt[sem])
        return self.emit(eng, f, inc=sem, k=16)


def build_nc(dbg=(), n_heads=H, do_p05=True, do_p1=True):
    nc = bass.Bass("TRN2", target_bir_lowering=False)
    dbg = set(dbg)
    _, _, _, bd_all = _decay_consts()

    xp = nc.dram_tensor("xp", [T, C], F32, kind="ExternalInput")
    xq = nc.dram_tensor("xq", [T, C], F32, kind="ExternalInput")
    qkvw = nc.dram_tensor("qkvw", [H, 3, 128, CCH * 128], F32, kind="ExternalInput")
    gatew = nc.dram_tensor("gatew", [CCH, 128, CCH * 128], F32, kind="ExternalInput")
    outw = nc.dram_tensor("outw", [CCH, 128, CCH * 128], F32, kind="ExternalInput")
    normw = nc.dram_tensor("normw", [128, CCH], F32, kind="ExternalInput")
    qd_t = nc.dram_tensor("qd", [128, H * BS], BF16, kind="ExternalInput")
    kd_t = nc.dram_tensor("kd", [128, H * 2], F32, kind="ExternalInput")
    dt_t = nc.dram_tensor("dt", [128, H * 2 * BS], BF16, kind="ExternalInput")
    ident_t = nc.dram_tensor("ident", [128, 128], BF16, kind="ExternalInput")
    ones1_t = nc.dram_tensor("ones1", [128, 1], BF16, kind="ExternalInput")
    onesr_t = nc.dram_tensor("onesr", [1, 128], F32, kind="ExternalInput")
    gq_t = nc.dram_tensor("gq", [43, 128, 16, 256], U8, kind="ExternalInput")
    uq_t = nc.dram_tensor("uq", [43, 128, 16, 256], U8, kind="ExternalInput")
    dq_t = nc.dram_tensor("dq", [16, 43, 128, 256], U8, kind="ExternalInput")
    sc_in = {}
    for nm, shp in (("gs_re", [4, 16 * FI]), ("us_re", [4, 16 * FI]), ("ds_re", [4, 43 * C]),
                    ("gsT", [64, FI]), ("gzT", [64, FI]), ("usT", [64, FI]), ("uzT", [64, FI]),
                    ("dsT", [172, C]), ("dzT", [172, C])):
        sc_in[nm] = nc.dram_tensor(nm, shp, F32, kind="ExternalInput")
    band_t = nc.dram_tensor("band", [128, 304], BF16, kind="ExternalInput")
    pat4_t = nc.dram_tensor("pat4", [4, 128], BF16, kind="ExternalInput")

    out_t = nc.dram_tensor("out", [T, C], F32, kind="ExternalOutput")

    xbf = nc.dram_tensor("xbf", [T, C], BF16, kind="Internal")
    xqbf = nc.dram_tensor("xqbf", [T, C], BF16, kind="Internal")
    a2d = nc.dram_tensor("a2d", [C, T], BF16, kind="Internal")
    dpd = nc.dram_tensor("dpd", [C, T], BF16, kind="Internal")
    zs_g = nc.dram_tensor("zs_g", [64, FI], BF16, kind="Internal")
    zs_u = nc.dram_tensor("zs_u", [64, FI], BF16, kind="Internal")
    zs_d = nc.dram_tensor("zs_d", [172, C], BF16, kind="Internal")
    s1_g = nc.dram_tensor("s1_g", [4, 16 * FI], BF16, kind="Internal")
    s1_u = nc.dram_tensor("s1_u", [4, 16 * FI], BF16, kind="Internal")
    s1_d = nc.dram_tensor("s1_d", [4, 43 * C], BF16, kind="Internal")

    dbg_t = {}
    for name, shp, dt in (("attn", [C, T], F32), ("r1", [C, T], F32),
                          ("kvin", [128, H * 128], F32)):
        if name in dbg:
            dbg_t[name] = nc.dram_tensor("dbg_" + name, shp, dt, kind="ExternalOutput")

    b = Bld()
    b.sem("dc", "db", "dx", "ds0", "ds1", "ds2", "dd0", "dd1",
          "dw0", "dw1", "dfb", "dfs", "dfz", "dzt", "ddp", "dsa", "dtm", "dfo", "ds1c",
          "pe", "dv", "ac", "po", "fin")

    with ExitStack() as st:
        def sb(name, shape, dtype):
            return st.enter_context(nc.sbuf_tensor(name, shape, dtype))

        x_sb = sb("x_sb", [128, CCH * T], BF16)
        at_sb = sb("at_sb", [128, CCH * T], BF16)
        wsl = sb("wsl", [128, 3 * 4096], BF16)
        qkvt = sb("qkvt", [128, 3 * T], BF16)
        q2 = sb("q2", [128, T], BF16)
        ssb = sb("ssb", [128, 512], BF16)
        ktm = sb("ktm", [128, 256], BF16)
        vtm = sb("vtm", [128, 256], BF16)
        stf = sb("stf", [128, 2 * 128], F32)
        kvbf = sb("kvbf", [128, 128], BF16)
        gaux = sb("gaux", [128, 14848], BF16)
        kvin = gaux[:, 9216:9216 + H * 128]
        qdsb = gaux[:, 0:H * BS]
        kdsb = sb("kdsb", [128, H * 2], F32)
        dtsb = gaux[:, 8192:8192 + 1024]
        identb = sb("identb", [128, 128], BF16)
        ones1b = sb("ones1b", [128, 1], BF16)
        onesrb = sb("onesrb", [1, 128], F32)
        normwb = sb("normwb", [128, CCH], F32)
        fout = sb("fout", [128, 3072], F32)
        rows = fout[0:1, 0:2048]
        rwork = fout[0:1, 2048:3072]
        bandb = sb("bandb", [128, 304], BF16)
        pat4b = sb("pat4b", [4, 128], BF16)

        pbk = [st.enter_context(nc.psum_tensor(f"pb{i}", [128, 512], F32)) for i in range(7)]
        pbf = st.enter_context(nc.psum_tensor("pbf", [128, 1024], BF16))

        sems = {n: st.enter_context(nc.semaphore(n)) for n in b.cnt}

        b.emit("g", lambda e, S: e.dma_start(out=xbf[:, :], in_=xp[:, :]), inc="db", k=16)
        b.emit("g", lambda e, S: e.dma_start(out=xqbf[:, :], in_=xq[:, :]), inc="db", k=16)
        b.emit("g", lambda e, S: e.dma_start(out=qdsb[:, :], in_=qd_t[:, :]), inc="dc", k=16)
        b.emit("g", lambda e, S: e.dma_start(out=kdsb[:, :], in_=kd_t[:, :]), inc="dc", k=16)
        b.emit("g", lambda e, S: e.dma_start(out=identb[:, :], in_=ident_t[:, :]), inc="dc", k=16)
        b.emit("g", lambda e, S: e.dma_start(out=ones1b[:, :], in_=ones1_t[:, :]), inc="dc", k=16)
        b.emit("g", lambda e, S: e.dma_start(out=onesrb[:, :], in_=onesr_t[:, :]), inc="dc", k=16)
        b.emit("g", lambda e, S: e.dma_start(out=normwb[:, :], in_=normw[:, :]), inc="dc", k=16)
        b.emit("g", lambda e, S: e.dma_start(out=bandb[:, :], in_=band_t[:, :]), inc="dc", k=16)
        b.emit("g", lambda e, S: e.dma_start(out=pat4b[:, :], in_=pat4_t[:, :]), inc="dc", k=16)
        consts_done = b.cnt["dc"]

        b.wait("y", "db", 32)
        for cc in range(CCH):
            b.emit("y", lambda e, S, cc=cc: e.dma_start_transpose(
                x_sb[:, cc * T:(cc + 1) * T], xbf[:, cc * 128:(cc + 1) * 128]), inc="dx", k=16)
        for cc in range(CCH):
            b.emit("y", lambda e, S, cc=cc: e.dma_start_transpose(
                at_sb[:, cc * T:(cc + 1) * T], xqbf[:, cc * 128:(cc + 1) * 128]), inc="dx", k=16)
        x_loaded = b.cnt["dx"]

        NSLOT = 3
        slot_of = {}
        slab_ready = {}
        slab_guard = [0] * NSLOT
        next_slot = [0]

        def load_slab(h, z):
            sl = next_slot[0] % NSLOT
            next_slot[0] += 1
            slot_of[(h, z)] = sl
            b.wait("g", "pe", slab_guard[sl])
            sem = f"ds{sl}"
            m = b.dma("g", sem, lambda e, S, h=h, z=z, sl=sl: e.dma_start(
                out=wsl[:, sl * 4096:(sl + 1) * 4096], in_=qkvw[h, z, :, :]))
            slab_ready[(h, z)] = (sem, m)

        def qkv_matmuls(h, z, comp_idx, src):
            sl = slot_of[(h, z)]
            sem, m = slab_ready[(h, z)]
            b.wait("t", sem, m)
            b.wait("t", "dv", b.cnt["dv"])
            for th in range(2):
                bank = pbk[2 * comp_idx + th]
                for cc in range(CCH):
                    b.emit("t", lambda e, S, bank=bank, sl=sl, cc=cc, th=th, src=src:
                           e.matmul(bank[:, :],
                                    wsl[:, sl * 4096 + cc * 128: sl * 4096 + (cc + 1) * 128],
                                    src[:, cc * T + th * TH: cc * T + th * TH + TH],
                                    start=(cc == 0), stop=(cc == CCH - 1)),
                           inc="pe" if cc == CCH - 1 else None)
            pe_m = b.cnt["pe"]
            b.wait("s", "pe", pe_m)
            for th in range(2):
                bank = pbk[2 * comp_idx + th]
                b.emit("s", lambda e, S, bank=bank, comp_idx=comp_idx, th=th:
                       e.activation(qkvt[:, comp_idx * T + th * TH: comp_idx * T + th * TH + TH],
                                    bank[:, :], AF.Silu), inc="ac")
            slab_guard[sl] = pe_m
            return b.cnt["ac"]

        def scan_block(h, n, with_outputs, phase1):
            bd = float(bd_all[h])
            par = n % 2
            kcol = T + n * BS
            vcol = 2 * T + n * BS
            kvin_col = h * 128
            b.wait("t", "dv", b.cnt["dv"])
            for sub in range(2):
                b.emit("t", lambda e, S, kcol=kcol, sub=sub:
                       e.transpose(pbf[:, sub * 128:(sub + 1) * 128],
                                   qkvt[:, kcol + sub * 128: kcol + (sub + 1) * 128], identb[:, :]))
            for sub in range(2):
                b.emit("t", lambda e, S, vcol=vcol, sub=sub:
                       e.transpose(pbf[:, 256 + sub * 128: 256 + (sub + 1) * 128],
                                   qkvt[:, vcol + sub * 128: vcol + (sub + 1) * 128], identb[:, :]),
                       inc="pe" if sub == 1 else None)
            tr_done = b.cnt["pe"]
            if with_outputs:
                qcol = n * BS
                for sub in range(2):
                    b.emit("t", lambda e, S, kcol=kcol, qcol=qcol, sub=sub:
                           e.matmul(pbk[6][:, sub * 256:(sub + 1) * 256],
                                    qkvt[:, kcol + sub * 128: kcol + (sub + 1) * 128],
                                    qkvt[:, qcol: qcol + 256], start=True, stop=True),
                           inc="pe" if sub == 1 else None)
                s_done = b.cnt["pe"]
                b.wait("v", "pe", s_done)
                dslot = (h % 2) * 512
                for sub in range(2):
                    b.emit("v", lambda e, S, sub=sub, dslot=dslot:
                           e.tensor_mul(ssb[:, sub * 256:(sub + 1) * 256],
                                        pbk[6][:, sub * 256:(sub + 1) * 256],
                                        dtsb[:, dslot + sub * 256: dslot + (sub + 1) * 256]),
                           inc="dv" if sub == 1 else None)
            b.wait("v", "pe", tr_done)
            for sub in range(2):
                b.emit("v", lambda e, S, sub=sub, h=h:
                       e.tensor_scalar(ktm[:, sub * 128:(sub + 1) * 128],
                                       pbf[:, sub * 128:(sub + 1) * 128],
                                       kdsb[:, h * 2 + sub: h * 2 + sub + 1], None, ALU.mult))
            b.emit("v", lambda e, S: e.tensor_copy(vtm[:, :], pbf[:, 256:512]), inc="dv")
            ev_done = b.cnt["dv"]
            b.wait("t", "dv", ev_done)
            if with_outputs:
                qcol = n * BS
                for sub in range(2):
                    b.emit("t", lambda e, S, sub=sub:
                           e.matmul(pbk[1][:, 0:256], vtm[:, sub * 128:(sub + 1) * 128],
                                    ssb[:, sub * 256:(sub + 1) * 256],
                                    start=(sub == 0), stop=False))
                kv_ap = (kvin[:, kvin_col:kvin_col + 128] if n == 0 else kvbf[:, :])
                b.emit("t", lambda e, S, kv_ap=kv_ap, qcol=qcol:
                       e.matmul(pbk[1][:, 0:256], kv_ap, q2[:, qcol:qcol + 256],
                                start=False, stop=True), inc="pe")
                att_done = b.cnt["pe"]
                b.wait("v", "pe", att_done)
                b.emit("v", lambda e, S, h=h, n=n:
                       e.tensor_copy(at_sb[:, h * T + n * BS: h * T + (n + 1) * BS],
                                     pbk[1][:, 0:256]), inc="dv")
            for sub in range(2):
                b.emit("t", lambda e, S, sub=sub:
                       e.matmul(pbk[0][:, 0:128], ktm[:, sub * 128:(sub + 1) * 128],
                                vtm[:, sub * 128:(sub + 1) * 128],
                                start=(sub == 0), stop=(sub == 1)),
                       inc="pe" if sub == 1 else None)
            c_done = b.cnt["pe"]
            b.wait("v", "pe", c_done)
            b.drain("v")
            src_st = stf[:, par * 128:(par + 1) * 128]
            dst_st = stf[:, (1 - par) * 128:(2 - par) * 128]
            if n == 0:
                if phase1:
                    b.emit("v", lambda e, S, dst=dst_st, bd=bd, kvin_col=kvin_col:
                           e.scalar_tensor_tensor(dst, kvin[:, kvin_col:kvin_col + 128], bd,
                                                  pbk[0][:, 0:128], ALU.mult, ALU.add))
                else:
                    b.emit("v", lambda e, S, dst=dst_st: e.tensor_copy(dst, pbk[0][:, 0:128]))
            else:
                b.emit("v", lambda e, S, src=src_st, dst=dst_st, bd=bd:
                       e.scalar_tensor_tensor(dst, src, bd, pbk[0][:, 0:128], ALU.mult, ALU.add))
            b.drain("v")
            b.emit("v", lambda e, S, dst=dst_st: e.tensor_copy(kvbf[:, :], dst), inc="dv")

        b.emit("g", lambda e, S: e.dma_start(out=s1_g[:, :], in_=sc_in["gs_re"][:, :]), inc="ds1c", k=16)
        b.emit("g", lambda e, S: e.dma_start(out=s1_u[:, :], in_=sc_in["us_re"][:, :]), inc="ds1c", k=16)
        b.emit("g", lambda e, S: e.dma_start(out=s1_d[:, :], in_=sc_in["ds_re"][:, :]), inc="ds1c", k=16)
        s1_done = b.cnt["ds1c"]
        for sT_n, zT_n, dst in (("gsT", "gzT", zs_g), ("usT", "uzT", zs_u)):
            for c0, c1 in ((0, 5504), (5504, FI)):
                w = c1 - c0
                b.wait("g", "dfb", b.cnt["dfb"])
                b.dma("g", "dfz", lambda e, S, sT_n=sT_n, c0=c0, c1=c1, w=w: e.dma_start(
                    out=wsl[0:64, 0:w], in_=sc_in[sT_n][:, c0:c1]))
                lm = b.dma("g", "dfz", lambda e, S, zT_n=zT_n, c0=c0, c1=c1, w=w: e.dma_start(
                    out=wsl[0:64, 5504:5504 + w], in_=sc_in[zT_n][:, c0:c1]))
                b.wait("g", "dfz", lm)
                b.emit("g", lambda e, S, w=w: e.tensor_mul(
                    wsl[0:64, 5504:5504 + w], wsl[0:64, 5504:5504 + w], wsl[0:64, 0:w]),
                    inc="po")
                b.wait("g", "po", b.cnt["po"])
                b.dma("g", "dfb", lambda e, S, dst=dst, c0=c0, c1=c1, w=w: e.dma_start(
                    out=dst[:, c0:c1], in_=wsl[0:64, 5504:5504 + w]))
        for r0_, r1_ in ((0, 128), (128, 172)):
            nrw = r1_ - r0_
            b.wait("g", "dfb", b.cnt["dfb"])
            b.dma("g", "dfz", lambda e, S, r0_=r0_, nrw=nrw: e.dma_start(
                out=wsl[0:nrw, 0:C], in_=sc_in["dsT"][r0_:r0_ + nrw, :]))
            lm = b.dma("g", "dfz", lambda e, S, r0_=r0_, nrw=nrw: e.dma_start(
                out=wsl[0:nrw, 4096:4096 + C], in_=sc_in["dzT"][r0_:r0_ + nrw, :]))
            b.wait("g", "dfz", lm)
            b.emit("g", lambda e, S, nrw=nrw: e.tensor_mul(
                wsl[0:nrw, 8192:8192 + C], wsl[0:nrw, 4096:4096 + C],
                wsl[0:nrw, 0:C]), inc="po")
            b.wait("g", "po", b.cnt["po"])
            b.dma("g", "dfb", lambda e, S, r0_=r0_, nrw=nrw: e.dma_start(
                out=zs_d[r0_:r0_ + nrw, :], in_=wsl[0:nrw, 8192:8192 + C]))
        b.wait("g", "dfb", b.cnt["dfb"])

        b.emit("g", lambda e, S: e.memset(kvin[:, :], 0), inc="po")
        b.wait("t", "dx", x_loaded)
        b.wait("t", "dc", consts_done)
        b.wait("v", "dc", consts_done)
        b.wait("v", "dx", x_loaded)
        for h in range(n_heads if do_p05 else 0):
            load_slab(h, 1)
            load_slab(h, 2)
            qkv_matmuls(h, 1, 1, at_sb)
            qkv_matmuls(h, 2, 2, at_sb)
            b.wait("t", "ac", b.cnt["ac"])
            for n in range(NB):
                scan_block(h, n, with_outputs=False, phase1=False)
            b.drain("v")
            fp = NB % 2
            b.emit("v", lambda e, S, h=h, fp=fp:
                   e.tensor_copy(kvin[:, h * 128:(h + 1) * 128], stf[:, fp * 128:(fp + 1) * 128]),
                   inc="dv")

        if "kvin" in dbg:
            b.wait("g", "dv", b.cnt["dv"])
            b.emit("g", lambda e, S: e.dma_start(out=dbg_t["kvin"][:, :], in_=kvin[:, :]),
                   inc="fin", k=16)

        for h in range(n_heads if do_p1 else 0):
            load_slab(h, 0)
            load_slab(h, 1)
            load_slab(h, 2)
            dslot = (h % 2) * 512
            dsem = f"dd{h % 2}"
            b.wait("y", "dv", b.cnt["dv"])
            d_ready = b.dma("y", dsem, lambda e, S, h=h, dslot=dslot: e.dma_start(
                out=dtsb[:, dslot:dslot + 512], in_=dt_t[:, h * 512:(h + 1) * 512]))
            qkv_matmuls(h, 0, 0, x_sb)
            qkv_matmuls(h, 1, 1, x_sb)
            qkv_matmuls(h, 2, 2, x_sb)
            ac_m = b.cnt["ac"]
            b.wait("v", "ac", ac_m)
            b.wait("v", dsem, d_ready)
            for n in range(NB):
                b.emit("v", lambda e, S, h=h, n=n:
                       e.tensor_mul(q2[:, n * BS:(n + 1) * BS],
                                    qkvt[:, n * BS:(n + 1) * BS],
                                    qdsb[:, h * BS:(h + 1) * BS]),
                       inc="dv" if n == NB - 1 else None)
            b.wait("t", "dv", b.cnt["dv"])
            for n in range(NB):
                scan_block(h, n, with_outputs=True, phase1=True)

        if "attn" in dbg:
            b.wait("g", "dv", b.cnt["dv"])
            for cc in range(CCH):
                b.emit("g", lambda e, S, cc=cc: e.dma_start(
                    out=dbg_t["attn"][cc * 128:(cc + 1) * 128, :],
                    in_=at_sb[:, cc * T:(cc + 1) * T]), inc="fin", k=16)

        SQ0, SQ1 = 8192, 9216
        GS0, GS1 = 8192, 9216
        RSTD_BC, MEAN_BC = 12288, 13312

        sq_guard = [0, 0]
        attn_done = b.cnt["dv"]
        for cc in range(CCH):
            sl = cc % 2
            b.wait("v", "pe", sq_guard[sl])
            b.drain("v")
            b.emit("v", lambda e, S, cc=cc, sl=sl:
                   e.tensor_mul(gaux[:, sl * 1024:(sl + 1) * 1024],
                                at_sb[:, cc * T:(cc + 1) * T], at_sb[:, cc * T:(cc + 1) * T]),
                   inc="dv")
            dv_m = b.cnt["dv"]
            b.wait("t", "dv", dv_m)
            for th in range(2):
                b.emit("t", lambda e, S, sl=sl, th=th, cc=cc:
                       e.matmul(pbk[4 + th][0:1, :], ones1b[:, :],
                                gaux[:, sl * 1024 + th * TH: sl * 1024 + th * TH + TH],
                                start=(cc == 0), stop=(cc == CCH - 1)),
                       inc="pe" if th == 1 else None)
            sq_guard[sl] = b.cnt["pe"]
        rms_mm = b.cnt["pe"]
        b.wait("v", "pe", rms_mm)
        b.drain("v")
        b.emit("v", lambda e, S: e.tensor_copy(rows[0:1, 0:512], pbk[4][0:1, :]))
        b.emit("v", lambda e, S: e.tensor_copy(rows[0:1, 512:1024], pbk[5][0:1, :]))
        b.drain("v")
        b.emit("v", lambda e, S: e.tensor_scalar(rows[0:1, 0:1024], rows[0:1, 0:1024],
                                                 1.0 / C, EPS, ALU.mult, ALU.add), inc="dv")
        b.wait("s", "dv", b.cnt["dv"])
        b.emit("s", lambda e, S: e.activation(rows[0:1, 1024:2048], rows[0:1, 0:1024], AF.Sqrt),
               inc="ac")
        b.wait("v", "ac", b.cnt["ac"])
        b.emit("v", lambda e, S: e.reciprocal(rows[0:1, 0:1024], rows[0:1, 1024:2048]), inc="dv")
        b.wait("t", "dv", b.cnt["dv"])
        for th in range(2):
            b.emit("t", lambda e, S, th=th:
                   e.matmul(pbk[4 + th][:, :], onesrb[:, :], rows[0:1, th * TH:(th + 1) * TH],
                            start=True, stop=True), inc="pe" if th == 1 else None)
        b.wait("s", "pe", b.cnt["pe"])
        for th in range(2):
            b.emit("s", lambda e, S, th=th:
                   e.activation(gaux[:, RSTD_BC + th * TH: RSTD_BC + th * TH + TH],
                                pbk[4 + th][:, :], AF.Copy), inc="ac")
        rstd_bc_done = b.cnt["ac"]

        for dsn in ("ds0", "ds1", "ds2"):
            b.wait("g", dsn, b.cnt[dsn])
        wslot_guard = [b.cnt["pe"], b.cnt["pe"]]
        gpsum_guard = [0, 0]
        gstage_guard = [0, 0]
        z_first = True
        for oc in range(CCH):
            sl = oc % 2
            wsem = f"dw{sl}"
            b.wait("g", "pe", wslot_guard[sl])
            wm = b.dma("g", wsem, lambda e, S, oc=oc, sl=sl: e.dma_start(
                out=wsl[:, sl * 4096:(sl + 1) * 4096], in_=gatew[oc, :, :]))
            b.wait("t", wsem, wm)
            b.wait("t", "ac", gpsum_guard[sl])
            for th in range(2):
                bank = pbk[2 * sl + th]
                for cc in range(CCH):
                    b.emit("t", lambda e, S, bank=bank, sl=sl, cc=cc, th=th:
                           e.matmul(bank[:, :],
                                    wsl[:, sl * 4096 + cc * 128: sl * 4096 + (cc + 1) * 128],
                                    x_sb[:, cc * T + th * TH: cc * T + th * TH + TH],
                                    start=(cc == 0), stop=(cc == CCH - 1)),
                           inc="pe" if cc == CCH - 1 else None)
            wslot_guard[sl] = b.cnt["pe"]
            b.wait("s", "pe", b.cnt["pe"])
            if z_first:
                b.wait("s", "ac", rstd_bc_done)
            b.wait("s", "dv", gstage_guard[sl])
            for th in range(2):
                b.emit("s", lambda e, S, sl=sl, th=th:
                       e.activation(wsl[:, (GS0 if sl == 0 else GS1) + th * TH:
                                        (GS0 if sl == 0 else GS1) + th * TH + TH],
                                    pbk[2 * sl + th][:, :], AF.Sigmoid), inc="ac")
            gpsum_guard[sl] = b.cnt["ac"]
            b.wait("v", "ac", b.cnt["ac"])
            if z_first:
                b.wait("v", "ac", rstd_bc_done)
                z_first = False
            b.drain("v")
            b.emit("v", lambda e, S, oc=oc:
                   e.tensor_mul(q2[:, :], at_sb[:, oc * T:(oc + 1) * T],
                                gaux[:, RSTD_BC:RSTD_BC + 1024]))
            b.drain("v")
            b.emit("v", lambda e, S, oc=oc, sl=sl:
                   e.scalar_tensor_tensor(at_sb[:, oc * T:(oc + 1) * T], q2[:, :],
                                          normwb[:, oc:oc + 1],
                                          wsl[:, (GS0 if sl == 0 else GS1):(GS0 if sl == 0 else GS1) + 1024],
                                          ALU.mult, ALU.mult), inc="dv")
            gstage_guard[sl] = b.cnt["dv"]

        a_guard = 0
        b.wait("t", "ac", b.cnt["ac"])
        b.wait("t", "dv", b.cnt["dv"])
        for oc in range(CCH):
            sl = oc % 2
            wsem = f"dw{sl}"
            b.wait("g", "pe", wslot_guard[sl])
            wm = b.dma("g", wsem, lambda e, S, oc=oc, sl=sl: e.dma_start(
                out=wsl[:, sl * 4096:(sl + 1) * 4096], in_=outw[oc, :, :]))
            b.wait("t", wsem, wm)
            b.wait("t", "dv", a_guard)
            for th in range(2):
                for cc in range(CCH):
                    b.emit("t", lambda e, S, sl=sl, cc=cc, th=th:
                           e.matmul(pbk[th][:, :],
                                    wsl[:, sl * 4096 + cc * 128: sl * 4096 + (cc + 1) * 128],
                                    at_sb[:, cc * T + th * TH: cc * T + th * TH + TH],
                                    start=(cc == 0), stop=(cc == CCH - 1)),
                           inc="pe" if cc == CCH - 1 else None)
            wslot_guard[sl] = b.cnt["pe"]
            b.wait("v", "pe", b.cnt["pe"])
            for th in range(2):
                b.emit("v", lambda e, S, oc=oc, th=th:
                       e.tensor_add(x_sb[:, oc * T + th * TH: oc * T + th * TH + TH],
                                    x_sb[:, oc * T + th * TH: oc * T + th * TH + TH],
                                    pbk[th][:, :]), inc="dv" if th == 1 else None)
            a_guard = b.cnt["dv"]
            b.wait("v", "pe", sq_guard[sl])
            b.drain("v")
            b.emit("v", lambda e, S, oc=oc, sl=sl:
                   e.tensor_mul(gaux[:, sl * 1024:(sl + 1) * 1024],
                                x_sb[:, oc * T:(oc + 1) * T], x_sb[:, oc * T:(oc + 1) * T]),
                   inc="dv")
            b.wait("t", "dv", b.cnt["dv"])
            for th in range(2):
                b.emit("t", lambda e, S, oc=oc, th=th:
                       e.matmul(pbk[4 + th][0:1, :], ones1b[:, :],
                                x_sb[:, oc * T + th * TH: oc * T + th * TH + TH],
                                start=(oc == 0), stop=(oc == CCH - 1)))
            for th in range(2):
                b.emit("t", lambda e, S, oc=oc, sl=sl, th=th:
                       e.matmul(pbk[2 + th][0:1, :], ones1b[:, :],
                                gaux[:, (0 if sl == 0 else 1024) + th * TH:
                                     (0 if sl == 0 else 1024) + th * TH + TH],
                                start=(oc == 0), stop=(oc == CCH - 1)),
                       inc="pe" if th == 1 else None)
            sq_guard[sl] = b.cnt["pe"]
        ln1_mm = b.cnt["pe"]
        b.wait("v", "pe", ln1_mm)
        b.emit("v", lambda e, S: e.tensor_copy(rows[0:1, 0:512], pbk[4][0:1, :]))
        b.emit("v", lambda e, S: e.tensor_copy(rows[0:1, 512:1024], pbk[5][0:1, :]))
        b.emit("v", lambda e, S: e.tensor_copy(rows[0:1, 1024:1536], pbk[2][0:1, :]))
        b.emit("v", lambda e, S: e.tensor_copy(rows[0:1, 1536:2048], pbk[3][0:1, :]))
        b.drain("v")
        b.emit("v", lambda e, S: e.tensor_scalar(rows[0:1, 0:1024], rows[0:1, 0:1024],
                                                 1.0 / C, None, ALU.mult))
        b.drain("v")
        b.emit("v", lambda e, S: e.tensor_mul(rwork, rows[0:1, 0:1024],
                                              rows[0:1, 0:1024]))
        b.drain("v")
        b.emit("v", lambda e, S: e.scalar_tensor_tensor(rows[0:1, 1024:2048], rows[0:1, 1024:2048],
                                                        1.0 / C, rwork,
                                                        ALU.mult, ALU.subtract))
        b.drain("v")
        b.emit("v", lambda e, S: e.tensor_scalar(rows[0:1, 1024:2048], rows[0:1, 1024:2048],
                                                 EPS, None, ALU.add), inc="dv")
        b.wait("s", "dv", b.cnt["dv"])
        b.emit("s", lambda e, S: e.activation(rwork, rows[0:1, 1024:2048], AF.Sqrt),
               inc="ac")
        b.wait("v", "ac", b.cnt["ac"])
        b.emit("v", lambda e, S: e.reciprocal(rows[0:1, 1024:2048], rwork), inc="dv")
        b.wait("t", "dv", b.cnt["dv"])
        for th in range(2):
            b.emit("t", lambda e, S, th=th:
                   e.matmul(pbk[4 + th][:, :], onesrb[:, :], rows[0:1, th * TH:(th + 1) * TH],
                            start=True, stop=True))
        for th in range(2):
            b.emit("t", lambda e, S, th=th:
                   e.matmul(pbk[2 + th][:, :], onesrb[:, :], rows[0:1, 1024 + th * TH:1024 + (th + 1) * TH],
                            start=True, stop=True), inc="pe" if th == 1 else None)
        b.wait("s", "pe", b.cnt["pe"])
        for th in range(2):
            b.emit("s", lambda e, S, th=th:
                   e.activation(gaux[:, MEAN_BC + th * TH: MEAN_BC + th * TH + TH],
                                pbk[4 + th][:, :], AF.Copy))
        for th in range(2):
            b.emit("s", lambda e, S, th=th:
                   e.activation(gaux[:, RSTD_BC + th * TH: RSTD_BC + th * TH + TH],
                                pbk[2 + th][:, :], AF.Copy), inc="ac")
        ln1_bc = b.cnt["ac"]
        b.wait("v", "ac", ln1_bc)
        for cc in range(CCH):
            b.drain("v")
            b.emit("v", lambda e, S, cc=cc:
                   e.tensor_sub(q2[:, :], x_sb[:, cc * T:(cc + 1) * T],
                                gaux[:, MEAN_BC:MEAN_BC + 1024]))
            b.drain("v")
            b.emit("v", lambda e, S, cc=cc:
                   e.tensor_mul(x_sb[:, cc * T:(cc + 1) * T], q2[:, :],
                                gaux[:, RSTD_BC:RSTD_BC + 1024]), inc="dv")
        r1_done = b.cnt["dv"]

        if "r1" in dbg:
            b.wait("g", "dv", r1_done)
            for cc in range(CCH):
                b.emit("g", lambda e, S, cc=cc: e.dma_start(
                    out=dbg_t["r1"][cc * 128:(cc + 1) * 128, :],
                    in_=x_sb[:, cc * T:(cc + 1) * T]), inc="fin", k=16)

        HI_T, LO_T, S_SL = 0, 4096, 8192
        XSUM, HSUM = 12288, 13312
        ZSD_T, ZSG_T = 14336, 14592
        dpo = qkvt[:, 2048:3072]

        b.wait("t", "dv", r1_done)
        for cc in range(CCH):
            jc = cc // 2
            for th in range(2):
                b.emit("t", lambda e, S, cc=cc, jc=jc, th=th:
                       e.matmul(pbk[th][:, :], bandb[:, 172 - 4 * jc: 300 - 4 * jc],
                                x_sb[:, cc * T + th * TH: cc * T + th * TH + TH],
                                start=(cc == 0), stop=(cc == CCH - 1)),
                       inc="pe" if (cc == CCH - 1 and th == 1) else None)
        pe_ffn0 = b.cnt["pe"]
        b.wait("v", "pe", pe_ffn0)
        b.emit("v", lambda e, S: e.tensor_scalar(gaux[0:64, XSUM:XSUM + 512],
                                                 pbk[0][0:64, :], -1.0, None, ALU.mult))
        b.emit("v", lambda e, S: e.tensor_scalar(gaux[0:64, XSUM + 512:XSUM + 1024],
                                                 pbk[1][0:64, :], -1.0, None, ALU.mult),
               inc="dv")
        xsum_done = b.cnt["dv"]

        b.wait("y", "ds1c", s1_done)
        b.wait("y", "dd0", b.cnt["dd0"])
        b.wait("y", "dd1", b.cnt["dd1"])
        b.wait("y", "dv", r1_done)
        b.wait("g", "dw0", b.cnt["dw0"])
        b.wait("g", "dw1", b.cnt["dw1"])
        PH_R = [(0, 15), (15, 29), (29, 43)]
        H135 = 135.53125
        HB = 2048
        ZSL = (ZSD_T, ZSG_T)

        units = []
        for ph, (rr0, rr1) in enumerate(PH_R):
            for r in range(rr0, rr1):
                for X in ("g", "u"):
                    units.append(("gu", ph, r, X, 0))
                    units.append(("gu", ph, r, X, 1))
            for how in range(16):
                units.append(("dn", ph, 0, how, 0))
                units.append(("dn", ph, 0, how, 1))

        st_deq_dv = {}
        st_pe_free = {}
        init_dv = r1_done
        init_pe = pe_ffn0
        zs_free = [init_pe, init_pe]
        zs_mark = {}
        tile_ctr = [0]
        ddp_m = {}
        ms = {
            "g_bank_dv": xsum_done,
            "silu_ac": 0,
            "h_evac_dv": 0,
            "pending_hsum": None,
            "hsum_done": 0,
            "dn_evac_dv": [0, 0],
            "dpp_guard": [0, 0],
        }

        def deq_unit(i):
            kind = units[i][0]
            pp = i % 2
            g2 = st_deq_dv.get(i - 2, init_dv)
            p2 = st_pe_free.get(i - 2, init_pe)
            if kind == "gu":
                _, ph, r, X, h = units[i]
                w = HB
                wq_src = gq_t if X == "g" else uq_t
                s1w = s1_g if X == "g" else s1_u
                b.wait("g", "dv", g2)
                bm = b.dma("g", "dfb", lambda e, S, wq_src=wq_src, r=r, h=h, pp=pp: e.dma_start(
                    out=wsl[:, pp * HB: pp * HB + HB],
                    in_=wq_src[r, :, 8 * h:8 * h + 8, :].rearrange("p cb i -> p (cb i)")))
                b.wait("y", "dv", g2)
                b.wait("y", "dfs", b.cnt["dfs"])
                for gs4 in range(4):
                    b.emit("y", lambda e, S, s1w=s1w, r=r, h=h, gs4=gs4, pp=pp: e.dma_start(
                        out=gaux[gs4 * 32:(gs4 + 1) * 32, S_SL + pp * HB:S_SL + pp * HB + HB],
                        in_=bass.AP(s1w, gs4 * 16 * FI + 8 * h * FI + 256 * r,
                                    [[0, 32], [FI, 8], [1, 256]])), inc="dfs", k=16)
                sm = b.cnt["dfs"]
                if h == 0:
                    zsd = zs_g if X == "g" else zs_u
                    sl = tile_ctr[0] % 2
                    tile_ctr[0] += 1
                    b.wait("y", "pe", zs_free[sl])
                    zs_mark[("gu", r, X)] = (sl, b.dma(
                        "y", "dzt", lambda e, S, zsd=zsd, r=r, sl=sl: e.dma_start(
                            out=gaux[0:64, ZSL[sl]:ZSL[sl] + 256],
                            in_=zsd[0:64, 256 * r:256 * r + 256])))
            else:
                _, ph, _, how, h = units[i]
                rr0, rr1 = PH_R[ph]
                nr = rr1 - rr0
                k0, nh = (0, 8) if h == 0 else (8, nr - 8)
                w = nh * 256
                b.wait("g", "dv", g2)
                bm = b.dma("g", "dfb", lambda e, S, how=how, rr0=rr0, k0=k0, nh=nh, pp=pp: e.dma_start(
                    out=wsl[:, pp * HB: pp * HB + nh * 256].rearrange("p (ib i) -> p ib i", ib=nh),
                    in_=dq_t[how, rr0 + k0:rr0 + k0 + nh].rearrange("ib p i -> p ib i")))
                b.wait("y", "dv", g2)
                b.wait("y", "dfs", b.cnt["dfs"])
                for gs4 in range(4):
                    b.emit("y", lambda e, S, how=how, rr0=rr0, k0=k0, nh=nh, gs4=gs4, pp=pp: e.dma_start(
                        out=gaux[gs4 * 32:(gs4 + 1) * 32, S_SL + pp * HB:S_SL + pp * HB + nh * 256],
                        in_=bass.AP(s1_d, gs4 * 43 * C + (rr0 + k0) * C + 256 * how,
                                    [[0, 32], [C, nh], [1, 256]])), inc="dfs", k=16)
                sm = b.cnt["dfs"]
                if h == 0:
                    sl = tile_ctr[0] % 2
                    tile_ctr[0] += 1
                    b.wait("y", "pe", zs_free[sl])
                    zs_mark[("dn", ph, how)] = (sl, b.dma(
                        "y", "dzt", lambda e, S, rr0=rr0, nr=nr, how=how, sl=sl: e.dma_start(
                            out=gaux[0:4 * nr, ZSL[sl]:ZSL[sl] + 256],
                            in_=zs_d[4 * rr0:4 * rr0 + 4 * nr, 256 * how:256 * how + 256])))
            b.wait("s", "dfb", bm)
            b.wait("s", "dv", g2)
            b.emit("s", lambda e, S, pp=pp, w=w: e.activation(
                wsl[:, 4096 + pp * HB:4096 + pp * HB + w], wsl[:, pp * HB:pp * HB + w],
                AF.Copy, bias=H135, scale=0.0625), inc="ac")
            b.wait("v", "ac", b.cnt["ac"])
            b.emit("v", lambda e, S, pp=pp, w=w: e.tensor_scalar(
                wsl[:, 8192 + pp * HB:8192 + pp * HB + w],
                wsl[:, 4096 + pp * HB:4096 + pp * HB + w], 136.0, None, ALU.subtract))
            b.drain("v")
            b.emit("v", lambda e, S, pp=pp, w=w: e.scalar_tensor_tensor(
                wsl[:, 4096 + pp * HB:4096 + pp * HB + w],
                wsl[:, 8192 + pp * HB:8192 + pp * HB + w], -16.0,
                wsl[:, pp * HB:pp * HB + w], ALU.mult, ALU.add))
            b.wait("v", "dfs", sm)
            b.wait("v", "pe", p2)
            b.drain("v")
            b.emit("v", lambda e, S, pp=pp, w=w: e.tensor_mul(
                gaux[:, LO_T + pp * HB:LO_T + pp * HB + w],
                wsl[:, 4096 + pp * HB:4096 + pp * HB + w],
                gaux[:, S_SL + pp * HB:S_SL + pp * HB + w]))
            b.emit("v", lambda e, S, pp=pp, w=w: e.tensor_mul(
                gaux[:, HI_T + pp * HB:HI_T + pp * HB + w],
                wsl[:, 8192 + pp * HB:8192 + pp * HB + w],
                gaux[:, S_SL + pp * HB:S_SL + pp * HB + w]), inc="dv")
            st_deq_dv[i] = b.cnt["dv"]

        def emit_hsum(ph, r, guard_dv):
            rr0, rr1 = PH_R[ph]
            nr = rr1 - rr0
            b.wait("t", "dv", guard_dv)
            for par in range(2):
                idx = 2 * (r - rr0) + par
                lr = r - rr0
                for th in range(2):
                    b.emit("t", lambda e, S, idx=idx, lr=lr, th=th, nr=nr:
                           e.matmul(pbk[4 + th][:, :],
                                    bandb[:, 172 - 4 * lr: 300 - 4 * lr],
                                    at_sb[:, idx * T + th * TH: idx * T + th * TH + TH],
                                    start=(idx == 0), stop=(idx == 2 * nr - 1)),
                           inc="pe" if (par == 1 and th == 1) else None)

        def emit_hsum_evac(ph):
            rr0, rr1 = PH_R[ph]
            nr = rr1 - rr0
            b.wait("v", "pe", b.cnt["pe"])
            b.emit("v", lambda e, S, nr=nr: e.tensor_scalar(
                gaux[0:4 * nr, HSUM:HSUM + 512], pbk[4][0:4 * nr, :], -1.0, None, ALU.mult))
            b.emit("v", lambda e, S, nr=nr: e.tensor_scalar(
                gaux[0:4 * nr, HSUM + 512:HSUM + 1024], pbk[5][0:4 * nr, :], -1.0, None, ALU.mult),
                inc="dv")
            ms["hsum_done"] = b.cnt["dv"]

        def grp_gu(i):
            _, ph, r, X, h = units[i]
            rr0, rr1 = PH_R[ph]
            pp = i % 2
            b.wait("t", "dv", st_deq_dv[i])
            if h == 0:
                sl, zm = zs_mark[("gu", r, X)]
                b.wait("t", "dzt", zm)
                if X == "g":
                    b.wait("t", "dv", ms["g_bank_dv"])
                else:
                    b.wait("t", "ac", ms["silu_ac"])
                for par in range(2):
                    for th in range(2):
                        bank = pbk[2 * par + th]
                        b.emit("t", lambda e, S, bank=bank, par=par, th=th, sl=sl:
                               e.matmul(bank[:, :],
                                        gaux[0:64, ZSL[sl] + par:ZSL[sl] + 256:2],
                                        gaux[0:64, XSUM + th * TH:XSUM + th * TH + TH],
                                        start=True, stop=False))
                        for cb in range(8):
                            for cpar in range(2):
                                src_t = HI_T if cpar == 0 else LO_T
                                cc = 2 * cb + cpar
                                lastu = (par == 1 and th == 1 and cb == 7 and cpar == 1)
                                b.emit("t", lambda e, S, bank=bank, src_t=src_t, cb=cb,
                                       pp=pp, cc=cc, th=th, par=par:
                                       e.matmul(bank[:, :],
                                                gaux[:, src_t + pp * HB + cb * 256 + par: src_t + pp * HB + (cb + 1) * 256: 2],
                                                x_sb[:, cc * T + th * TH: cc * T + th * TH + TH],
                                                start=False, stop=False),
                                       inc="pe" if lastu else None)
                zs_free[sl] = b.cnt["pe"]
                st_pe_free[i] = b.cnt["pe"]
            else:
                for par in range(2):
                    for th in range(2):
                        bank = pbk[2 * par + th]
                        for cb in range(8, 16):
                            for cpar in range(2):
                                src_t = HI_T if cpar == 0 else LO_T
                                cc = 2 * cb + cpar
                                last = (cb == 15 and cpar == 1)
                                lastu = (par == 1 and th == 1 and last)
                                b.emit("t", lambda e, S, bank=bank, src_t=src_t, cb=cb,
                                       pp=pp, cc=cc, th=th, last=last, par=par:
                                       e.matmul(bank[:, :],
                                                gaux[:, src_t + pp * HB + (cb - 8) * 256 + par: src_t + pp * HB + (cb - 7) * 256: 2],
                                                x_sb[:, cc * T + th * TH: cc * T + th * TH + TH],
                                                start=False, stop=last),
                                       inc="pe" if lastu else None)
                st_pe_free[i] = b.cnt["pe"]
                if X == "g":
                    b.wait("s", "pe", b.cnt["pe"])
                    b.wait("s", "dv", ms["g_bank_dv"])
                    for par in range(2):
                        for th in range(2):
                            b.emit("s", lambda e, S, par=par, th=th:
                                   e.activation(qkvt[:, par * 1024 + th * TH: par * 1024 + th * TH + TH],
                                                pbk[2 * par + th][:, :], AF.Silu), inc="ac")
                    ms["silu_ac"] = b.cnt["ac"]
                    if ms["pending_hsum"] is not None:
                        emit_hsum(*ms["pending_hsum"])
                        ms["pending_hsum"] = None
                else:
                    b.wait("v", "pe", b.cnt["pe"])
                    for par in range(2):
                        idx = 2 * (r - rr0) + par
                        for th in range(2):
                            b.emit("v", lambda e, S, par=par, th=th, idx=idx:
                                   e.tensor_mul(at_sb[:, idx * T + th * TH: idx * T + th * TH + TH],
                                                qkvt[:, par * 1024 + th * TH: par * 1024 + th * TH + TH],
                                                pbk[2 * par + th][:, :]),
                                   inc="dv" if (par == 1 and th == 1) else None)
                    ms["h_evac_dv"] = b.cnt["dv"]
                    ms["g_bank_dv"] = b.cnt["dv"]
                    if r == rr1 - 1:
                        emit_hsum(ph, r, ms["h_evac_dv"])
                        emit_hsum_evac(ph)
                    else:
                        ms["pending_hsum"] = (ph, r, ms["h_evac_dv"])

        def dn_group(ph, how, hsub, half, pp, sl=None):
            rr0, rr1 = PH_R[ph]
            nr = rr1 - rr0
            k0, nh = (0, 8) if half == 0 else (8, nr - 8)
            banks = (pbk[0], pbk[1]) if hsub == 0 else (pbk[2], pbk[3])
            for th in range(2):
                bank = banks[th]
                if half == 0:
                    b.emit("t", lambda e, S, bank=bank, hsub=hsub, th=th, nr=nr, sl=sl:
                           e.matmul(bank[:, :],
                                    gaux[0:4 * nr, ZSL[sl] + hsub * 128:ZSL[sl] + (hsub + 1) * 128],
                                    gaux[0:4 * nr, HSUM + th * TH:HSUM + th * TH + TH],
                                    start=True, stop=False))
                for k in range(k0, k0 + nh):
                    for par in range(2):
                        src_t = HI_T if par == 0 else LO_T
                        idx2 = 2 * k + par
                        last = (half == 1 and k == k0 + nh - 1 and par == 1)
                        lastu = (th == 1 and k == k0 + nh - 1 and par == 1)
                        b.emit("t", lambda e, S, bank=bank, src_t=src_t, k=k, k0=k0, hsub=hsub,
                               idx2=idx2, th=th, last=last, pp=pp:
                               e.matmul(bank[:, :],
                                        gaux[:, src_t + pp * HB + (k - k0) * 256 + hsub * 128: src_t + pp * HB + (k - k0) * 256 + hsub * 128 + 128],
                                        at_sb[:, idx2 * T + th * TH: idx2 * T + th * TH + TH],
                                        start=False, stop=last),
                               inc="pe" if lastu else None)

        def emit_dn_evac(ph, how, hsub, guard_pe):
            cc_o = 2 * how + hsub
            banks = (pbk[0], pbk[1]) if hsub == 0 else (pbk[2], pbk[3])
            b.wait("v", "pe", guard_pe)
            b.wait("v", "dsa", b.cnt["dsa"])
            if ph > 0:
                b.wait("v", "ddp", ddp_m[(ph, how, hsub)])
            for th in range(2):
                last = (ph != 2 and th == 1)
                if ph == 0:
                    b.emit("v", lambda e, S, th=th, banks=banks:
                           e.tensor_copy(dpo[:, th * TH: th * TH + TH],
                                         banks[th][:, :]), inc="dv" if last else None)
                else:
                    b.emit("v", lambda e, S, th=th, hsub=hsub, banks=banks:
                           e.tensor_add(dpo[:, th * TH: th * TH + TH],
                                        qkvt[:, hsub * 1024 + th * TH: hsub * 1024 + th * TH + TH],
                                        banks[th][:, :]), inc="dv" if last else None)
            if ph == 2:
                b.drain("v")
                for th in range(2):
                    b.emit("v", lambda e, S, th=th, cc_o=cc_o:
                           e.tensor_add(dpo[:, th * TH: th * TH + TH],
                                        dpo[:, th * TH: th * TH + TH],
                                        x_sb[:, cc_o * T + th * TH: cc_o * T + th * TH + TH]),
                           inc="dv" if th == 1 else None)
            ms["dn_evac_dv"][hsub] = b.cnt["dv"]
            ms["dpp_guard"][hsub] = b.cnt["dv"]

        def emit_dn_store(ph, how, hsub, guard_dv):
            cc_o = 2 * how + hsub
            b.wait("g", "dv", guard_dv)
            dst_d = a2d if ph == 2 else dpd
            b.dma("g", "dsa", lambda e, S, dst_d=dst_d, cc_o=cc_o: e.dma_start(
                out=dst_d[cc_o * 128:(cc_o + 1) * 128, :],
                in_=dpo[:, 0:1024]))

        def grp_dn(i):
            _, ph, _, how, h = units[i]
            pp = i % 2
            sl, zm = zs_mark[("dn", ph, how)]
            if h == 0:
                if ph > 0:
                    for hsub in range(2):
                        b.wait("y", "dv", ms["dpp_guard"][hsub])
                        b.wait("y", "dv", ms["h_evac_dv"])
                        ddp_m[(ph, how, hsub)] = b.dma(
                            "y", "ddp", lambda e, S, how=how, hsub=hsub: e.dma_start(
                                out=qkvt[:, hsub * 1024:(hsub + 1) * 1024],
                                in_=dpd[(2 * how + hsub) * 128:(2 * how + hsub + 1) * 128, :]))
                b.wait("t", "dv", st_deq_dv[i])
                b.wait("t", "dzt", zm)
                b.wait("t", "dv", ms["hsum_done"])
                b.wait("t", "dv", max(ms["dn_evac_dv"][0], ms["h_evac_dv"]))
                dn_group(ph, how, hsub=0, half=0, pp=pp, sl=sl)
                b.wait("t", "dv", st_deq_dv[i + 1])
                dn_group(ph, how, hsub=0, half=1, pp=(i + 1) % 2)
                emit_dn_evac(ph, how, hsub=0, guard_pe=b.cnt["pe"])
                ev_dv = b.cnt["dv"]
                b.wait("t", "dv", max(ms["dn_evac_dv"][1], ms["h_evac_dv"]))
                dn_group(ph, how, hsub=1, half=0, pp=pp, sl=sl)
                zs_free[sl] = b.cnt["pe"]
                st_pe_free[i] = b.cnt["pe"]

                def post():
                    emit_dn_store(ph, how, 0, ev_dv)
                return post
            else:
                b.wait("t", "dv", st_deq_dv[i])
                dn_group(ph, how, hsub=1, half=1, pp=pp)
                st_pe_free[i] = b.cnt["pe"]
                pe_11 = b.cnt["pe"]

                def post():
                    emit_dn_evac(ph, how, hsub=1, guard_pe=pe_11)
                    emit_dn_store(ph, how, 1, b.cnt["dv"])
                    if how == 15:
                        ms["g_bank_dv"] = b.cnt["dv"]
                return post

        deq_unit(0)
        deq_unit(1)
        for i in range(len(units)):
            if units[i][0] == "gu":
                post = grp_gu(i)
            else:
                post = grp_dn(i)
            if i + 2 < len(units):
                deq_unit(i + 2)
            if post is not None:
                post()

        a2_stores = b.cnt["dsa"]
        b.wait("y", "dsa", a2_stores)
        tm_guard = [0, 0]
        sq_fin_guard = 0
        for tc in range(8):
            sl = tc % 2
            b.wait("y", "dv", tm_guard[sl])
            tm_m = b.dma("y", "dtm", lambda e, S, tc=tc, sl=sl: e.dma_start_transpose(
                gaux[:, sl * 4096:(sl + 1) * 4096], a2d[:, tc * 128:(tc + 1) * 128]))
            ro = 8 * sl
            b.wait("v", "dtm", tm_m)
            b.emit("v", lambda e, S, sl=sl, ro=ro: e.tensor_reduce(
                stf[:, ro:ro + 1], gaux[:, sl * 4096:(sl + 1) * 4096], AXX, ALU.add))
            b.wait("s", "dtm", tm_m)
            b.wait("s", "dv", sq_fin_guard)
            b.emit("s", lambda e, S, sl=sl: e.activation(
                gaux[:, 8192:12288], gaux[:, sl * 4096:(sl + 1) * 4096], AF.Square), inc="ac")
            b.wait("v", "ac", b.cnt["ac"])
            b.emit("v", lambda e, S, ro=ro: e.tensor_reduce(
                stf[:, ro + 1:ro + 2], gaux[:, 8192:12288], AXX, ALU.add), inc="dv")
            sq_fin_guard = b.cnt["dv"]
            b.drain("v")
            b.emit("v", lambda e, S, ro=ro: e.tensor_scalar(
                stf[:, ro + 2:ro + 3], stf[:, ro:ro + 1], 1.0 / C, None, ALU.mult))
            b.drain("v")
            b.emit("v", lambda e, S, ro=ro: e.tensor_mul(stf[:, ro + 3:ro + 4],
                                                         stf[:, ro + 2:ro + 3], stf[:, ro + 2:ro + 3]))
            b.drain("v")
            b.emit("v", lambda e, S, ro=ro: e.scalar_tensor_tensor(
                stf[:, ro + 4:ro + 5], stf[:, ro + 1:ro + 2], 1.0 / C, stf[:, ro + 3:ro + 4],
                ALU.mult, ALU.subtract))
            b.drain("v")
            b.emit("v", lambda e, S, ro=ro: e.tensor_scalar(
                stf[:, ro + 4:ro + 5], stf[:, ro + 4:ro + 5], EPS, None, ALU.add), inc="dv")
            b.wait("s", "dv", b.cnt["dv"])
            b.emit("s", lambda e, S, ro=ro: e.activation(stf[:, ro + 5:ro + 6],
                                                         stf[:, ro + 4:ro + 5], AF.Sqrt), inc="ac")
            b.wait("v", "ac", b.cnt["ac"])
            b.emit("v", lambda e, S, ro=ro: e.reciprocal(stf[:, ro + 6:ro + 7],
                                                         stf[:, ro + 5:ro + 6]))
            b.drain("v")
            for qv in range(4):
                fsl = qv % 2
                b.wait("v", "dfo", max(0, b.cnt["dfo"] - 16))
                src_ap = gaux[:, sl * 4096 + qv * 1024: sl * 4096 + (qv + 1) * 1024]
                b.emit("v", lambda e, S, src_ap=src_ap, ro=ro, fsl=fsl:
                       e.tensor_scalar(
                           fout[:, fsl * 1024:(fsl + 1) * 1024].rearrange(
                               "t (jb p par) -> t jb par p", jb=4, p=128, par=2),
                           src_ap.rearrange("t (jb par p) -> t jb par p", jb=4, par=2, p=128),
                           stf[:, ro + 2:ro + 3], stf[:, ro + 6:ro + 7],
                           ALU.subtract, ALU.mult), inc="dv")
                vm = b.cnt["dv"]
                b.wait("y", "dv", vm)
                b.dma("y", "dfo", lambda e, S, tc=tc, qv=qv, fsl=fsl: e.dma_start(
                    out=out_t[tc * 128:(tc + 1) * 128, qv * 1024:(qv + 1) * 1024],
                    in_=fout[:, fsl * 1024:(fsl + 1) * 1024]))
            tm_guard[sl] = b.cnt["dv"]

        b.wait("g", "dfo", b.cnt["dfo"])
        b.wait("g", "fin", b.cnt["fin"])

        with nc.Block() as blk:
            @blk.tensor
            def _(t):
                for fn in b.q["t"]:
                    fn(t, sems)

            @blk.vector
            def _(v):
                for fn in b.q["v"]:
                    fn(v, sems)

            @blk.scalar
            def _(s):
                for fn in b.q["s"]:
                    fn(s, sems)

            @blk.gpsimd
            def _(g):
                for fn in b.q["g"]:
                    fn(g, sems)

            @blk.sync
            def _(y):
                for fn in b.q["y"]:
                    fn(y, sems)

    return nc


def kernel(**inputs):
    shared, per_core = host_prep(inputs)
    nc = build_nc()
    in_maps = [dict(shared, **pc) for pc in per_core]
    res = run_bass_kernel_spmd(nc, in_maps, core_ids=list(range(NCORE)))
    out = np.empty((B, S, C), np.float32)
    for c in range(NCORE):
        i, j = c // 2, c % 2
        out[i, j * T:(j + 1) * T, :] = res.results[c]["out"]
    return out


# version 09
# speedup vs baseline: 1.5330x; 1.1019x over previous
import numpy as np
from contextlib import ExitStack

import concourse.bass as bass
import concourse.mybir as mybir
from concourse.bass_utils import run_bass_kernel_spmd

F32 = mybir.dt.float32
BF16 = mybir.dt.bfloat16
U8 = mybir.dt.uint8
AF = mybir.ActivationFunctionType
ALU = mybir.AluOpType
AXX = mybir.AxisListType.X

NCORE = 8
B, S, C = 4, 2048, 4096
T = 1024
H, D, BS = 32, 128, 256
NB = T // BS
CCH = C // 128
TH = 512
FI = 11008
EPS = 1e-5


def _lattice_perm():
    l = np.arange(C)
    jb, rem = l // 256, l % 256
    par, p = rem // 128, rem % 128
    return jb * 256 + 2 * p + par


def _decay_consts():
    base = 1.0 / 2.0 ** (8.0 / H)
    expo = np.arange(H, dtype=np.float64) + 1.0
    factor = 1.0 - 1 / (31 + 1e-05) + 1e-05
    slope = base ** expo * factor
    r = np.arange(BS, dtype=np.float64) + 1.0
    qd = np.exp(-slope[:, None] * r[None, :])
    kd = np.exp(-slope[:, None] * (BS - r)[None, :])
    nm = r[:, None] - r[None, :]
    dg = np.exp(-slope[:, None, None] * nm[None])
    dg = np.where(nm[None] >= 0, dg, 0.0)
    bd = np.exp(-slope * BS)
    return (qd.astype(np.float32), kd.astype(np.float32),
            dg.astype(np.float32), bd.astype(np.float32))


def _u8_view(int32_arr):
    a = np.ascontiguousarray(int32_arr)
    return np.ascontiguousarray(a.view(np.uint8).reshape(a.shape + (4,))[..., 0])


def host_prep(inp):
    import ml_dtypes
    perm = _lattice_perm()
    shared = {}

    x = np.ascontiguousarray(inp["L0_attn_hidden_states"], dtype=np.float32)
    x_lat = np.ascontiguousarray(x[:, :, perm])
    per_core = []
    for c in range(NCORE):
        i, j = c // 2, c % 2
        d = {"xp": np.ascontiguousarray(x_lat[i, j * T:(j + 1) * T]),
             "mask": np.full((128, 1), float(j), np.float32)}
        per_core.append(d)

    qkv_w = inp["L0_attn_qkv_w"]
    wt = np.ascontiguousarray(qkv_w.T[perm, :])
    wt = wt.reshape(CCH, 128, H, 3, 128)
    shared["qkvw"] = np.ascontiguousarray(wt.transpose(2, 3, 1, 0, 4)).reshape(H, 3, 128, CCH * 128)

    gt = np.ascontiguousarray(inp["L0_attn_gate_w"].T[perm, :])
    gt = gt.reshape(CCH, 128, CCH, 128)
    shared["gatew"] = np.ascontiguousarray(gt.transpose(2, 1, 0, 3)).reshape(CCH, 128, CCH * 128)

    ot = np.ascontiguousarray(inp["L0_attn_out_w"].T[:, perm])
    ot = ot.reshape(CCH, 128, CCH, 128)
    shared["outw"] = np.ascontiguousarray(ot.transpose(2, 1, 0, 3)).reshape(CCH, 128, CCH * 128)

    shared["normw"] = np.ascontiguousarray(inp["L0_attn_norm_w"].reshape(CCH, 128).T)

    qd, kd, dg, bd = _decay_consts()
    shared["qd"] = np.ascontiguousarray(
        np.broadcast_to(qd[:, None, :], (H, 128, BS)).transpose(1, 0, 2).reshape(128, H * BS)
    ).astype(ml_dtypes.bfloat16)
    shared["kd"] = np.ascontiguousarray(kd.reshape(H, 2, 128).transpose(2, 0, 1).reshape(128, H * 2))
    dgt = dg.transpose(0, 2, 1).reshape(H, 2, 128, BS).transpose(2, 0, 1, 3)
    shared["dt"] = np.ascontiguousarray(dgt.reshape(128, H * 2 * BS)).astype(ml_dtypes.bfloat16)
    shared["ident"] = np.eye(128, dtype=ml_dtypes.bfloat16)
    shared["ones1"] = np.ones((128, 1), dtype=ml_dtypes.bfloat16)
    shared["onesr"] = np.ones((1, 128), dtype=np.float32)

    for nm, key in (("gq", "L0_ffn_gate_Wq"), ("uq", "L0_ffn_up_Wq")):
        wq = _u8_view(inp[key])
        wqt = np.ascontiguousarray(wq.T)
        r = wqt.reshape(16, 128, FI // 256, 256)
        shared[nm] = np.ascontiguousarray(r.transpose(2, 1, 0, 3))
    dq = _u8_view(inp["L0_ffn_down_Wq"])
    dqt = np.ascontiguousarray(dq.T[:, perm])
    r = dqt.reshape(43, 128, 16, 256)
    shared["dq"] = np.ascontiguousarray(r.transpose(2, 0, 1, 3))

    def s_re(sc, nchunks, ncols, colperm=None):
        a = np.ascontiguousarray(sc.T)
        if colperm is not None:
            a = np.ascontiguousarray(a[:, colperm])
        re = a.reshape(nchunks, 4, ncols).transpose(1, 0, 2)
        return np.ascontiguousarray(re).reshape(4, -1), a
    shared["gs_re"], shared["gsT"] = s_re(inp["L0_ffn_gate_scale"], 16, FI)
    _, shared["gzT"] = s_re(inp["L0_ffn_gate_zero"], 16, FI)
    shared["us_re"], shared["usT"] = s_re(inp["L0_ffn_up_scale"], 16, FI)
    _, shared["uzT"] = s_re(inp["L0_ffn_up_zero"], 16, FI)
    shared["ds_re"], shared["dsT"] = s_re(inp["L0_ffn_down_scale"], 43, C, perm)
    _, shared["dzT"] = s_re(inp["L0_ffn_down_zero"], 43, C, perm)

    band = np.zeros((128, 304), dtype=ml_dtypes.bfloat16)
    for p in range(128):
        band[p, 172 + p // 32] = 1
    shared["band"] = band
    pat4 = np.zeros((4, 128), dtype=ml_dtypes.bfloat16)
    for p in range(128):
        pat4[p // 32, p] = 1
    shared["pat4"] = pat4
    return shared, per_core



class Bld:
    ENG = ("t", "v", "s", "g", "y")

    def __init__(self):
        self.q = {e: [] for e in self.ENG}
        self.cnt = {}

    def sem(self, *names):
        for n in names:
            self.cnt[n] = 0

    def emit(self, eng, f, inc=None, k=1):
        if inc is not None:
            self.cnt[inc] += k

            def fn(e, S, f=f, inc=inc, k=k):
                f(e, S).then_inc(S[inc], k)
            self.q[eng].append(fn)
            return self.cnt[inc]
        self.q[eng].append(lambda e, S, f=f: f(e, S))
        return None

    def wait(self, eng, sem, val):
        if val > 0:
            self.q[eng].append(lambda e, S, sem=sem, val=val: e.wait_ge(S[sem], val))

    def drain(self, eng):
        self.q[eng].append(lambda e, S: e.drain())

    def dma(self, eng, sem, f):
        self.wait(eng, sem, self.cnt[sem])
        return self.emit(eng, f, inc=sem, k=16)


def build_nc(dbg=(), n_heads=H, do_p1=True, no_cc=False, skip=()):
    nc = bass.Bass("TRN2", target_bir_lowering=False)
    dbg = set(dbg)
    _, _, _, bd_all = _decay_consts()

    xp = nc.dram_tensor("xp", [T, C], F32, kind="ExternalInput")
    mask_t = nc.dram_tensor("mask", [128, 1], F32, kind="ExternalInput")
    qkvw = nc.dram_tensor("qkvw", [H, 3, 128, CCH * 128], F32, kind="ExternalInput")
    gatew = nc.dram_tensor("gatew", [CCH, 128, CCH * 128], F32, kind="ExternalInput")
    outw = nc.dram_tensor("outw", [CCH, 128, CCH * 128], F32, kind="ExternalInput")
    normw = nc.dram_tensor("normw", [128, CCH], F32, kind="ExternalInput")
    qd_t = nc.dram_tensor("qd", [128, H * BS], BF16, kind="ExternalInput")
    kd_t = nc.dram_tensor("kd", [128, H * 2], F32, kind="ExternalInput")
    dt_t = nc.dram_tensor("dt", [128, H * 2 * BS], BF16, kind="ExternalInput")
    ident_t = nc.dram_tensor("ident", [128, 128], BF16, kind="ExternalInput")
    ones1_t = nc.dram_tensor("ones1", [128, 1], BF16, kind="ExternalInput")
    onesr_t = nc.dram_tensor("onesr", [1, 128], F32, kind="ExternalInput")
    gq_t = nc.dram_tensor("gq", [43, 128, 16, 256], U8, kind="ExternalInput")
    uq_t = nc.dram_tensor("uq", [43, 128, 16, 256], U8, kind="ExternalInput")
    dq_t = nc.dram_tensor("dq", [16, 43, 128, 256], U8, kind="ExternalInput")
    sc_in = {}
    for nm, shp in (("gs_re", [4, 16 * FI]), ("us_re", [4, 16 * FI]), ("ds_re", [4, 43 * C]),
                    ("gsT", [64, FI]), ("gzT", [64, FI]), ("usT", [64, FI]), ("uzT", [64, FI]),
                    ("dsT", [172, C]), ("dzT", [172, C])):
        sc_in[nm] = nc.dram_tensor(nm, shp, F32, kind="ExternalInput")
    band_t = nc.dram_tensor("band", [128, 304], BF16, kind="ExternalInput")
    pat4_t = nc.dram_tensor("pat4", [4, 128], BF16, kind="ExternalInput")

    out_t = nc.dram_tensor("out", [T, C], F32, kind="ExternalOutput")

    xbf = nc.dram_tensor("xbf", [T, C], BF16, kind="Internal")
    q2d = nc.dram_tensor("q2d", [H, 128, T], BF16, kind="Internal")
    agin = nc.dram_tensor("agin", [128, H * 128], BF16, kind="Internal")
    agout = nc.dram_tensor("agout", [256, H * 128], BF16, kind="Internal")
    a2d = nc.dram_tensor("a2d", [C, T], BF16, kind="Internal")
    dpd = nc.dram_tensor("dpd", [C, T], BF16, kind="Internal")
    zs_g = nc.dram_tensor("zs_g", [64, FI], BF16, kind="Internal")
    zs_u = nc.dram_tensor("zs_u", [64, FI], BF16, kind="Internal")
    zs_d = nc.dram_tensor("zs_d", [172, C], BF16, kind="Internal")
    s1_g = nc.dram_tensor("s1_g", [4, 16 * FI], BF16, kind="Internal")
    s1_u = nc.dram_tensor("s1_u", [4, 16 * FI], BF16, kind="Internal")
    s1_d = nc.dram_tensor("s1_d", [4, 43 * C], BF16, kind="Internal")

    dbg_t = {}
    for name, shp, dt in (("attn", [C, T], F32), ("r1", [C, T], F32)):
        if name in dbg:
            dbg_t[name] = nc.dram_tensor("dbg_" + name, shp, dt, kind="ExternalOutput")

    b = Bld()
    b.sem("dc", "db", "dx", "ds0", "ds1", "ds2", "dd0", "dd1",
          "dw0", "dw1", "dfb", "dfs", "dfz", "dzt", "ddp", "dsa", "dtm", "dfo", "ds1c",
          "dag", "dcc", "dq2", "dq2s",
          "pe", "dv", "ac", "po", "fin")

    with ExitStack() as st:
        def sb(name, shape, dtype):
            return st.enter_context(nc.sbuf_tensor(name, shape, dtype))

        x_sb = sb("x_sb", [128, CCH * T], BF16)
        at_sb = sb("at_sb", [128, CCH * T], BF16)
        wsl = sb("wsl", [128, 3 * 4096], BF16)
        qkvt = sb("qkvt", [128, 3 * T], BF16)
        q2 = sb("q2", [128, T], BF16)
        ssb = sb("ssb", [128, 512], BF16)
        ktm = sb("ktm", [128, 256], BF16)
        vtm = sb("vtm", [128, 256], BF16)
        stf = sb("stf", [128, 2 * 128], F32)
        kvbf = sb("kvbf", [128, 128], BF16)
        gaux = sb("gaux", [128, 14848], BF16)
        kvin = gaux[:, 9216:9216 + H * 128]
        qdsb = gaux[:, 0:H * BS]
        kdsb = sb("kdsb", [128, H * 2], F32)
        dtsb = gaux[:, 8192:8192 + 1024]
        identb = sb("identb", [128, 128], BF16)
        ones1b = sb("ones1b", [128, 1], BF16)
        onesrb = sb("onesrb", [1, 128], F32)
        normwb = sb("normwb", [128, CCH], F32)
        masksb = sb("masksb", [128, 1], F32)
        fout = sb("fout", [128, 3072], F32)
        rows = fout[0:1, 0:2048]
        rwork = fout[0:1, 2048:3072]
        bandb = sb("bandb", [128, 304], BF16)
        pat4b = sb("pat4b", [4, 128], BF16)

        pbk = [st.enter_context(nc.psum_tensor(f"pb{i}", [128, 512], F32)) for i in range(7)]
        pbf = st.enter_context(nc.psum_tensor("pbf", [128, 1024], BF16))

        sems = {n: st.enter_context(nc.semaphore(n)) for n in b.cnt}

        b.emit("g", lambda e, S: e.dma_start(out=xbf[:, :], in_=xp[:, :]), inc="db", k=16)
        b.emit("g", lambda e, S: e.dma_start(out=masksb[:, :], in_=mask_t[:, :]), inc="dc", k=16)
        b.emit("g", lambda e, S: e.dma_start(out=qdsb[:, :], in_=qd_t[:, :]), inc="dc", k=16)
        b.emit("g", lambda e, S: e.dma_start(out=kdsb[:, :], in_=kd_t[:, :]), inc="dc", k=16)
        b.emit("g", lambda e, S: e.dma_start(out=identb[:, :], in_=ident_t[:, :]), inc="dc", k=16)
        b.emit("g", lambda e, S: e.dma_start(out=ones1b[:, :], in_=ones1_t[:, :]), inc="dc", k=16)
        b.emit("g", lambda e, S: e.dma_start(out=onesrb[:, :], in_=onesr_t[:, :]), inc="dc", k=16)
        b.emit("g", lambda e, S: e.dma_start(out=normwb[:, :], in_=normw[:, :]), inc="dc", k=16)
        b.emit("g", lambda e, S: e.dma_start(out=bandb[:, :], in_=band_t[:, :]), inc="dc", k=16)
        b.emit("g", lambda e, S: e.dma_start(out=pat4b[:, :], in_=pat4_t[:, :]), inc="dc", k=16)
        consts_done = b.cnt["dc"]

        b.wait("y", "db", 16)
        for cc in range(CCH):
            b.emit("y", lambda e, S, cc=cc: e.dma_start_transpose(
                x_sb[:, cc * T:(cc + 1) * T], xbf[:, cc * 128:(cc + 1) * 128]), inc="dx", k=16)
        x_loaded = b.cnt["dx"]

        NSLOT = 3
        slot_of = {}
        slab_ready = {}
        slab_guard = [0] * NSLOT
        next_slot = [0]

        def load_slab(h, z):
            sl = next_slot[0] % NSLOT
            next_slot[0] += 1
            slot_of[(h, z)] = sl
            b.wait("g", "pe", slab_guard[sl])
            sem = f"ds{sl}"
            m = b.dma("g", sem, lambda e, S, h=h, z=z, sl=sl: e.dma_start(
                out=wsl[:, sl * 4096:(sl + 1) * 4096], in_=qkvw[h, z, :, :]))
            slab_ready[(h, z)] = (sem, m)

        def qkv_matmuls(h, z, comp_idx, src):
            sl = slot_of[(h, z)]
            sem, m = slab_ready[(h, z)]
            b.wait("t", sem, m)
            b.wait("t", "dv", b.cnt["dv"])
            for th in range(2):
                bank = pbk[2 * comp_idx + th]
                for cc in range(CCH):
                    b.emit("t", lambda e, S, bank=bank, sl=sl, cc=cc, th=th, src=src:
                           e.matmul(bank[:, :],
                                    wsl[:, sl * 4096 + cc * 128: sl * 4096 + (cc + 1) * 128],
                                    src[:, cc * T + th * TH: cc * T + th * TH + TH],
                                    start=(cc == 0), stop=(cc == CCH - 1)),
                           inc="pe" if cc == CCH - 1 else None)
            pe_m = b.cnt["pe"]
            b.wait("s", "pe", pe_m)
            for th in range(2):
                bank = pbk[2 * comp_idx + th]
                b.emit("s", lambda e, S, bank=bank, comp_idx=comp_idx, th=th:
                       e.activation(qkvt[:, comp_idx * T + th * TH: comp_idx * T + th * TH + TH],
                                    bank[:, :], AF.Silu), inc="ac")
            slab_guard[sl] = pe_m
            return b.cnt["ac"]

        def scan_block(h, n, with_outputs):
            bd = float(bd_all[h])
            par = n % 2
            kcol = T + n * BS
            vcol = 2 * T + n * BS
            b.wait("t", "dv", b.cnt["dv"])
            for sub in range(2):
                b.emit("t", lambda e, S, kcol=kcol, sub=sub:
                       e.transpose(pbf[:, sub * 128:(sub + 1) * 128],
                                   qkvt[:, kcol + sub * 128: kcol + (sub + 1) * 128], identb[:, :]))
            for sub in range(2):
                b.emit("t", lambda e, S, vcol=vcol, sub=sub:
                       e.transpose(pbf[:, 256 + sub * 128: 256 + (sub + 1) * 128],
                                   qkvt[:, vcol + sub * 128: vcol + (sub + 1) * 128], identb[:, :]),
                       inc="pe" if sub == 1 else None)
            tr_done = b.cnt["pe"]
            if with_outputs:
                qcol = n * BS
                for sub in range(2):
                    b.emit("t", lambda e, S, kcol=kcol, qcol=qcol, sub=sub:
                           e.matmul(pbk[6][:, sub * 256:(sub + 1) * 256],
                                    qkvt[:, kcol + sub * 128: kcol + (sub + 1) * 128],
                                    qkvt[:, qcol: qcol + 256], start=True, stop=True),
                           inc="pe" if sub == 1 else None)
                s_done = b.cnt["pe"]
                b.wait("v", "pe", s_done)
                dslot = (h % 2) * 512
                for sub in range(2):
                    b.emit("v", lambda e, S, sub=sub, dslot=dslot:
                           e.tensor_mul(ssb[:, sub * 256:(sub + 1) * 256],
                                        pbk[6][:, sub * 256:(sub + 1) * 256],
                                        dtsb[:, dslot + sub * 256: dslot + (sub + 1) * 256]),
                           inc="dv" if sub == 1 else None)
            b.wait("v", "pe", tr_done)
            for sub in range(2):
                b.emit("v", lambda e, S, sub=sub, h=h:
                       e.tensor_scalar(ktm[:, sub * 128:(sub + 1) * 128],
                                       pbf[:, sub * 128:(sub + 1) * 128],
                                       kdsb[:, h * 2 + sub: h * 2 + sub + 1], None, ALU.mult))
            b.emit("v", lambda e, S: e.tensor_copy(vtm[:, :], pbf[:, 256:512]), inc="dv")
            ev_done = b.cnt["dv"]
            b.wait("t", "dv", ev_done)
            if with_outputs:
                qcol = n * BS
                for sub in range(2):
                    b.emit("t", lambda e, S, sub=sub, last=(n == 0 and sub == 1):
                           e.matmul(pbk[1][:, 0:256], vtm[:, sub * 128:(sub + 1) * 128],
                                    ssb[:, sub * 256:(sub + 1) * 256],
                                    start=(sub == 0), stop=last),
                           inc="pe" if (n == 0 and sub == 1) else None)
                if n > 0:
                    b.emit("t", lambda e, S, qcol=qcol:
                           e.matmul(pbk[1][:, 0:256], kvbf[:, :], q2[:, qcol:qcol + 256],
                                    start=False, stop=True), inc="pe")
                att_done = b.cnt["pe"]
                b.wait("v", "pe", att_done)
                b.emit("v", lambda e, S, h=h, n=n:
                       e.tensor_copy(at_sb[:, h * T + n * BS: h * T + (n + 1) * BS],
                                     pbk[1][:, 0:256]), inc="dv")
            for sub in range(2):
                b.emit("t", lambda e, S, sub=sub:
                       e.matmul(pbk[0][:, 0:128], ktm[:, sub * 128:(sub + 1) * 128],
                                vtm[:, sub * 128:(sub + 1) * 128],
                                start=(sub == 0), stop=(sub == 1)),
                       inc="pe" if sub == 1 else None)
            c_done = b.cnt["pe"]
            b.wait("v", "pe", c_done)
            b.drain("v")
            src_st = stf[:, par * 128:(par + 1) * 128]
            dst_st = stf[:, (1 - par) * 128:(2 - par) * 128]
            if n == 0:
                b.emit("v", lambda e, S, dst=dst_st: e.tensor_copy(dst, pbk[0][:, 0:128]))
            else:
                b.emit("v", lambda e, S, src=src_st, dst=dst_st, bd=bd:
                       e.scalar_tensor_tensor(dst, src, bd, pbk[0][:, 0:128], ALU.mult, ALU.add))
            b.drain("v")
            b.emit("v", lambda e, S, dst=dst_st: e.tensor_copy(kvbf[:, :], dst), inc="dv")

        b.emit("g", lambda e, S: e.dma_start(out=s1_g[:, :], in_=sc_in["gs_re"][:, :]), inc="ds1c", k=16)
        b.emit("g", lambda e, S: e.dma_start(out=s1_u[:, :], in_=sc_in["us_re"][:, :]), inc="ds1c", k=16)
        b.emit("g", lambda e, S: e.dma_start(out=s1_d[:, :], in_=sc_in["ds_re"][:, :]), inc="ds1c", k=16)
        s1_done = b.cnt["ds1c"]
        for sT_n, zT_n, dst in (("gsT", "gzT", zs_g), ("usT", "uzT", zs_u)):
            for c0, c1 in ((0, 5504), (5504, FI)):
                w = c1 - c0
                b.wait("g", "dfb", b.cnt["dfb"])
                b.dma("g", "dfz", lambda e, S, sT_n=sT_n, c0=c0, c1=c1, w=w: e.dma_start(
                    out=wsl[0:64, 0:w], in_=sc_in[sT_n][:, c0:c1]))
                lm = b.dma("g", "dfz", lambda e, S, zT_n=zT_n, c0=c0, c1=c1, w=w: e.dma_start(
                    out=wsl[0:64, 5504:5504 + w], in_=sc_in[zT_n][:, c0:c1]))
                b.wait("g", "dfz", lm)
                b.emit("g", lambda e, S, w=w: e.tensor_mul(
                    wsl[0:64, 5504:5504 + w], wsl[0:64, 5504:5504 + w], wsl[0:64, 0:w]),
                    inc="po")
                b.wait("g", "po", b.cnt["po"])
                b.dma("g", "dfb", lambda e, S, dst=dst, c0=c0, c1=c1, w=w: e.dma_start(
                    out=dst[:, c0:c1], in_=wsl[0:64, 5504:5504 + w]))
        for r0_, r1_ in ((0, 128), (128, 172)):
            nrw = r1_ - r0_
            b.wait("g", "dfb", b.cnt["dfb"])
            b.dma("g", "dfz", lambda e, S, r0_=r0_, nrw=nrw: e.dma_start(
                out=wsl[0:nrw, 0:C], in_=sc_in["dsT"][r0_:r0_ + nrw, :]))
            lm = b.dma("g", "dfz", lambda e, S, r0_=r0_, nrw=nrw: e.dma_start(
                out=wsl[0:nrw, 4096:4096 + C], in_=sc_in["dzT"][r0_:r0_ + nrw, :]))
            b.wait("g", "dfz", lm)
            b.emit("g", lambda e, S, nrw=nrw: e.tensor_mul(
                wsl[0:nrw, 8192:8192 + C], wsl[0:nrw, 4096:4096 + C],
                wsl[0:nrw, 0:C]), inc="po")
            b.wait("g", "po", b.cnt["po"])
            b.dma("g", "dfb", lambda e, S, r0_=r0_, nrw=nrw: e.dma_start(
                out=zs_d[r0_:r0_ + nrw, :], in_=wsl[0:nrw, 8192:8192 + C]))
        b.wait("g", "dfb", b.cnt["dfb"])

        b.wait("t", "dx", x_loaded)
        b.wait("t", "dc", consts_done)
        b.wait("v", "dc", consts_done)
        b.wait("v", "dx", x_loaded)
        q2s_guard = [0, 0]
        for h in range(n_heads if do_p1 else 0):
            bd = float(bd_all[h])
            load_slab(h, 0)
            load_slab(h, 1)
            load_slab(h, 2)
            dslot = (h % 2) * 512
            dsem = f"dd{h % 2}"
            b.wait("y", "dv", b.cnt["dv"])
            d_ready = b.dma("y", dsem, lambda e, S, h=h, dslot=dslot: e.dma_start(
                out=dtsb[:, dslot:dslot + 512], in_=dt_t[:, h * 512:(h + 1) * 512]))
            qkv_matmuls(h, 0, 0, x_sb)
            qkv_matmuls(h, 1, 1, x_sb)
            qkv_matmuls(h, 2, 2, x_sb)
            ac_m = b.cnt["ac"]
            b.wait("v", "ac", ac_m)
            b.wait("v", dsem, d_ready)
            for n in range(NB):
                b.emit("v", lambda e, S, h=h, n=n:
                       e.tensor_mul(q2[:, n * BS:(n + 1) * BS],
                                    qkvt[:, n * BS:(n + 1) * BS],
                                    qdsb[:, h * BS:(h + 1) * BS]),
                       inc="dv" if n == NB - 1 else None)
            fsl = h % 2
            if "q2s" not in skip:
                b.wait("v", "dq2s", q2s_guard[fsl])
                b.drain("v")
                for n in range(NB):
                    b.emit("v", lambda e, S, n=n, fsl=fsl, f=float(bd ** n):
                           e.tensor_scalar(fout[:, fsl * 1024 + n * BS: fsl * 1024 + (n + 1) * BS],
                                           q2[:, n * BS:(n + 1) * BS], f, None, ALU.mult),
                           inc="dv" if n == NB - 1 else None)
                b.wait("g", "dv", b.cnt["dv"])
                q2s_guard[fsl] = b.dma("g", "dq2s", lambda e, S, h=h, fsl=fsl: e.dma_start(
                    out=q2d[h], in_=fout[:, fsl * 1024:(fsl + 1) * 1024]))
            b.wait("t", "dv", b.cnt["dv"])
            for n in range(NB):
                scan_block(h, n, with_outputs=True)
            b.drain("v")
            b.emit("v", lambda e, S, h=h:
                   e.tensor_copy(kvin[:, h * 128:(h + 1) * 128], stf[:, 0:128]),
                   inc="dv")

        qkv_done_pe = b.cnt["pe"]

        if "ag" not in skip:
            b.wait("g", "dv", b.cnt["dv"])
            agm = b.dma("g", "dag", lambda e, S: e.dma_start(out=agin[:, :], in_=kvin[:, :]))
            b.wait("g", "dag", agm)
            if no_cc:
                b.emit("g", lambda e, S: e.dma_start(out=agout[0:128, :], in_=agin[:, :]),
                       inc="dcc", k=16)
                b.wait("g", "dcc", 16)
            else:
                b.emit("g", lambda e, S: e.collective_compute(
                    "AllGather", ALU.bypass,
                    replica_groups=[[0, 1], [2, 3], [4, 5], [6, 7]],
                    ins=[agin.ap().opt()], outs=[agout.ap().opt()]), inc="dcc", k=1)
                b.wait("g", "dcc", 1)
            agm2 = b.dma("g", "dag", lambda e, S: e.dma_start(out=kvin[:, :], in_=agout[0:128, :]))
            b.wait("v", "dag", agm2)
            b.drain("v")
            b.emit("v", lambda e, S: e.tensor_scalar(kvin[:, :], kvin[:, :],
                                                     masksb[:, 0:1], None, ALU.mult), inc="dv")
        mask_dv = b.cnt["dv"]

        b.wait("t", "ac", b.cnt["ac"])
        corr_guard = [b.cnt["pe"], b.cnt["pe"]]
        corr_bank_dv = [0, 0]
        for h in range(0 if "corr" in skip else (n_heads if do_p1 else 0)):
            sl = h % 2
            bks = (pbk[4], pbk[5]) if sl == 0 else (pbk[2], pbk[3])
            if "corrld" not in skip:
                b.wait("y", "pe", corr_guard[sl])
                qm = b.dma("y", "dq2", lambda e, S, h=h, sl=sl: e.dma_start(
                    out=qkvt[:, sl * 1024:(sl + 1) * 1024], in_=q2d[h]))
                b.wait("t", "dq2", qm)
            b.wait("t", "dv", mask_dv)
            if "corrmm" not in skip:
                b.wait("t", "dv", corr_bank_dv[sl])
                for th in range(2):
                    b.emit("t", lambda e, S, h=h, sl=sl, th=th, bks=bks:
                           e.matmul(bks[th][:, :], kvin[:, h * 128:(h + 1) * 128],
                                    qkvt[:, sl * 1024 + th * TH: sl * 1024 + th * TH + TH],
                                    start=True, stop=True),
                           inc="pe" if th == 1 else None)
                corr_guard[sl] = b.cnt["pe"]
            if "corradd" not in skip:
                b.wait("v", "pe", b.cnt["pe"])
                for th in range(2):
                    b.emit("v", lambda e, S, h=h, th=th, bks=bks:
                           e.tensor_add(at_sb[:, h * T + th * TH: h * T + th * TH + TH],
                                        at_sb[:, h * T + th * TH: h * T + th * TH + TH],
                                        bks[th][:, :]),
                           inc="dv" if th == 1 else None)
                corr_bank_dv[sl] = b.cnt["dv"]

        if "attn" in dbg:
            b.wait("g", "dv", b.cnt["dv"])
            for cc in range(CCH):
                b.emit("g", lambda e, S, cc=cc: e.dma_start(
                    out=dbg_t["attn"][cc * 128:(cc + 1) * 128, :],
                    in_=at_sb[:, cc * T:(cc + 1) * T]), inc="fin", k=16)

        SQ0, SQ1 = 8192, 9216
        GS0, GS1 = 8192, 9216
        RSTD_BC, MEAN_BC = 12288, 13312

        sq_guard = [0, 0]
        attn_done = b.cnt["dv"]
        for cc in range(CCH):
            sl = cc % 2
            b.wait("v", "pe", sq_guard[sl])
            b.drain("v")
            b.emit("v", lambda e, S, cc=cc, sl=sl:
                   e.tensor_mul(gaux[:, sl * 1024:(sl + 1) * 1024],
                                at_sb[:, cc * T:(cc + 1) * T], at_sb[:, cc * T:(cc + 1) * T]),
                   inc="dv")
            dv_m = b.cnt["dv"]
            b.wait("t", "dv", dv_m)
            for th in range(2):
                b.emit("t", lambda e, S, sl=sl, th=th, cc=cc:
                       e.matmul(pbk[4 + th][0:1, :], ones1b[:, :],
                                gaux[:, sl * 1024 + th * TH: sl * 1024 + th * TH + TH],
                                start=(cc == 0), stop=(cc == CCH - 1)),
                       inc="pe" if th == 1 else None)
            sq_guard[sl] = b.cnt["pe"]
        rms_mm = b.cnt["pe"]
        b.wait("v", "dq2s", b.cnt["dq2s"])
        b.wait("v", "pe", rms_mm)
        b.drain("v")
        b.emit("v", lambda e, S: e.tensor_copy(rows[0:1, 0:512], pbk[4][0:1, :]))
        b.emit("v", lambda e, S: e.tensor_copy(rows[0:1, 512:1024], pbk[5][0:1, :]))
        b.drain("v")
        b.emit("v", lambda e, S: e.tensor_scalar(rows[0:1, 0:1024], rows[0:1, 0:1024],
                                                 1.0 / C, EPS, ALU.mult, ALU.add), inc="dv")
        b.wait("s", "dv", b.cnt["dv"])
        b.emit("s", lambda e, S: e.activation(rows[0:1, 1024:2048], rows[0:1, 0:1024], AF.Sqrt),
               inc="ac")
        b.wait("v", "ac", b.cnt["ac"])
        b.emit("v", lambda e, S: e.reciprocal(rows[0:1, 0:1024], rows[0:1, 1024:2048]), inc="dv")
        b.wait("t", "dv", b.cnt["dv"])
        for th in range(2):
            b.emit("t", lambda e, S, th=th:
                   e.matmul(pbk[4 + th][:, :], onesrb[:, :], rows[0:1, th * TH:(th + 1) * TH],
                            start=True, stop=True), inc="pe" if th == 1 else None)
        b.wait("s", "pe", b.cnt["pe"])
        for th in range(2):
            b.emit("s", lambda e, S, th=th:
                   e.activation(gaux[:, RSTD_BC + th * TH: RSTD_BC + th * TH + TH],
                                pbk[4 + th][:, :], AF.Copy), inc="ac")
        rstd_bc_done = b.cnt["ac"]

        for dsn in ("ds0", "ds1", "ds2"):
            b.wait("g", dsn, b.cnt[dsn])
        wslot_guard = [qkv_done_pe, qkv_done_pe]
        gpsum_guard = [0, 0]
        gstage_guard = [0, 0]
        z_first = True
        for oc in range(CCH):
            sl = oc % 2
            wsem = f"dw{sl}"
            b.wait("g", "pe", wslot_guard[sl])
            wm = b.dma("g", wsem, lambda e, S, oc=oc, sl=sl: e.dma_start(
                out=wsl[:, sl * 4096:(sl + 1) * 4096], in_=gatew[oc, :, :]))
            b.wait("t", wsem, wm)
            b.wait("t", "ac", gpsum_guard[sl])
            for th in range(2):
                bank = pbk[2 * sl + th]
                for cc in range(CCH):
                    b.emit("t", lambda e, S, bank=bank, sl=sl, cc=cc, th=th:
                           e.matmul(bank[:, :],
                                    wsl[:, sl * 4096 + cc * 128: sl * 4096 + (cc + 1) * 128],
                                    x_sb[:, cc * T + th * TH: cc * T + th * TH + TH],
                                    start=(cc == 0), stop=(cc == CCH - 1)),
                           inc="pe" if cc == CCH - 1 else None)
            wslot_guard[sl] = b.cnt["pe"]
            b.wait("s", "pe", b.cnt["pe"])
            if z_first:
                b.wait("s", "ac", rstd_bc_done)
            b.wait("s", "dv", gstage_guard[sl])
            for th in range(2):
                b.emit("s", lambda e, S, sl=sl, th=th:
                       e.activation(wsl[:, (GS0 if sl == 0 else GS1) + th * TH:
                                        (GS0 if sl == 0 else GS1) + th * TH + TH],
                                    pbk[2 * sl + th][:, :], AF.Sigmoid), inc="ac")
            gpsum_guard[sl] = b.cnt["ac"]
            b.wait("v", "ac", b.cnt["ac"])
            if z_first:
                b.wait("v", "ac", rstd_bc_done)
                z_first = False
            b.drain("v")
            b.emit("v", lambda e, S, oc=oc:
                   e.tensor_mul(q2[:, :], at_sb[:, oc * T:(oc + 1) * T],
                                gaux[:, RSTD_BC:RSTD_BC + 1024]))
            b.drain("v")
            b.emit("v", lambda e, S, oc=oc, sl=sl:
                   e.scalar_tensor_tensor(at_sb[:, oc * T:(oc + 1) * T], q2[:, :],
                                          normwb[:, oc:oc + 1],
                                          wsl[:, (GS0 if sl == 0 else GS1):(GS0 if sl == 0 else GS1) + 1024],
                                          ALU.mult, ALU.mult), inc="dv")
            gstage_guard[sl] = b.cnt["dv"]

        a_guard = 0
        b.wait("t", "ac", b.cnt["ac"])
        b.wait("t", "dv", b.cnt["dv"])
        for oc in range(CCH):
            sl = oc % 2
            wsem = f"dw{sl}"
            b.wait("g", "pe", wslot_guard[sl])
            wm = b.dma("g", wsem, lambda e, S, oc=oc, sl=sl: e.dma_start(
                out=wsl[:, sl * 4096:(sl + 1) * 4096], in_=outw[oc, :, :]))
            b.wait("t", wsem, wm)
            b.wait("t", "dv", a_guard)
            for th in range(2):
                for cc in range(CCH):
                    b.emit("t", lambda e, S, sl=sl, cc=cc, th=th:
                           e.matmul(pbk[th][:, :],
                                    wsl[:, sl * 4096 + cc * 128: sl * 4096 + (cc + 1) * 128],
                                    at_sb[:, cc * T + th * TH: cc * T + th * TH + TH],
                                    start=(cc == 0), stop=(cc == CCH - 1)),
                           inc="pe" if cc == CCH - 1 else None)
            wslot_guard[sl] = b.cnt["pe"]
            b.wait("v", "pe", b.cnt["pe"])
            for th in range(2):
                b.emit("v", lambda e, S, oc=oc, th=th:
                       e.tensor_add(x_sb[:, oc * T + th * TH: oc * T + th * TH + TH],
                                    x_sb[:, oc * T + th * TH: oc * T + th * TH + TH],
                                    pbk[th][:, :]), inc="dv" if th == 1 else None)
            a_guard = b.cnt["dv"]
            b.wait("v", "pe", sq_guard[sl])
            b.drain("v")
            b.emit("v", lambda e, S, oc=oc, sl=sl:
                   e.tensor_mul(gaux[:, sl * 1024:(sl + 1) * 1024],
                                x_sb[:, oc * T:(oc + 1) * T], x_sb[:, oc * T:(oc + 1) * T]),
                   inc="dv")
            b.wait("t", "dv", b.cnt["dv"])
            for th in range(2):
                b.emit("t", lambda e, S, oc=oc, th=th:
                       e.matmul(pbk[4 + th][0:1, :], ones1b[:, :],
                                x_sb[:, oc * T + th * TH: oc * T + th * TH + TH],
                                start=(oc == 0), stop=(oc == CCH - 1)))
            for th in range(2):
                b.emit("t", lambda e, S, oc=oc, sl=sl, th=th:
                       e.matmul(pbk[2 + th][0:1, :], ones1b[:, :],
                                gaux[:, (0 if sl == 0 else 1024) + th * TH:
                                     (0 if sl == 0 else 1024) + th * TH + TH],
                                start=(oc == 0), stop=(oc == CCH - 1)),
                       inc="pe" if th == 1 else None)
            sq_guard[sl] = b.cnt["pe"]
        ln1_mm = b.cnt["pe"]
        b.wait("v", "pe", ln1_mm)
        b.emit("v", lambda e, S: e.tensor_copy(rows[0:1, 0:512], pbk[4][0:1, :]))
        b.emit("v", lambda e, S: e.tensor_copy(rows[0:1, 512:1024], pbk[5][0:1, :]))
        b.emit("v", lambda e, S: e.tensor_copy(rows[0:1, 1024:1536], pbk[2][0:1, :]))
        b.emit("v", lambda e, S: e.tensor_copy(rows[0:1, 1536:2048], pbk[3][0:1, :]))
        b.drain("v")
        b.emit("v", lambda e, S: e.tensor_scalar(rows[0:1, 0:1024], rows[0:1, 0:1024],
                                                 1.0 / C, None, ALU.mult))
        b.drain("v")
        b.emit("v", lambda e, S: e.tensor_mul(rwork, rows[0:1, 0:1024],
                                              rows[0:1, 0:1024]))
        b.drain("v")
        b.emit("v", lambda e, S: e.scalar_tensor_tensor(rows[0:1, 1024:2048], rows[0:1, 1024:2048],
                                                        1.0 / C, rwork,
                                                        ALU.mult, ALU.subtract))
        b.drain("v")
        b.emit("v", lambda e, S: e.tensor_scalar(rows[0:1, 1024:2048], rows[0:1, 1024:2048],
                                                 EPS, None, ALU.add), inc="dv")
        b.wait("s", "dv", b.cnt["dv"])
        b.emit("s", lambda e, S: e.activation(rwork, rows[0:1, 1024:2048], AF.Sqrt),
               inc="ac")
        b.wait("v", "ac", b.cnt["ac"])
        b.emit("v", lambda e, S: e.reciprocal(rows[0:1, 1024:2048], rwork), inc="dv")
        b.wait("t", "dv", b.cnt["dv"])
        for th in range(2):
            b.emit("t", lambda e, S, th=th:
                   e.matmul(pbk[4 + th][:, :], onesrb[:, :], rows[0:1, th * TH:(th + 1) * TH],
                            start=True, stop=True))
        for th in range(2):
            b.emit("t", lambda e, S, th=th:
                   e.matmul(pbk[2 + th][:, :], onesrb[:, :], rows[0:1, 1024 + th * TH:1024 + (th + 1) * TH],
                            start=True, stop=True), inc="pe" if th == 1 else None)
        b.wait("s", "pe", b.cnt["pe"])
        for th in range(2):
            b.emit("s", lambda e, S, th=th:
                   e.activation(gaux[:, MEAN_BC + th * TH: MEAN_BC + th * TH + TH],
                                pbk[4 + th][:, :], AF.Copy))
        for th in range(2):
            b.emit("s", lambda e, S, th=th:
                   e.activation(gaux[:, RSTD_BC + th * TH: RSTD_BC + th * TH + TH],
                                pbk[2 + th][:, :], AF.Copy), inc="ac")
        ln1_bc = b.cnt["ac"]
        b.wait("v", "ac", ln1_bc)
        for cc in range(CCH):
            b.drain("v")
            b.emit("v", lambda e, S, cc=cc:
                   e.tensor_sub(q2[:, :], x_sb[:, cc * T:(cc + 1) * T],
                                gaux[:, MEAN_BC:MEAN_BC + 1024]))
            b.drain("v")
            b.emit("v", lambda e, S, cc=cc:
                   e.tensor_mul(x_sb[:, cc * T:(cc + 1) * T], q2[:, :],
                                gaux[:, RSTD_BC:RSTD_BC + 1024]), inc="dv")
        r1_done = b.cnt["dv"]

        if "r1" in dbg:
            b.wait("g", "dv", r1_done)
            for cc in range(CCH):
                b.emit("g", lambda e, S, cc=cc: e.dma_start(
                    out=dbg_t["r1"][cc * 128:(cc + 1) * 128, :],
                    in_=x_sb[:, cc * T:(cc + 1) * T]), inc="fin", k=16)

        HI_T, LO_T, S_SL = 0, 4096, 8192
        XSUM, HSUM = 12288, 13312
        ZSD_T, ZSG_T = 14336, 14592
        dpo = qkvt[:, 2048:3072]

        b.wait("t", "dv", r1_done)
        for cc in range(CCH):
            jc = cc // 2
            for th in range(2):
                b.emit("t", lambda e, S, cc=cc, jc=jc, th=th:
                       e.matmul(pbk[th][:, :], bandb[:, 172 - 4 * jc: 300 - 4 * jc],
                                x_sb[:, cc * T + th * TH: cc * T + th * TH + TH],
                                start=(cc == 0), stop=(cc == CCH - 1)),
                       inc="pe" if (cc == CCH - 1 and th == 1) else None)
        pe_ffn0 = b.cnt["pe"]
        b.wait("v", "pe", pe_ffn0)
        b.emit("v", lambda e, S: e.tensor_scalar(gaux[0:64, XSUM:XSUM + 512],
                                                 pbk[0][0:64, :], -1.0, None, ALU.mult))
        b.emit("v", lambda e, S: e.tensor_scalar(gaux[0:64, XSUM + 512:XSUM + 1024],
                                                 pbk[1][0:64, :], -1.0, None, ALU.mult),
               inc="dv")
        xsum_done = b.cnt["dv"]

        b.wait("y", "ds1c", s1_done)
        b.wait("y", "dd0", b.cnt["dd0"])
        b.wait("y", "dd1", b.cnt["dd1"])
        b.wait("y", "dv", r1_done)
        b.wait("g", "dw0", b.cnt["dw0"])
        b.wait("g", "dw1", b.cnt["dw1"])
        PH_R = [(0, 15), (15, 29), (29, 43)]
        H135 = 135.53125
        HB = 2048
        ZSL = (ZSD_T, ZSG_T)

        units = []
        for ph, (rr0, rr1) in enumerate(PH_R):
            for r in range(rr0, rr1):
                for X in ("g", "u"):
                    units.append(("gu", ph, r, X, 0))
                    units.append(("gu", ph, r, X, 1))
            for how in range(16):
                units.append(("dn", ph, 0, how, 0))
                units.append(("dn", ph, 0, how, 1))

        st_deq_dv = {}
        st_pe_free = {}
        init_dv = r1_done
        init_pe = pe_ffn0
        zs_free = [init_pe, init_pe]
        zs_mark = {}
        tile_ctr = [0]
        ddp_m = {}
        ms = {
            "g_bank_dv": xsum_done,
            "silu_ac": 0,
            "h_evac_dv": 0,
            "pending_hsum": None,
            "hsum_done": 0,
            "dn_evac_dv": [0, 0],
            "dpp_guard": [0, 0],
        }

        def deq_unit(i):
            kind = units[i][0]
            pp = i % 2
            g2 = st_deq_dv.get(i - 2, init_dv)
            p2 = st_pe_free.get(i - 2, init_pe)
            if kind == "gu":
                _, ph, r, X, h = units[i]
                w = HB
                wq_src = gq_t if X == "g" else uq_t
                s1w = s1_g if X == "g" else s1_u
                b.wait("g", "dv", g2)
                bm = b.dma("g", "dfb", lambda e, S, wq_src=wq_src, r=r, h=h, pp=pp: e.dma_start(
                    out=wsl[:, pp * HB: pp * HB + HB],
                    in_=wq_src[r, :, 8 * h:8 * h + 8, :].rearrange("p cb i -> p (cb i)")))
                b.wait("y", "dv", g2)
                b.wait("y", "dfs", b.cnt["dfs"])
                for gs4 in range(4):
                    b.emit("y", lambda e, S, s1w=s1w, r=r, h=h, gs4=gs4, pp=pp: e.dma_start(
                        out=gaux[gs4 * 32:(gs4 + 1) * 32, S_SL + pp * HB:S_SL + pp * HB + HB],
                        in_=bass.AP(s1w, gs4 * 16 * FI + 8 * h * FI + 256 * r,
                                    [[0, 32], [FI, 8], [1, 256]])), inc="dfs", k=16)
                sm = b.cnt["dfs"]
                if h == 0:
                    zsd = zs_g if X == "g" else zs_u
                    sl = tile_ctr[0] % 2
                    tile_ctr[0] += 1
                    b.wait("y", "pe", zs_free[sl])
                    zs_mark[("gu", r, X)] = (sl, b.dma(
                        "y", "dzt", lambda e, S, zsd=zsd, r=r, sl=sl: e.dma_start(
                            out=gaux[0:64, ZSL[sl]:ZSL[sl] + 256],
                            in_=zsd[0:64, 256 * r:256 * r + 256])))
            else:
                _, ph, _, how, h = units[i]
                rr0, rr1 = PH_R[ph]
                nr = rr1 - rr0
                k0, nh = (0, 8) if h == 0 else (8, nr - 8)
                w = nh * 256
                b.wait("g", "dv", g2)
                bm = b.dma("g", "dfb", lambda e, S, how=how, rr0=rr0, k0=k0, nh=nh, pp=pp: e.dma_start(
                    out=wsl[:, pp * HB: pp * HB + nh * 256].rearrange("p (ib i) -> p ib i", ib=nh),
                    in_=dq_t[how, rr0 + k0:rr0 + k0 + nh].rearrange("ib p i -> p ib i")))
                b.wait("y", "dv", g2)
                b.wait("y", "dfs", b.cnt["dfs"])
                for gs4 in range(4):
                    b.emit("y", lambda e, S, how=how, rr0=rr0, k0=k0, nh=nh, gs4=gs4, pp=pp: e.dma_start(
                        out=gaux[gs4 * 32:(gs4 + 1) * 32, S_SL + pp * HB:S_SL + pp * HB + nh * 256],
                        in_=bass.AP(s1_d, gs4 * 43 * C + (rr0 + k0) * C + 256 * how,
                                    [[0, 32], [C, nh], [1, 256]])), inc="dfs", k=16)
                sm = b.cnt["dfs"]
                if h == 0:
                    sl = tile_ctr[0] % 2
                    tile_ctr[0] += 1
                    b.wait("y", "pe", zs_free[sl])
                    zs_mark[("dn", ph, how)] = (sl, b.dma(
                        "y", "dzt", lambda e, S, rr0=rr0, nr=nr, how=how, sl=sl: e.dma_start(
                            out=gaux[0:4 * nr, ZSL[sl]:ZSL[sl] + 256],
                            in_=zs_d[4 * rr0:4 * rr0 + 4 * nr, 256 * how:256 * how + 256])))
            b.wait("s", "dfb", bm)
            b.wait("s", "dv", g2)
            b.emit("s", lambda e, S, pp=pp, w=w: e.activation(
                wsl[:, 4096 + pp * HB:4096 + pp * HB + w], wsl[:, pp * HB:pp * HB + w],
                AF.Copy, bias=H135, scale=0.0625), inc="ac")
            b.wait("v", "ac", b.cnt["ac"])
            b.emit("v", lambda e, S, pp=pp, w=w: e.tensor_scalar(
                wsl[:, 8192 + pp * HB:8192 + pp * HB + w],
                wsl[:, 4096 + pp * HB:4096 + pp * HB + w], 136.0, None, ALU.subtract))
            b.drain("v")
            b.emit("v", lambda e, S, pp=pp, w=w: e.scalar_tensor_tensor(
                wsl[:, 4096 + pp * HB:4096 + pp * HB + w],
                wsl[:, 8192 + pp * HB:8192 + pp * HB + w], -16.0,
                wsl[:, pp * HB:pp * HB + w], ALU.mult, ALU.add))
            b.wait("v", "dfs", sm)
            b.wait("v", "pe", p2)
            b.drain("v")
            b.emit("v", lambda e, S, pp=pp, w=w: e.tensor_mul(
                gaux[:, LO_T + pp * HB:LO_T + pp * HB + w],
                wsl[:, 4096 + pp * HB:4096 + pp * HB + w],
                gaux[:, S_SL + pp * HB:S_SL + pp * HB + w]))
            b.emit("v", lambda e, S, pp=pp, w=w: e.tensor_mul(
                gaux[:, HI_T + pp * HB:HI_T + pp * HB + w],
                wsl[:, 8192 + pp * HB:8192 + pp * HB + w],
                gaux[:, S_SL + pp * HB:S_SL + pp * HB + w]), inc="dv")
            st_deq_dv[i] = b.cnt["dv"]

        def emit_hsum(ph, r, guard_dv):
            rr0, rr1 = PH_R[ph]
            nr = rr1 - rr0
            b.wait("t", "dv", guard_dv)
            for par in range(2):
                idx = 2 * (r - rr0) + par
                lr = r - rr0
                for th in range(2):
                    b.emit("t", lambda e, S, idx=idx, lr=lr, th=th, nr=nr:
                           e.matmul(pbk[4 + th][:, :],
                                    bandb[:, 172 - 4 * lr: 300 - 4 * lr],
                                    at_sb[:, idx * T + th * TH: idx * T + th * TH + TH],
                                    start=(idx == 0), stop=(idx == 2 * nr - 1)),
                           inc="pe" if (par == 1 and th == 1) else None)

        def emit_hsum_evac(ph):
            rr0, rr1 = PH_R[ph]
            nr = rr1 - rr0
            b.wait("v", "pe", b.cnt["pe"])
            b.emit("v", lambda e, S, nr=nr: e.tensor_scalar(
                gaux[0:4 * nr, HSUM:HSUM + 512], pbk[4][0:4 * nr, :], -1.0, None, ALU.mult))
            b.emit("v", lambda e, S, nr=nr: e.tensor_scalar(
                gaux[0:4 * nr, HSUM + 512:HSUM + 1024], pbk[5][0:4 * nr, :], -1.0, None, ALU.mult),
                inc="dv")
            ms["hsum_done"] = b.cnt["dv"]

        def grp_gu(i):
            _, ph, r, X, h = units[i]
            rr0, rr1 = PH_R[ph]
            pp = i % 2
            b.wait("t", "dv", st_deq_dv[i])
            if h == 0:
                sl, zm = zs_mark[("gu", r, X)]
                b.wait("t", "dzt", zm)
                if X == "g":
                    b.wait("t", "dv", ms["g_bank_dv"])
                else:
                    b.wait("t", "ac", ms["silu_ac"])
                for par in range(2):
                    for th in range(2):
                        bank = pbk[2 * par + th]
                        b.emit("t", lambda e, S, bank=bank, par=par, th=th, sl=sl:
                               e.matmul(bank[:, :],
                                        gaux[0:64, ZSL[sl] + par:ZSL[sl] + 256:2],
                                        gaux[0:64, XSUM + th * TH:XSUM + th * TH + TH],
                                        start=True, stop=False))
                        for cb in range(8):
                            for cpar in range(2):
                                src_t = HI_T if cpar == 0 else LO_T
                                cc = 2 * cb + cpar
                                lastu = (par == 1 and th == 1 and cb == 7 and cpar == 1)
                                b.emit("t", lambda e, S, bank=bank, src_t=src_t, cb=cb,
                                       pp=pp, cc=cc, th=th, par=par:
                                       e.matmul(bank[:, :],
                                                gaux[:, src_t + pp * HB + cb * 256 + par: src_t + pp * HB + (cb + 1) * 256: 2],
                                                x_sb[:, cc * T + th * TH: cc * T + th * TH + TH],
                                                start=False, stop=False),
                                       inc="pe" if lastu else None)
                zs_free[sl] = b.cnt["pe"]
                st_pe_free[i] = b.cnt["pe"]
            else:
                for par in range(2):
                    for th in range(2):
                        bank = pbk[2 * par + th]
                        for cb in range(8, 16):
                            for cpar in range(2):
                                src_t = HI_T if cpar == 0 else LO_T
                                cc = 2 * cb + cpar
                                last = (cb == 15 and cpar == 1)
                                lastu = (par == 1 and th == 1 and last)
                                b.emit("t", lambda e, S, bank=bank, src_t=src_t, cb=cb,
                                       pp=pp, cc=cc, th=th, last=last, par=par:
                                       e.matmul(bank[:, :],
                                                gaux[:, src_t + pp * HB + (cb - 8) * 256 + par: src_t + pp * HB + (cb - 7) * 256: 2],
                                                x_sb[:, cc * T + th * TH: cc * T + th * TH + TH],
                                                start=False, stop=last),
                                       inc="pe" if lastu else None)
                st_pe_free[i] = b.cnt["pe"]
                if X == "g":
                    b.wait("s", "pe", b.cnt["pe"])
                    b.wait("s", "dv", ms["g_bank_dv"])
                    for par in range(2):
                        for th in range(2):
                            b.emit("s", lambda e, S, par=par, th=th:
                                   e.activation(qkvt[:, par * 1024 + th * TH: par * 1024 + th * TH + TH],
                                                pbk[2 * par + th][:, :], AF.Silu), inc="ac")
                    ms["silu_ac"] = b.cnt["ac"]
                    if ms["pending_hsum"] is not None:
                        emit_hsum(*ms["pending_hsum"])
                        ms["pending_hsum"] = None
                else:
                    b.wait("v", "pe", b.cnt["pe"])
                    for par in range(2):
                        idx = 2 * (r - rr0) + par
                        for th in range(2):
                            b.emit("v", lambda e, S, par=par, th=th, idx=idx:
                                   e.tensor_mul(at_sb[:, idx * T + th * TH: idx * T + th * TH + TH],
                                                qkvt[:, par * 1024 + th * TH: par * 1024 + th * TH + TH],
                                                pbk[2 * par + th][:, :]),
                                   inc="dv" if (par == 1 and th == 1) else None)
                    ms["h_evac_dv"] = b.cnt["dv"]
                    ms["g_bank_dv"] = b.cnt["dv"]
                    if r == rr1 - 1:
                        emit_hsum(ph, r, ms["h_evac_dv"])
                        emit_hsum_evac(ph)
                    else:
                        ms["pending_hsum"] = (ph, r, ms["h_evac_dv"])

        def dn_group(ph, how, hsub, half, pp, sl=None):
            rr0, rr1 = PH_R[ph]
            nr = rr1 - rr0
            k0, nh = (0, 8) if half == 0 else (8, nr - 8)
            banks = (pbk[0], pbk[1]) if hsub == 0 else (pbk[2], pbk[3])
            for th in range(2):
                bank = banks[th]
                if half == 0:
                    b.emit("t", lambda e, S, bank=bank, hsub=hsub, th=th, nr=nr, sl=sl:
                           e.matmul(bank[:, :],
                                    gaux[0:4 * nr, ZSL[sl] + hsub * 128:ZSL[sl] + (hsub + 1) * 128],
                                    gaux[0:4 * nr, HSUM + th * TH:HSUM + th * TH + TH],
                                    start=True, stop=False))
                for k in range(k0, k0 + nh):
                    for par in range(2):
                        src_t = HI_T if par == 0 else LO_T
                        idx2 = 2 * k + par
                        last = (half == 1 and k == k0 + nh - 1 and par == 1)
                        lastu = (th == 1 and k == k0 + nh - 1 and par == 1)
                        b.emit("t", lambda e, S, bank=bank, src_t=src_t, k=k, k0=k0, hsub=hsub,
                               idx2=idx2, th=th, last=last, pp=pp:
                               e.matmul(bank[:, :],
                                        gaux[:, src_t + pp * HB + (k - k0) * 256 + hsub * 128: src_t + pp * HB + (k - k0) * 256 + hsub * 128 + 128],
                                        at_sb[:, idx2 * T + th * TH: idx2 * T + th * TH + TH],
                                        start=False, stop=last),
                               inc="pe" if lastu else None)

        def emit_dn_evac(ph, how, hsub, guard_pe):
            cc_o = 2 * how + hsub
            banks = (pbk[0], pbk[1]) if hsub == 0 else (pbk[2], pbk[3])
            b.wait("v", "pe", guard_pe)
            b.wait("v", "dsa", b.cnt["dsa"])
            if ph > 0:
                b.wait("v", "ddp", ddp_m[(ph, how, hsub)])
            for th in range(2):
                last = (ph != 2 and th == 1)
                if ph == 0:
                    b.emit("v", lambda e, S, th=th, banks=banks:
                           e.tensor_copy(dpo[:, th * TH: th * TH + TH],
                                         banks[th][:, :]), inc="dv" if last else None)
                else:
                    b.emit("v", lambda e, S, th=th, hsub=hsub, banks=banks:
                           e.tensor_add(dpo[:, th * TH: th * TH + TH],
                                        qkvt[:, hsub * 1024 + th * TH: hsub * 1024 + th * TH + TH],
                                        banks[th][:, :]), inc="dv" if last else None)
            if ph == 2:
                b.drain("v")
                for th in range(2):
                    b.emit("v", lambda e, S, th=th, cc_o=cc_o:
                           e.tensor_add(dpo[:, th * TH: th * TH + TH],
                                        dpo[:, th * TH: th * TH + TH],
                                        x_sb[:, cc_o * T + th * TH: cc_o * T + th * TH + TH]),
                           inc="dv" if th == 1 else None)
            ms["dn_evac_dv"][hsub] = b.cnt["dv"]
            ms["dpp_guard"][hsub] = b.cnt["dv"]

        def emit_dn_store(ph, how, hsub, guard_dv):
            cc_o = 2 * how + hsub
            b.wait("g", "dv", guard_dv)
            dst_d = a2d if ph == 2 else dpd
            b.dma("g", "dsa", lambda e, S, dst_d=dst_d, cc_o=cc_o: e.dma_start(
                out=dst_d[cc_o * 128:(cc_o + 1) * 128, :],
                in_=dpo[:, 0:1024]))

        def grp_dn(i):
            _, ph, _, how, h = units[i]
            pp = i % 2
            sl, zm = zs_mark[("dn", ph, how)]
            if h == 0:
                if ph > 0:
                    for hsub in range(2):
                        b.wait("y", "dv", ms["dpp_guard"][hsub])
                        b.wait("y", "dv", ms["h_evac_dv"])
                        ddp_m[(ph, how, hsub)] = b.dma(
                            "y", "ddp", lambda e, S, how=how, hsub=hsub: e.dma_start(
                                out=qkvt[:, hsub * 1024:(hsub + 1) * 1024],
                                in_=dpd[(2 * how + hsub) * 128:(2 * how + hsub + 1) * 128, :]))
                b.wait("t", "dv", st_deq_dv[i])
                b.wait("t", "dzt", zm)
                b.wait("t", "dv", ms["hsum_done"])
                b.wait("t", "dv", max(ms["dn_evac_dv"][0], ms["h_evac_dv"]))
                dn_group(ph, how, hsub=0, half=0, pp=pp, sl=sl)
                b.wait("t", "dv", st_deq_dv[i + 1])
                dn_group(ph, how, hsub=0, half=1, pp=(i + 1) % 2)
                emit_dn_evac(ph, how, hsub=0, guard_pe=b.cnt["pe"])
                ev_dv = b.cnt["dv"]
                b.wait("t", "dv", max(ms["dn_evac_dv"][1], ms["h_evac_dv"]))
                dn_group(ph, how, hsub=1, half=0, pp=pp, sl=sl)
                zs_free[sl] = b.cnt["pe"]
                st_pe_free[i] = b.cnt["pe"]

                def post():
                    emit_dn_store(ph, how, 0, ev_dv)
                return post
            else:
                b.wait("t", "dv", st_deq_dv[i])
                dn_group(ph, how, hsub=1, half=1, pp=pp)
                st_pe_free[i] = b.cnt["pe"]
                pe_11 = b.cnt["pe"]

                def post():
                    emit_dn_evac(ph, how, hsub=1, guard_pe=pe_11)
                    emit_dn_store(ph, how, 1, b.cnt["dv"])
                    if how == 15:
                        ms["g_bank_dv"] = b.cnt["dv"]
                return post

        deq_unit(0)
        deq_unit(1)
        for i in range(len(units)):
            if units[i][0] == "gu":
                post = grp_gu(i)
            else:
                post = grp_dn(i)
            if i + 2 < len(units):
                deq_unit(i + 2)
            if post is not None:
                post()

        a2_stores = b.cnt["dsa"]
        b.wait("y", "dsa", a2_stores)
        tm_guard = [0, 0]
        sq_fin_guard = 0
        for tc in range(8):
            sl = tc % 2
            b.wait("y", "dv", tm_guard[sl])
            tm_m = b.dma("y", "dtm", lambda e, S, tc=tc, sl=sl: e.dma_start_transpose(
                gaux[:, sl * 4096:(sl + 1) * 4096], a2d[:, tc * 128:(tc + 1) * 128]))
            ro = 8 * sl
            b.wait("v", "dtm", tm_m)
            b.emit("v", lambda e, S, sl=sl, ro=ro: e.tensor_reduce(
                stf[:, ro:ro + 1], gaux[:, sl * 4096:(sl + 1) * 4096], AXX, ALU.add))
            b.wait("s", "dtm", tm_m)
            b.wait("s", "dv", sq_fin_guard)
            b.emit("s", lambda e, S, sl=sl: e.activation(
                gaux[:, 8192:12288], gaux[:, sl * 4096:(sl + 1) * 4096], AF.Square), inc="ac")
            b.wait("v", "ac", b.cnt["ac"])
            b.emit("v", lambda e, S, ro=ro: e.tensor_reduce(
                stf[:, ro + 1:ro + 2], gaux[:, 8192:12288], AXX, ALU.add), inc="dv")
            sq_fin_guard = b.cnt["dv"]
            b.drain("v")
            b.emit("v", lambda e, S, ro=ro: e.tensor_scalar(
                stf[:, ro + 2:ro + 3], stf[:, ro:ro + 1], 1.0 / C, None, ALU.mult))
            b.drain("v")
            b.emit("v", lambda e, S, ro=ro: e.tensor_mul(stf[:, ro + 3:ro + 4],
                                                         stf[:, ro + 2:ro + 3], stf[:, ro + 2:ro + 3]))
            b.drain("v")
            b.emit("v", lambda e, S, ro=ro: e.scalar_tensor_tensor(
                stf[:, ro + 4:ro + 5], stf[:, ro + 1:ro + 2], 1.0 / C, stf[:, ro + 3:ro + 4],
                ALU.mult, ALU.subtract))
            b.drain("v")
            b.emit("v", lambda e, S, ro=ro: e.tensor_scalar(
                stf[:, ro + 4:ro + 5], stf[:, ro + 4:ro + 5], EPS, None, ALU.add), inc="dv")
            b.wait("s", "dv", b.cnt["dv"])
            b.emit("s", lambda e, S, ro=ro: e.activation(stf[:, ro + 5:ro + 6],
                                                         stf[:, ro + 4:ro + 5], AF.Sqrt), inc="ac")
            b.wait("v", "ac", b.cnt["ac"])
            b.emit("v", lambda e, S, ro=ro: e.reciprocal(stf[:, ro + 6:ro + 7],
                                                         stf[:, ro + 5:ro + 6]))
            b.drain("v")
            for qv in range(4):
                fsl = qv % 2
                b.wait("v", "dfo", max(0, b.cnt["dfo"] - 16))
                src_ap = gaux[:, sl * 4096 + qv * 1024: sl * 4096 + (qv + 1) * 1024]
                b.emit("v", lambda e, S, src_ap=src_ap, ro=ro, fsl=fsl:
                       e.tensor_scalar(
                           fout[:, fsl * 1024:(fsl + 1) * 1024].rearrange(
                               "t (jb p par) -> t jb par p", jb=4, p=128, par=2),
                           src_ap.rearrange("t (jb par p) -> t jb par p", jb=4, par=2, p=128),
                           stf[:, ro + 2:ro + 3], stf[:, ro + 6:ro + 7],
                           ALU.subtract, ALU.mult), inc="dv")
                vm = b.cnt["dv"]
                b.wait("y", "dv", vm)
                b.dma("y", "dfo", lambda e, S, tc=tc, qv=qv, fsl=fsl: e.dma_start(
                    out=out_t[tc * 128:(tc + 1) * 128, qv * 1024:(qv + 1) * 1024],
                    in_=fout[:, fsl * 1024:(fsl + 1) * 1024]))
            tm_guard[sl] = b.cnt["dv"]

        b.wait("g", "dfo", b.cnt["dfo"])
        b.wait("g", "fin", b.cnt["fin"])

        with nc.Block() as blk:
            @blk.tensor
            def _(t):
                for fn in b.q["t"]:
                    fn(t, sems)

            @blk.vector
            def _(v):
                for fn in b.q["v"]:
                    fn(v, sems)

            @blk.scalar
            def _(s):
                for fn in b.q["s"]:
                    fn(s, sems)

            @blk.gpsimd
            def _(g):
                for fn in b.q["g"]:
                    fn(g, sems)

            @blk.sync
            def _(y):
                for fn in b.q["y"]:
                    fn(y, sems)

    return nc


def kernel(**inputs):
    shared, per_core = host_prep(inputs)
    nc = build_nc()
    in_maps = [dict(shared, **pc) for pc in per_core]
    res = run_bass_kernel_spmd(nc, in_maps, core_ids=list(range(NCORE)))
    out = np.empty((B, S, C), np.float32)
    for c in range(NCORE):
        i, j = c // 2, c % 2
        out[i, j * T:(j + 1) * T, :] = res.results[c]["out"]
    return out


# version 11
# speedup vs baseline: 1.6318x; 1.0645x over previous
import numpy as np
from contextlib import ExitStack

import concourse.bass as bass
import concourse.mybir as mybir
from concourse.bass_utils import run_bass_kernel_spmd

F32 = mybir.dt.float32
BF16 = mybir.dt.bfloat16
U8 = mybir.dt.uint8
AF = mybir.ActivationFunctionType
ALU = mybir.AluOpType
AXX = mybir.AxisListType.X

NCORE = 8
B, S, C = 4, 2048, 4096
T = 1024
H, D, BS = 32, 128, 256
NB = T // BS
CCH = C // 128
TH = 512
FI = 11008
EPS = 1e-5


def _lattice_perm():
    l = np.arange(C)
    jb, rem = l // 256, l % 256
    par, p = rem // 128, rem % 128
    return jb * 256 + 2 * p + par


def _decay_consts():
    base = 1.0 / 2.0 ** (8.0 / H)
    expo = np.arange(H, dtype=np.float64) + 1.0
    factor = 1.0 - 1 / (31 + 1e-05) + 1e-05
    slope = base ** expo * factor
    r = np.arange(BS, dtype=np.float64) + 1.0
    qd = np.exp(-slope[:, None] * r[None, :])
    kd = np.exp(-slope[:, None] * (BS - r)[None, :])
    nm = r[:, None] - r[None, :]
    dg = np.exp(-slope[:, None, None] * nm[None])
    dg = np.where(nm[None] >= 0, dg, 0.0)
    bd = np.exp(-slope * BS)
    return (qd.astype(np.float32), kd.astype(np.float32),
            dg.astype(np.float32), bd.astype(np.float32))


def _u8_view(int32_arr):
    a = np.ascontiguousarray(int32_arr)
    return np.ascontiguousarray(a.view(np.uint8).reshape(a.shape + (4,))[..., 0])


def host_prep(inp):
    import ml_dtypes
    perm = _lattice_perm()
    shared = {}

    x = np.ascontiguousarray(inp["L0_attn_hidden_states"], dtype=np.float32)
    x_lat = np.ascontiguousarray(x[:, :, perm])
    per_core = []
    for c in range(NCORE):
        i, j = c // 2, c % 2
        d = {"xp": np.ascontiguousarray(x_lat[i, j * T:(j + 1) * T]),
             "mask": np.full((128, 1), float(j), np.float32)}
        per_core.append(d)

    qkv_w = inp["L0_attn_qkv_w"]
    wt = np.ascontiguousarray(qkv_w.T[perm, :])
    wt = wt.reshape(CCH, 128, H, 3, 128)
    shared["qkvw"] = np.ascontiguousarray(wt.transpose(2, 3, 1, 0, 4)).reshape(H, 3, 128, CCH * 128)

    gt = np.ascontiguousarray(inp["L0_attn_gate_w"].T[perm, :])
    gt = gt.reshape(CCH, 128, CCH, 128)
    shared["gatew"] = np.ascontiguousarray(gt.transpose(2, 1, 0, 3)).reshape(CCH, 128, CCH * 128)

    ot = np.ascontiguousarray(inp["L0_attn_out_w"].T[:, perm])
    ot = ot.reshape(CCH, 128, CCH, 128)
    shared["outw"] = np.ascontiguousarray(ot.transpose(2, 1, 0, 3)).reshape(CCH, 128, CCH * 128)

    shared["normw"] = np.ascontiguousarray(inp["L0_attn_norm_w"].reshape(CCH, 128).T)

    qd, kd, dg, bd = _decay_consts()
    shared["qd"] = np.ascontiguousarray(
        np.broadcast_to(qd[:, None, :], (H, 128, BS)).transpose(1, 0, 2).reshape(128, H * BS)
    ).astype(ml_dtypes.bfloat16)
    shared["kd"] = np.ascontiguousarray(kd.reshape(H, 2, 128).transpose(2, 0, 1).reshape(128, H * 2))
    dgt = dg.transpose(0, 2, 1).reshape(H, 2, 128, BS).transpose(2, 0, 1, 3)
    shared["dt"] = np.ascontiguousarray(dgt.reshape(128, H * 2 * BS)).astype(ml_dtypes.bfloat16)
    shared["ident"] = np.eye(128, dtype=ml_dtypes.bfloat16)
    shared["ones1"] = np.ones((128, 1), dtype=ml_dtypes.bfloat16)
    shared["onesr"] = np.ones((1, 128), dtype=np.float32)

    for nm, key in (("gq", "L0_ffn_gate_Wq"), ("uq", "L0_ffn_up_Wq")):
        wq = _u8_view(inp[key])
        wqt = np.ascontiguousarray(wq.T)
        r = wqt.reshape(16, 128, FI // 256, 256)
        shared[nm] = np.ascontiguousarray(r.transpose(2, 1, 0, 3))
    dq = _u8_view(inp["L0_ffn_down_Wq"])
    dqt = np.ascontiguousarray(dq.T[:, perm])
    r = dqt.reshape(43, 128, 16, 256)
    shared["dq"] = np.ascontiguousarray(r.transpose(2, 0, 1, 3))

    def s_re(sc, nchunks, ncols, colperm=None):
        a = np.ascontiguousarray(sc.T)
        if colperm is not None:
            a = np.ascontiguousarray(a[:, colperm])
        re = a.reshape(nchunks, 4, ncols).transpose(1, 0, 2)
        return np.ascontiguousarray(re).reshape(4, -1), a
    shared["gs_re"], shared["gsT"] = s_re(inp["L0_ffn_gate_scale"], 16, FI)
    _, shared["gzT"] = s_re(inp["L0_ffn_gate_zero"], 16, FI)
    shared["us_re"], shared["usT"] = s_re(inp["L0_ffn_up_scale"], 16, FI)
    _, shared["uzT"] = s_re(inp["L0_ffn_up_zero"], 16, FI)
    shared["ds_re"], shared["dsT"] = s_re(inp["L0_ffn_down_scale"], 43, C, perm)
    _, shared["dzT"] = s_re(inp["L0_ffn_down_zero"], 43, C, perm)

    band = np.zeros((128, 304), dtype=ml_dtypes.bfloat16)
    for p in range(128):
        band[p, 172 + p // 32] = 1
    shared["band"] = band
    pat4 = np.zeros((4, 128), dtype=ml_dtypes.bfloat16)
    for p in range(128):
        pat4[p // 32, p] = 1
    shared["pat4"] = pat4
    return shared, per_core



class Bld:
    ENG = ("t", "v", "s", "g", "y")

    def __init__(self):
        self.q = {e: [] for e in self.ENG}
        self.cnt = {}

    def sem(self, *names):
        for n in names:
            self.cnt[n] = 0

    def emit(self, eng, f, inc=None, k=1):
        if inc is not None:
            self.cnt[inc] += k

            def fn(e, S, f=f, inc=inc, k=k):
                f(e, S).then_inc(S[inc], k)
            self.q[eng].append(fn)
            return self.cnt[inc]
        self.q[eng].append(lambda e, S, f=f: f(e, S))
        return None

    def wait(self, eng, sem, val):
        if val > 0:
            self.q[eng].append(lambda e, S, sem=sem, val=val: e.wait_ge(S[sem], val))

    def drain(self, eng):
        self.q[eng].append(lambda e, S: e.drain())

    def dma(self, eng, sem, f):
        self.wait(eng, sem, self.cnt[sem])
        return self.emit(eng, f, inc=sem, k=16)


def build_nc(dbg=(), n_heads=H, do_p1=True, no_cc=False, skip=()):
    nc = bass.Bass("TRN2", target_bir_lowering=False)
    dbg = set(dbg)
    _, _, _, bd_all = _decay_consts()

    xp = nc.dram_tensor("xp", [T, C], F32, kind="ExternalInput")
    mask_t = nc.dram_tensor("mask", [128, 1], F32, kind="ExternalInput")
    qkvw = nc.dram_tensor("qkvw", [H, 3, 128, CCH * 128], F32, kind="ExternalInput")
    gatew = nc.dram_tensor("gatew", [CCH, 128, CCH * 128], F32, kind="ExternalInput")
    outw = nc.dram_tensor("outw", [CCH, 128, CCH * 128], F32, kind="ExternalInput")
    normw = nc.dram_tensor("normw", [128, CCH], F32, kind="ExternalInput")
    qd_t = nc.dram_tensor("qd", [128, H * BS], BF16, kind="ExternalInput")
    kd_t = nc.dram_tensor("kd", [128, H * 2], F32, kind="ExternalInput")
    dt_t = nc.dram_tensor("dt", [128, H * 2 * BS], BF16, kind="ExternalInput")
    ident_t = nc.dram_tensor("ident", [128, 128], BF16, kind="ExternalInput")
    ones1_t = nc.dram_tensor("ones1", [128, 1], BF16, kind="ExternalInput")
    onesr_t = nc.dram_tensor("onesr", [1, 128], F32, kind="ExternalInput")
    gq_t = nc.dram_tensor("gq", [43, 128, 16, 256], U8, kind="ExternalInput")
    uq_t = nc.dram_tensor("uq", [43, 128, 16, 256], U8, kind="ExternalInput")
    dq_t = nc.dram_tensor("dq", [16, 43, 128, 256], U8, kind="ExternalInput")
    sc_in = {}
    for nm, shp in (("gs_re", [4, 16 * FI]), ("us_re", [4, 16 * FI]), ("ds_re", [4, 43 * C]),
                    ("gsT", [64, FI]), ("gzT", [64, FI]), ("usT", [64, FI]), ("uzT", [64, FI]),
                    ("dsT", [172, C]), ("dzT", [172, C])):
        sc_in[nm] = nc.dram_tensor(nm, shp, F32, kind="ExternalInput")
    band_t = nc.dram_tensor("band", [128, 304], BF16, kind="ExternalInput")
    pat4_t = nc.dram_tensor("pat4", [4, 128], BF16, kind="ExternalInput")

    out_t = nc.dram_tensor("out", [T, C], F32, kind="ExternalOutput")

    xbf = nc.dram_tensor("xbf", [T, C], BF16, kind="Internal")
    q2d = nc.dram_tensor("q2d", [H, 128, T], BF16, kind="Internal")
    agin = nc.dram_tensor("agin", [128, H * 128], BF16, kind="Internal")
    agout = nc.dram_tensor("agout", [256, H * 128], BF16, kind="Internal")
    a2d = nc.dram_tensor("a2d", [C, T], BF16, kind="Internal")
    dpd = nc.dram_tensor("dpd", [C, T], BF16, kind="Internal")
    zs_g = nc.dram_tensor("zs_g", [64, FI], BF16, kind="Internal")
    zs_u = nc.dram_tensor("zs_u", [64, FI], BF16, kind="Internal")
    zs_d = nc.dram_tensor("zs_d", [172, C], BF16, kind="Internal")
    s1_g = nc.dram_tensor("s1_g", [4, 16 * FI], BF16, kind="Internal")
    s1_u = nc.dram_tensor("s1_u", [4, 16 * FI], BF16, kind="Internal")
    s1_d = nc.dram_tensor("s1_d", [4, 43 * C], BF16, kind="Internal")

    dbg_t = {}
    for name, shp, dt in (("attn", [C, T], F32), ("r1", [C, T], F32)):
        if name in dbg:
            dbg_t[name] = nc.dram_tensor("dbg_" + name, shp, dt, kind="ExternalOutput")

    b = Bld()
    b.sem("dc", "db", "dx", "ds0", "ds1", "ds2", "dd0", "dd1",
          "dw0", "dw1", "dfb", "dfs", "dfz", "dzt", "ddp", "dsa", "dtm", "dfo", "ds1c",
          "dag", "dcc", "dq2", "dq2s",
          "pe", "dv", "ac", "po", "fin")

    with ExitStack() as st:
        def sb(name, shape, dtype):
            return st.enter_context(nc.sbuf_tensor(name, shape, dtype))

        x_sb = sb("x_sb", [128, CCH * T], BF16)
        at_sb = sb("at_sb", [128, CCH * T], BF16)
        wsl = sb("wsl", [128, 3 * 4096], BF16)
        qkvt = sb("qkvt", [128, 3 * T], BF16)
        q2 = sb("q2", [128, T], BF16)
        ssb = sb("ssb", [128, 512], BF16)
        ktm = sb("ktm", [128, 256], BF16)
        vtm = sb("vtm", [128, 256], BF16)
        stf = sb("stf", [128, 2 * 128], F32)
        kvbf = sb("kvbf", [128, 128], BF16)
        gaux = sb("gaux", [128, 14848], BF16)
        kvin = gaux[:, 9216:9216 + H * 128]
        qdsb = gaux[:, 0:H * BS]
        kdsb = sb("kdsb", [128, H * 2], F32)
        dtsb = gaux[:, 8192:8192 + 1024]
        identb = sb("identb", [128, 128], BF16)
        ones1b = sb("ones1b", [128, 1], BF16)
        onesrb = sb("onesrb", [1, 128], F32)
        normwb = sb("normwb", [128, CCH], F32)
        masksb = sb("masksb", [128, 1], F32)
        ones128 = sb("ones128", [128, 128], BF16)
        fout = sb("fout", [128, 3072], F32)
        rows = fout[0:1, 0:2048]
        rwork = fout[0:1, 2048:3072]
        bandb = sb("bandb", [128, 304], BF16)
        pat4b = sb("pat4b", [4, 128], BF16)

        pbk = [st.enter_context(nc.psum_tensor(f"pb{i}", [128, 512], F32)) for i in range(7)]
        pbf = st.enter_context(nc.psum_tensor("pbf", [128, 1024], BF16))

        sems = {n: st.enter_context(nc.semaphore(n)) for n in b.cnt}

        b.emit("g", lambda e, S: e.dma_start(out=xbf[:, :], in_=xp[:, :]), inc="db", k=16)
        b.emit("g", lambda e, S: e.dma_start(out=masksb[:, :], in_=mask_t[:, :]), inc="dc", k=16)
        b.emit("g", lambda e, S: e.memset(ones128[:, :], 1.0), inc="po", k=1)
        ones_po = b.cnt["po"]
        b.emit("g", lambda e, S: e.dma_start(out=qdsb[:, :], in_=qd_t[:, :]), inc="dc", k=16)
        b.emit("g", lambda e, S: e.dma_start(out=kdsb[:, :], in_=kd_t[:, :]), inc="dc", k=16)
        b.emit("g", lambda e, S: e.dma_start(out=identb[:, :], in_=ident_t[:, :]), inc="dc", k=16)
        b.emit("g", lambda e, S: e.dma_start(out=ones1b[:, :], in_=ones1_t[:, :]), inc="dc", k=16)
        b.emit("g", lambda e, S: e.dma_start(out=onesrb[:, :], in_=onesr_t[:, :]), inc="dc", k=16)
        b.emit("g", lambda e, S: e.dma_start(out=normwb[:, :], in_=normw[:, :]), inc="dc", k=16)
        b.emit("g", lambda e, S: e.dma_start(out=bandb[:, :], in_=band_t[:, :]), inc="dc", k=16)
        b.emit("g", lambda e, S: e.dma_start(out=pat4b[:, :], in_=pat4_t[:, :]), inc="dc", k=16)
        consts_done = b.cnt["dc"]

        b.wait("y", "db", 16)
        for cc in range(CCH):
            b.emit("y", lambda e, S, cc=cc: e.dma_start_transpose(
                x_sb[:, cc * T:(cc + 1) * T], xbf[:, cc * 128:(cc + 1) * 128]), inc="dx", k=16)
        x_loaded = b.cnt["dx"]

        NSLOT = 3
        slot_of = {}
        slab_ready = {}
        slab_guard = [0] * NSLOT
        next_slot = [0]

        def load_slab(h, z):
            sl = next_slot[0] % NSLOT
            next_slot[0] += 1
            slot_of[(h, z)] = sl
            b.wait("g", "pe", slab_guard[sl])
            sem = f"ds{sl}"
            m = b.dma("g", sem, lambda e, S, h=h, z=z, sl=sl: e.dma_start(
                out=wsl[:, sl * 4096:(sl + 1) * 4096], in_=qkvw[h, z, :, :]))
            slab_ready[(h, z)] = (sem, m)

        def qkv_matmuls(h, z, comp_idx, src):
            sl = slot_of[(h, z)]
            sem, m = slab_ready[(h, z)]
            b.wait("t", sem, m)
            b.wait("t", "dv", b.cnt["dv"])
            for th in range(2):
                bank = pbk[2 * comp_idx + th]
                for cc in range(CCH):
                    b.emit("t", lambda e, S, bank=bank, sl=sl, cc=cc, th=th, src=src:
                           e.matmul(bank[:, :],
                                    wsl[:, sl * 4096 + cc * 128: sl * 4096 + (cc + 1) * 128],
                                    src[:, cc * T + th * TH: cc * T + th * TH + TH],
                                    start=(cc == 0), stop=(cc == CCH - 1)),
                           inc="pe" if cc == CCH - 1 else None)
            pe_m = b.cnt["pe"]
            b.wait("s", "pe", pe_m)
            for th in range(2):
                bank = pbk[2 * comp_idx + th]
                b.emit("s", lambda e, S, bank=bank, comp_idx=comp_idx, th=th:
                       e.activation(qkvt[:, comp_idx * T + th * TH: comp_idx * T + th * TH + TH],
                                    bank[:, :], AF.Silu), inc="ac")
            slab_guard[sl] = pe_m
            return b.cnt["ac"]

        def scan_block(h, n, with_outputs):
            bd = float(bd_all[h])
            par = n % 2
            kcol = T + n * BS
            vcol = 2 * T + n * BS
            b.wait("t", "dv", b.cnt["dv"])
            for sub in range(2):
                b.emit("t", lambda e, S, kcol=kcol, sub=sub:
                       e.transpose(pbf[:, sub * 128:(sub + 1) * 128],
                                   qkvt[:, kcol + sub * 128: kcol + (sub + 1) * 128], identb[:, :]))
            for sub in range(2):
                b.emit("t", lambda e, S, vcol=vcol, sub=sub:
                       e.transpose(pbf[:, 256 + sub * 128: 256 + (sub + 1) * 128],
                                   qkvt[:, vcol + sub * 128: vcol + (sub + 1) * 128], identb[:, :]),
                       inc="pe" if sub == 1 else None)
            tr_done = b.cnt["pe"]
            if with_outputs:
                qcol = n * BS
                for sub in range(2):
                    b.emit("t", lambda e, S, kcol=kcol, qcol=qcol, sub=sub:
                           e.matmul(pbk[6][:, sub * 256:(sub + 1) * 256],
                                    qkvt[:, kcol + sub * 128: kcol + (sub + 1) * 128],
                                    qkvt[:, qcol: qcol + 256], start=True, stop=True),
                           inc="pe" if sub == 1 else None)
                s_done = b.cnt["pe"]
                b.wait("v", "pe", s_done)
                dslot = (h % 2) * 512
                for sub in range(2):
                    b.emit("v", lambda e, S, sub=sub, dslot=dslot:
                           e.tensor_mul(ssb[:, sub * 256:(sub + 1) * 256],
                                        pbk[6][:, sub * 256:(sub + 1) * 256],
                                        dtsb[:, dslot + sub * 256: dslot + (sub + 1) * 256]),
                           inc="dv" if sub == 1 else None)
            b.wait("v", "pe", tr_done)
            for sub in range(2):
                b.emit("v", lambda e, S, sub=sub, h=h:
                       e.tensor_scalar(ktm[:, sub * 128:(sub + 1) * 128],
                                       pbf[:, sub * 128:(sub + 1) * 128],
                                       kdsb[:, h * 2 + sub: h * 2 + sub + 1], None, ALU.mult))
            b.emit("v", lambda e, S: e.tensor_copy(vtm[:, :], pbf[:, 256:512]), inc="dv")
            ev_done = b.cnt["dv"]
            b.wait("t", "dv", ev_done)
            if with_outputs:
                qcol = n * BS
                for sub in range(2):
                    b.emit("t", lambda e, S, sub=sub, last=(n == 0 and sub == 1):
                           e.matmul(pbk[1][:, 0:256], vtm[:, sub * 128:(sub + 1) * 128],
                                    ssb[:, sub * 256:(sub + 1) * 256],
                                    start=(sub == 0), stop=last),
                           inc="pe" if (n == 0 and sub == 1) else None)
                if n > 0:
                    b.emit("t", lambda e, S, qcol=qcol:
                           e.matmul(pbk[1][:, 0:256], kvbf[:, :], q2[:, qcol:qcol + 256],
                                    start=False, stop=True), inc="pe")
                att_done = b.cnt["pe"]
                b.wait("v", "pe", att_done)
                b.emit("v", lambda e, S, h=h, n=n:
                       e.tensor_copy(at_sb[:, h * T + n * BS: h * T + (n + 1) * BS],
                                     pbk[1][:, 0:256]), inc="dv")
            for sub in range(2):
                b.emit("t", lambda e, S, sub=sub:
                       e.matmul(pbk[0][:, 0:128], ktm[:, sub * 128:(sub + 1) * 128],
                                vtm[:, sub * 128:(sub + 1) * 128],
                                start=(sub == 0), stop=(sub == 1)),
                       inc="pe" if sub == 1 else None)
            c_done = b.cnt["pe"]
            b.wait("v", "pe", c_done)
            b.drain("v")
            src_st = stf[:, par * 128:(par + 1) * 128]
            dst_st = stf[:, (1 - par) * 128:(2 - par) * 128]
            if n == 0:
                b.emit("v", lambda e, S, dst=dst_st: e.tensor_copy(dst, pbk[0][:, 0:128]))
            else:
                b.emit("v", lambda e, S, src=src_st, dst=dst_st, bd=bd:
                       e.scalar_tensor_tensor(dst, src, bd, pbk[0][:, 0:128], ALU.mult, ALU.add))
            b.drain("v")
            b.emit("v", lambda e, S, dst=dst_st: e.tensor_copy(kvbf[:, :], dst), inc="dv")

        b.emit("g", lambda e, S: e.dma_start(out=s1_g[:, :], in_=sc_in["gs_re"][:, :]), inc="ds1c", k=16)
        b.emit("g", lambda e, S: e.dma_start(out=s1_u[:, :], in_=sc_in["us_re"][:, :]), inc="ds1c", k=16)
        b.emit("g", lambda e, S: e.dma_start(out=s1_d[:, :], in_=sc_in["ds_re"][:, :]), inc="ds1c", k=16)
        s1_done = b.cnt["ds1c"]
        for sT_n, zT_n, dst in (("gsT", "gzT", zs_g), ("usT", "uzT", zs_u)):
            for c0, c1 in ((0, 5504), (5504, FI)):
                w = c1 - c0
                b.wait("g", "dfb", b.cnt["dfb"])
                b.dma("g", "dfz", lambda e, S, sT_n=sT_n, c0=c0, c1=c1, w=w: e.dma_start(
                    out=wsl[0:64, 0:w], in_=sc_in[sT_n][:, c0:c1]))
                lm = b.dma("g", "dfz", lambda e, S, zT_n=zT_n, c0=c0, c1=c1, w=w: e.dma_start(
                    out=wsl[0:64, 5504:5504 + w], in_=sc_in[zT_n][:, c0:c1]))
                b.wait("g", "dfz", lm)
                b.emit("g", lambda e, S, w=w: e.tensor_mul(
                    wsl[0:64, 5504:5504 + w], wsl[0:64, 5504:5504 + w], wsl[0:64, 0:w]),
                    inc="po")
                b.wait("g", "po", b.cnt["po"])
                b.dma("g", "dfb", lambda e, S, dst=dst, c0=c0, c1=c1, w=w: e.dma_start(
                    out=dst[:, c0:c1], in_=wsl[0:64, 5504:5504 + w]))
        for r0_, r1_ in ((0, 128), (128, 172)):
            nrw = r1_ - r0_
            b.wait("g", "dfb", b.cnt["dfb"])
            b.dma("g", "dfz", lambda e, S, r0_=r0_, nrw=nrw: e.dma_start(
                out=wsl[0:nrw, 0:C], in_=sc_in["dsT"][r0_:r0_ + nrw, :]))
            lm = b.dma("g", "dfz", lambda e, S, r0_=r0_, nrw=nrw: e.dma_start(
                out=wsl[0:nrw, 4096:4096 + C], in_=sc_in["dzT"][r0_:r0_ + nrw, :]))
            b.wait("g", "dfz", lm)
            b.emit("g", lambda e, S, nrw=nrw: e.tensor_mul(
                wsl[0:nrw, 8192:8192 + C], wsl[0:nrw, 4096:4096 + C],
                wsl[0:nrw, 0:C]), inc="po")
            b.wait("g", "po", b.cnt["po"])
            b.dma("g", "dfb", lambda e, S, r0_=r0_, nrw=nrw: e.dma_start(
                out=zs_d[r0_:r0_ + nrw, :], in_=wsl[0:nrw, 8192:8192 + C]))
        b.wait("g", "dfb", b.cnt["dfb"])

        b.wait("t", "dx", x_loaded)
        b.wait("t", "dc", consts_done)
        b.wait("t", "po", ones_po)
        b.wait("v", "dc", consts_done)
        b.wait("v", "dx", x_loaded)
        q2s_guard = [0, 0]
        for h in range(n_heads if do_p1 else 0):
            bd = float(bd_all[h])
            load_slab(h, 0)
            load_slab(h, 1)
            load_slab(h, 2)
            dslot = (h % 2) * 512
            dsem = f"dd{h % 2}"
            b.wait("y", "dv", b.cnt["dv"])
            d_ready = b.dma("y", dsem, lambda e, S, h=h, dslot=dslot: e.dma_start(
                out=dtsb[:, dslot:dslot + 512], in_=dt_t[:, h * 512:(h + 1) * 512]))
            qkv_matmuls(h, 0, 0, x_sb)
            qkv_matmuls(h, 1, 1, x_sb)
            qkv_matmuls(h, 2, 2, x_sb)
            ac_m = b.cnt["ac"]
            b.wait("v", "ac", ac_m)
            b.wait("v", dsem, d_ready)
            for n in range(NB):
                b.emit("v", lambda e, S, h=h, n=n:
                       e.tensor_mul(q2[:, n * BS:(n + 1) * BS],
                                    qkvt[:, n * BS:(n + 1) * BS],
                                    qdsb[:, h * BS:(h + 1) * BS]),
                       inc="dv" if n == NB - 1 else None)
            fsl = h % 2
            if "q2s" not in skip:
                b.wait("v", "dq2s", q2s_guard[fsl])
                b.drain("v")
                for n in range(NB):
                    b.emit("v", lambda e, S, n=n, fsl=fsl, f=float(bd ** n):
                           e.tensor_scalar(fout[:, fsl * 1024 + n * BS: fsl * 1024 + (n + 1) * BS],
                                           q2[:, n * BS:(n + 1) * BS], f, None, ALU.mult),
                           inc="dv" if n == NB - 1 else None)
                b.wait("g", "dv", b.cnt["dv"])
                q2s_guard[fsl] = b.dma("g", "dq2s", lambda e, S, h=h, fsl=fsl: e.dma_start(
                    out=q2d[h], in_=fout[:, fsl * 1024:(fsl + 1) * 1024]))
            b.wait("t", "dv", b.cnt["dv"])
            for n in range(NB):
                scan_block(h, n, with_outputs=True)
            b.drain("v")
            b.emit("v", lambda e, S, h=h:
                   e.tensor_copy(kvin[:, h * 128:(h + 1) * 128], stf[:, 0:128]),
                   inc="dv")

        qkv_done_pe = b.cnt["pe"]

        if "ag" not in skip:
            b.wait("g", "dv", b.cnt["dv"])
            agm = b.dma("g", "dag", lambda e, S: e.dma_start(out=agin[:, :], in_=kvin[:, :]))
            b.wait("g", "dag", agm)
            if no_cc:
                b.emit("g", lambda e, S: e.dma_start(out=agout[0:128, :], in_=agin[:, :]),
                       inc="dcc", k=16)
                b.wait("g", "dcc", 16)
            else:
                b.emit("g", lambda e, S: e.collective_compute(
                    "AllGather", ALU.bypass,
                    replica_groups=[[0, 1], [2, 3], [4, 5], [6, 7]],
                    ins=[agin.ap().opt()], outs=[agout.ap().opt()]), inc="dcc", k=1)
                b.wait("g", "dcc", 1)
            agm2 = b.dma("g", "dag", lambda e, S: e.dma_start(out=kvin[:, :], in_=agout[0:128, :]))
            b.wait("v", "dag", agm2)
            b.drain("v")
            b.emit("v", lambda e, S: e.tensor_scalar(kvin[:, :], kvin[:, :],
                                                     masksb[:, 0:1], None, ALU.mult), inc="dv")
        mask_dv = b.cnt["dv"]

        b.wait("t", "ac", b.cnt["ac"])
        corr_guard = [b.cnt["pe"], b.cnt["pe"]]
        corr_bank_dv = [0, 0]
        for h in range(0 if "corr" in skip else (n_heads if do_p1 else 0)):
            sl = h % 2
            bks = (pbk[4], pbk[5]) if sl == 0 else (pbk[2], pbk[3])
            if "corrld" not in skip:
                b.wait("y", "pe", corr_guard[sl])
                qm = b.dma("y", "dq2", lambda e, S, h=h, sl=sl: e.dma_start(
                    out=qkvt[:, sl * 1024:(sl + 1) * 1024], in_=q2d[h]))
                b.wait("t", "dq2", qm)
            b.wait("t", "dv", mask_dv)
            if "corrmm" not in skip:
                b.wait("t", "dv", corr_bank_dv[sl])
                for th in range(2):
                    b.emit("t", lambda e, S, h=h, sl=sl, th=th, bks=bks:
                           e.matmul(bks[th][:, :], kvin[:, h * 128:(h + 1) * 128],
                                    qkvt[:, sl * 1024 + th * TH: sl * 1024 + th * TH + TH],
                                    start=True, stop=True),
                           inc="pe" if th == 1 else None)
                corr_guard[sl] = b.cnt["pe"]
            if "corradd" not in skip:
                b.wait("v", "pe", b.cnt["pe"])
                for th in range(2):
                    b.emit("v", lambda e, S, h=h, th=th, bks=bks:
                           e.tensor_add(at_sb[:, h * T + th * TH: h * T + th * TH + TH],
                                        at_sb[:, h * T + th * TH: h * T + th * TH + TH],
                                        bks[th][:, :]),
                           inc="dv" if th == 1 else None)
                corr_bank_dv[sl] = b.cnt["dv"]

        if "attn" in dbg:
            b.wait("g", "dv", b.cnt["dv"])
            for cc in range(CCH):
                b.emit("g", lambda e, S, cc=cc: e.dma_start(
                    out=dbg_t["attn"][cc * 128:(cc + 1) * 128, :],
                    in_=at_sb[:, cc * T:(cc + 1) * T]), inc="fin", k=16)

        SQ0, SQ1 = 8192, 9216
        GS0, GS1 = 8192, 9216
        RSTD_BC, MEAN_BC = 12288, 13312

        sq_guard = [0, 0]
        attn_done = b.cnt["dv"]
        for cc in range(CCH):
            sl = cc % 2
            b.wait("v", "pe", sq_guard[sl])
            b.drain("v")
            b.emit("v", lambda e, S, cc=cc, sl=sl:
                   e.tensor_mul(gaux[:, sl * 1024:(sl + 1) * 1024],
                                at_sb[:, cc * T:(cc + 1) * T], at_sb[:, cc * T:(cc + 1) * T]),
                   inc="dv")
            dv_m = b.cnt["dv"]
            b.wait("t", "dv", dv_m)
            for th in range(2):
                b.emit("t", lambda e, S, sl=sl, th=th, cc=cc:
                       e.matmul(pbk[4 + th][:, :], ones128[:, :],
                                gaux[:, sl * 1024 + th * TH: sl * 1024 + th * TH + TH],
                                start=(cc == 0), stop=(cc == CCH - 1)),
                       inc="pe" if th == 1 else None)
            sq_guard[sl] = b.cnt["pe"]
        rms_mm = b.cnt["pe"]
        b.wait("v", "dq2s", b.cnt["dq2s"])
        b.wait("v", "pe", rms_mm)
        b.drain("v")
        b.emit("v", lambda e, S: e.tensor_copy(rows[0:1, 0:512], pbk[4][0:1, :]))
        b.emit("v", lambda e, S: e.tensor_copy(rows[0:1, 512:1024], pbk[5][0:1, :]))
        b.drain("v")
        b.emit("v", lambda e, S: e.tensor_scalar(rows[0:1, 0:1024], rows[0:1, 0:1024],
                                                 1.0 / C, EPS, ALU.mult, ALU.add), inc="dv")
        b.wait("s", "dv", b.cnt["dv"])
        b.emit("s", lambda e, S: e.activation(rows[0:1, 1024:2048], rows[0:1, 0:1024], AF.Sqrt),
               inc="ac")
        b.wait("v", "ac", b.cnt["ac"])
        b.emit("v", lambda e, S: e.reciprocal(rows[0:1, 0:1024], rows[0:1, 1024:2048]), inc="dv")
        b.wait("t", "dv", b.cnt["dv"])
        for th in range(2):
            b.emit("t", lambda e, S, th=th:
                   e.matmul(pbk[4 + th][:, :], onesrb[:, :], rows[0:1, th * TH:(th + 1) * TH],
                            start=True, stop=True), inc="pe" if th == 1 else None)
        b.wait("s", "pe", b.cnt["pe"])
        for th in range(2):
            b.emit("s", lambda e, S, th=th:
                   e.activation(gaux[:, RSTD_BC + th * TH: RSTD_BC + th * TH + TH],
                                pbk[4 + th][:, :], AF.Copy), inc="ac")
        rstd_bc_done = b.cnt["ac"]

        for dsn in ("ds0", "ds1", "ds2"):
            b.wait("g", dsn, b.cnt[dsn])
        wslot_guard = [qkv_done_pe, qkv_done_pe]
        gpsum_guard = [0, 0]
        gstage_guard = [0, 0]
        z_first = True
        for oc in range(CCH):
            sl = oc % 2
            wsem = f"dw{sl}"
            b.wait("g", "pe", wslot_guard[sl])
            wm = b.dma("g", wsem, lambda e, S, oc=oc, sl=sl: e.dma_start(
                out=wsl[:, sl * 4096:(sl + 1) * 4096], in_=gatew[oc, :, :]))
            b.wait("t", wsem, wm)
            b.wait("t", "ac", gpsum_guard[sl])
            for th in range(2):
                bank = pbk[2 * sl + th]
                for cc in range(CCH):
                    b.emit("t", lambda e, S, bank=bank, sl=sl, cc=cc, th=th:
                           e.matmul(bank[:, :],
                                    wsl[:, sl * 4096 + cc * 128: sl * 4096 + (cc + 1) * 128],
                                    x_sb[:, cc * T + th * TH: cc * T + th * TH + TH],
                                    start=(cc == 0), stop=(cc == CCH - 1)),
                           inc="pe" if cc == CCH - 1 else None)
            wslot_guard[sl] = b.cnt["pe"]
            b.wait("s", "pe", b.cnt["pe"])
            if z_first:
                b.wait("s", "ac", rstd_bc_done)
            b.wait("s", "dv", gstage_guard[sl])
            for th in range(2):
                b.emit("s", lambda e, S, sl=sl, th=th:
                       e.activation(wsl[:, (GS0 if sl == 0 else GS1) + th * TH:
                                        (GS0 if sl == 0 else GS1) + th * TH + TH],
                                    pbk[2 * sl + th][:, :], AF.Sigmoid), inc="ac")
            gpsum_guard[sl] = b.cnt["ac"]
            b.wait("v", "ac", b.cnt["ac"])
            if z_first:
                b.wait("v", "ac", rstd_bc_done)
                z_first = False
            b.drain("v")
            b.emit("v", lambda e, S, oc=oc:
                   e.tensor_mul(q2[:, :], at_sb[:, oc * T:(oc + 1) * T],
                                gaux[:, RSTD_BC:RSTD_BC + 1024]))
            b.drain("v")
            b.emit("v", lambda e, S, oc=oc, sl=sl:
                   e.scalar_tensor_tensor(at_sb[:, oc * T:(oc + 1) * T], q2[:, :],
                                          normwb[:, oc:oc + 1],
                                          wsl[:, (GS0 if sl == 0 else GS1):(GS0 if sl == 0 else GS1) + 1024],
                                          ALU.mult, ALU.mult), inc="dv")
            gstage_guard[sl] = b.cnt["dv"]

        a_guard = 0
        b.wait("t", "ac", b.cnt["ac"])
        b.wait("t", "dv", b.cnt["dv"])
        for oc in range(CCH):
            sl = oc % 2
            wsem = f"dw{sl}"
            b.wait("g", "pe", wslot_guard[sl])
            wm = b.dma("g", wsem, lambda e, S, oc=oc, sl=sl: e.dma_start(
                out=wsl[:, sl * 4096:(sl + 1) * 4096], in_=outw[oc, :, :]))
            b.wait("t", wsem, wm)
            b.wait("t", "dv", a_guard)
            for th in range(2):
                for cc in range(CCH):
                    b.emit("t", lambda e, S, sl=sl, cc=cc, th=th:
                           e.matmul(pbk[th][:, :],
                                    wsl[:, sl * 4096 + cc * 128: sl * 4096 + (cc + 1) * 128],
                                    at_sb[:, cc * T + th * TH: cc * T + th * TH + TH],
                                    start=(cc == 0), stop=(cc == CCH - 1)),
                           inc="pe" if cc == CCH - 1 else None)
            wslot_guard[sl] = b.cnt["pe"]
            b.wait("v", "pe", b.cnt["pe"])
            for th in range(2):
                b.emit("v", lambda e, S, oc=oc, th=th:
                       e.tensor_add(x_sb[:, oc * T + th * TH: oc * T + th * TH + TH],
                                    x_sb[:, oc * T + th * TH: oc * T + th * TH + TH],
                                    pbk[th][:, :]), inc="dv" if th == 1 else None)
            a_guard = b.cnt["dv"]
            b.wait("v", "pe", sq_guard[sl])
            b.drain("v")
            b.emit("v", lambda e, S, oc=oc, sl=sl:
                   e.tensor_mul(gaux[:, sl * 1024:(sl + 1) * 1024],
                                x_sb[:, oc * T:(oc + 1) * T], x_sb[:, oc * T:(oc + 1) * T]),
                   inc="dv")
            b.wait("t", "dv", b.cnt["dv"])
            for th in range(2):
                b.emit("t", lambda e, S, oc=oc, th=th:
                       e.matmul(pbk[4 + th][:, :], ones128[:, :],
                                x_sb[:, oc * T + th * TH: oc * T + th * TH + TH],
                                start=(oc == 0), stop=(oc == CCH - 1)))
            for th in range(2):
                b.emit("t", lambda e, S, oc=oc, sl=sl, th=th:
                       e.matmul(pbk[2 + th][:, :], ones128[:, :],
                                gaux[:, (0 if sl == 0 else 1024) + th * TH:
                                     (0 if sl == 0 else 1024) + th * TH + TH],
                                start=(oc == 0), stop=(oc == CCH - 1)),
                       inc="pe" if th == 1 else None)
            sq_guard[sl] = b.cnt["pe"]
        ln1_mm = b.cnt["pe"]
        b.wait("v", "pe", ln1_mm)
        b.emit("v", lambda e, S: e.tensor_copy(rows[0:1, 0:512], pbk[4][0:1, :]))
        b.emit("v", lambda e, S: e.tensor_copy(rows[0:1, 512:1024], pbk[5][0:1, :]))
        b.emit("v", lambda e, S: e.tensor_copy(rows[0:1, 1024:1536], pbk[2][0:1, :]))
        b.emit("v", lambda e, S: e.tensor_copy(rows[0:1, 1536:2048], pbk[3][0:1, :]))
        b.drain("v")
        b.emit("v", lambda e, S: e.tensor_scalar(rows[0:1, 0:1024], rows[0:1, 0:1024],
                                                 1.0 / C, None, ALU.mult))
        b.drain("v")
        b.emit("v", lambda e, S: e.tensor_mul(rwork, rows[0:1, 0:1024],
                                              rows[0:1, 0:1024]))
        b.drain("v")
        b.emit("v", lambda e, S: e.scalar_tensor_tensor(rows[0:1, 1024:2048], rows[0:1, 1024:2048],
                                                        1.0 / C, rwork,
                                                        ALU.mult, ALU.subtract))
        b.drain("v")
        b.emit("v", lambda e, S: e.tensor_scalar(rows[0:1, 1024:2048], rows[0:1, 1024:2048],
                                                 EPS, None, ALU.add), inc="dv")
        b.wait("s", "dv", b.cnt["dv"])
        b.emit("s", lambda e, S: e.activation(rwork, rows[0:1, 1024:2048], AF.Sqrt),
               inc="ac")
        b.wait("v", "ac", b.cnt["ac"])
        b.emit("v", lambda e, S: e.reciprocal(rows[0:1, 1024:2048], rwork), inc="dv")
        b.wait("t", "dv", b.cnt["dv"])
        for th in range(2):
            b.emit("t", lambda e, S, th=th:
                   e.matmul(pbk[4 + th][:, :], onesrb[:, :], rows[0:1, th * TH:(th + 1) * TH],
                            start=True, stop=True))
        for th in range(2):
            b.emit("t", lambda e, S, th=th:
                   e.matmul(pbk[2 + th][:, :], onesrb[:, :], rows[0:1, 1024 + th * TH:1024 + (th + 1) * TH],
                            start=True, stop=True), inc="pe" if th == 1 else None)
        b.wait("s", "pe", b.cnt["pe"])
        for th in range(2):
            b.emit("s", lambda e, S, th=th:
                   e.activation(gaux[:, MEAN_BC + th * TH: MEAN_BC + th * TH + TH],
                                pbk[4 + th][:, :], AF.Copy))
        for th in range(2):
            b.emit("s", lambda e, S, th=th:
                   e.activation(gaux[:, RSTD_BC + th * TH: RSTD_BC + th * TH + TH],
                                pbk[2 + th][:, :], AF.Copy), inc="ac")
        ln1_bc = b.cnt["ac"]
        b.wait("v", "ac", ln1_bc)
        for cc in range(CCH):
            b.drain("v")
            b.emit("v", lambda e, S, cc=cc:
                   e.tensor_sub(q2[:, :], x_sb[:, cc * T:(cc + 1) * T],
                                gaux[:, MEAN_BC:MEAN_BC + 1024]))
            b.drain("v")
            b.emit("v", lambda e, S, cc=cc:
                   e.tensor_mul(x_sb[:, cc * T:(cc + 1) * T], q2[:, :],
                                gaux[:, RSTD_BC:RSTD_BC + 1024]), inc="dv")
        r1_done = b.cnt["dv"]

        if "r1" in dbg:
            b.wait("g", "dv", r1_done)
            for cc in range(CCH):
                b.emit("g", lambda e, S, cc=cc: e.dma_start(
                    out=dbg_t["r1"][cc * 128:(cc + 1) * 128, :],
                    in_=x_sb[:, cc * T:(cc + 1) * T]), inc="fin", k=16)

        HI_T, LO_T, S_SL = 0, 4096, 8192
        XSUM, HSUM = 12288, 13312
        ZSD_T, ZSG_T = 14336, 14592
        dpo = qkvt[:, 2048:3072]

        b.wait("t", "dv", r1_done)
        for cc in range(CCH):
            jc = cc // 2
            for th in range(2):
                b.emit("t", lambda e, S, cc=cc, jc=jc, th=th:
                       e.matmul(pbk[th][:, :], bandb[:, 172 - 4 * jc: 300 - 4 * jc],
                                x_sb[:, cc * T + th * TH: cc * T + th * TH + TH],
                                start=(cc == 0), stop=(cc == CCH - 1)),
                       inc="pe" if (cc == CCH - 1 and th == 1) else None)
        pe_ffn0 = b.cnt["pe"]
        b.wait("v", "pe", pe_ffn0)
        b.emit("v", lambda e, S: e.tensor_scalar(gaux[0:64, XSUM:XSUM + 512],
                                                 pbk[0][0:64, :], -1.0, None, ALU.mult))
        b.emit("v", lambda e, S: e.tensor_scalar(gaux[0:64, XSUM + 512:XSUM + 1024],
                                                 pbk[1][0:64, :], -1.0, None, ALU.mult),
               inc="dv")
        xsum_done = b.cnt["dv"]

        b.wait("y", "ds1c", s1_done)
        b.wait("y", "dd0", b.cnt["dd0"])
        b.wait("y", "dd1", b.cnt["dd1"])
        b.wait("y", "dv", r1_done)
        b.wait("g", "dw0", b.cnt["dw0"])
        b.wait("g", "dw1", b.cnt["dw1"])
        PH_R = [(0, 15), (15, 29), (29, 43)]
        H135 = 135.53125
        HB = 2048
        ZSL = (ZSD_T, ZSG_T)

        units = []
        for ph, (rr0, rr1) in enumerate(PH_R):
            for r in range(rr0, rr1):
                for X in ("g", "u"):
                    units.append(("gu", ph, r, X, 0))
                    units.append(("gu", ph, r, X, 1))
            for how in range(16):
                units.append(("dn", ph, 0, how, 0))
                units.append(("dn", ph, 0, how, 1))

        st_deq_dv = {}
        st_pe_free = {}
        init_dv = r1_done
        init_pe = pe_ffn0
        zs_free = [init_pe, init_pe]
        zs_mark = {}
        tile_ctr = [0]
        ddp_m = {}
        ms = {
            "g_bank_dv": xsum_done,
            "silu_ac": 0,
            "h_evac_dv": 0,
            "pending_hsum": None,
            "hsum_done": 0,
            "dn_evac_dv": [0, 0],
            "dpp_guard": [0, 0],
        }

        def deq_unit(i):
            kind = units[i][0]
            pp = i % 2
            g2 = st_deq_dv.get(i - 2, init_dv)
            p2 = st_pe_free.get(i - 2, init_pe)
            if kind == "gu":
                _, ph, r, X, h = units[i]
                w = HB
                wq_src = gq_t if X == "g" else uq_t
                s1w = s1_g if X == "g" else s1_u
                b.wait("g", "dv", g2)
                bm = b.dma("g", "dfb", lambda e, S, wq_src=wq_src, r=r, h=h, pp=pp: e.dma_start(
                    out=wsl[:, pp * HB: pp * HB + HB],
                    in_=wq_src[r, :, 8 * h:8 * h + 8, :].rearrange("p cb i -> p (cb i)")))
                b.wait("y", "dv", g2)
                b.wait("y", "dfs", b.cnt["dfs"])
                for gs4 in range(4):
                    b.emit("y", lambda e, S, s1w=s1w, r=r, h=h, gs4=gs4, pp=pp: e.dma_start(
                        out=gaux[gs4 * 32:(gs4 + 1) * 32, S_SL + pp * HB:S_SL + pp * HB + HB],
                        in_=bass.AP(s1w, gs4 * 16 * FI + 8 * h * FI + 256 * r,
                                    [[0, 32], [FI, 8], [1, 256]])), inc="dfs", k=16)
                sm = b.cnt["dfs"]
                if h == 0:
                    zsd = zs_g if X == "g" else zs_u
                    sl = tile_ctr[0] % 2
                    tile_ctr[0] += 1
                    b.wait("y", "pe", zs_free[sl])
                    zs_mark[("gu", r, X)] = (sl, b.dma(
                        "y", "dzt", lambda e, S, zsd=zsd, r=r, sl=sl: e.dma_start(
                            out=gaux[0:64, ZSL[sl]:ZSL[sl] + 256],
                            in_=zsd[0:64, 256 * r:256 * r + 256])))
            else:
                _, ph, _, how, h = units[i]
                rr0, rr1 = PH_R[ph]
                nr = rr1 - rr0
                k0, nh = (0, 8) if h == 0 else (8, nr - 8)
                w = nh * 256
                b.wait("g", "dv", g2)
                bm = b.dma("g", "dfb", lambda e, S, how=how, rr0=rr0, k0=k0, nh=nh, pp=pp: e.dma_start(
                    out=wsl[:, pp * HB: pp * HB + nh * 256].rearrange("p (ib i) -> p ib i", ib=nh),
                    in_=dq_t[how, rr0 + k0:rr0 + k0 + nh].rearrange("ib p i -> p ib i")))
                b.wait("y", "dv", g2)
                b.wait("y", "dfs", b.cnt["dfs"])
                for gs4 in range(4):
                    b.emit("y", lambda e, S, how=how, rr0=rr0, k0=k0, nh=nh, gs4=gs4, pp=pp: e.dma_start(
                        out=gaux[gs4 * 32:(gs4 + 1) * 32, S_SL + pp * HB:S_SL + pp * HB + nh * 256],
                        in_=bass.AP(s1_d, gs4 * 43 * C + (rr0 + k0) * C + 256 * how,
                                    [[0, 32], [C, nh], [1, 256]])), inc="dfs", k=16)
                sm = b.cnt["dfs"]
                if h == 0:
                    sl = tile_ctr[0] % 2
                    tile_ctr[0] += 1
                    b.wait("y", "pe", zs_free[sl])
                    zs_mark[("dn", ph, how)] = (sl, b.dma(
                        "y", "dzt", lambda e, S, rr0=rr0, nr=nr, how=how, sl=sl: e.dma_start(
                            out=gaux[0:4 * nr, ZSL[sl]:ZSL[sl] + 256],
                            in_=zs_d[4 * rr0:4 * rr0 + 4 * nr, 256 * how:256 * how + 256])))
            b.wait("s", "dfb", bm)
            b.wait("s", "dv", g2)
            b.emit("s", lambda e, S, pp=pp, w=w: e.activation(
                wsl[:, 4096 + pp * HB:4096 + pp * HB + w], wsl[:, pp * HB:pp * HB + w],
                AF.Copy, bias=H135, scale=0.0625), inc="ac")
            b.wait("v", "ac", b.cnt["ac"])
            b.emit("v", lambda e, S, pp=pp, w=w: e.tensor_scalar(
                wsl[:, 8192 + pp * HB:8192 + pp * HB + w],
                wsl[:, 4096 + pp * HB:4096 + pp * HB + w], 136.0, None, ALU.subtract))
            b.drain("v")
            b.emit("v", lambda e, S, pp=pp, w=w: e.scalar_tensor_tensor(
                wsl[:, 4096 + pp * HB:4096 + pp * HB + w],
                wsl[:, 8192 + pp * HB:8192 + pp * HB + w], -16.0,
                wsl[:, pp * HB:pp * HB + w], ALU.mult, ALU.add))
            b.wait("v", "dfs", sm)
            b.wait("v", "pe", p2)
            b.drain("v")
            b.emit("v", lambda e, S, pp=pp, w=w: e.tensor_mul(
                gaux[:, LO_T + pp * HB:LO_T + pp * HB + w],
                wsl[:, 4096 + pp * HB:4096 + pp * HB + w],
                gaux[:, S_SL + pp * HB:S_SL + pp * HB + w]))
            b.emit("v", lambda e, S, pp=pp, w=w: e.tensor_mul(
                gaux[:, HI_T + pp * HB:HI_T + pp * HB + w],
                wsl[:, 8192 + pp * HB:8192 + pp * HB + w],
                gaux[:, S_SL + pp * HB:S_SL + pp * HB + w]), inc="dv")
            st_deq_dv[i] = b.cnt["dv"]

        def emit_hsum(ph, r, guard_dv):
            rr0, rr1 = PH_R[ph]
            nr = rr1 - rr0
            b.wait("t", "dv", guard_dv)
            for par in range(2):
                idx = 2 * (r - rr0) + par
                lr = r - rr0
                for th in range(2):
                    b.emit("t", lambda e, S, idx=idx, lr=lr, th=th, nr=nr:
                           e.matmul(pbk[4 + th][:, :],
                                    bandb[:, 172 - 4 * lr: 300 - 4 * lr],
                                    at_sb[:, idx * T + th * TH: idx * T + th * TH + TH],
                                    start=(idx == 0), stop=(idx == 2 * nr - 1)),
                           inc="pe" if (par == 1 and th == 1) else None)

        def emit_hsum_evac(ph):
            rr0, rr1 = PH_R[ph]
            nr = rr1 - rr0
            b.wait("v", "pe", b.cnt["pe"])
            b.emit("v", lambda e, S, nr=nr: e.tensor_scalar(
                gaux[0:4 * nr, HSUM:HSUM + 512], pbk[4][0:4 * nr, :], -1.0, None, ALU.mult))
            b.emit("v", lambda e, S, nr=nr: e.tensor_scalar(
                gaux[0:4 * nr, HSUM + 512:HSUM + 1024], pbk[5][0:4 * nr, :], -1.0, None, ALU.mult),
                inc="dv")
            ms["hsum_done"] = b.cnt["dv"]

        def grp_gu(i):
            _, ph, r, X, h = units[i]
            rr0, rr1 = PH_R[ph]
            pp = i % 2
            b.wait("t", "dv", st_deq_dv[i])
            if h == 0:
                sl, zm = zs_mark[("gu", r, X)]
                b.wait("t", "dzt", zm)
                if X == "g":
                    b.wait("t", "dv", ms["g_bank_dv"])
                else:
                    b.wait("t", "ac", ms["silu_ac"])
                for par in range(2):
                    for th in range(2):
                        bank = pbk[2 * par + th]
                        b.emit("t", lambda e, S, bank=bank, par=par, th=th, sl=sl:
                               e.matmul(bank[:, :],
                                        gaux[0:64, ZSL[sl] + par:ZSL[sl] + 256:2],
                                        gaux[0:64, XSUM + th * TH:XSUM + th * TH + TH],
                                        start=True, stop=False))
                        for cb in range(8):
                            for cpar in range(2):
                                src_t = HI_T if cpar == 0 else LO_T
                                cc = 2 * cb + cpar
                                lastu = (par == 1 and th == 1 and cb == 7 and cpar == 1)
                                b.emit("t", lambda e, S, bank=bank, src_t=src_t, cb=cb,
                                       pp=pp, cc=cc, th=th, par=par:
                                       e.matmul(bank[:, :],
                                                gaux[:, src_t + pp * HB + cb * 256 + par: src_t + pp * HB + (cb + 1) * 256: 2],
                                                x_sb[:, cc * T + th * TH: cc * T + th * TH + TH],
                                                start=False, stop=False),
                                       inc="pe" if lastu else None)
                zs_free[sl] = b.cnt["pe"]
                st_pe_free[i] = b.cnt["pe"]
            else:
                for par in range(2):
                    for th in range(2):
                        bank = pbk[2 * par + th]
                        for cb in range(8, 16):
                            for cpar in range(2):
                                src_t = HI_T if cpar == 0 else LO_T
                                cc = 2 * cb + cpar
                                last = (cb == 15 and cpar == 1)
                                lastu = (par == 1 and th == 1 and last)
                                b.emit("t", lambda e, S, bank=bank, src_t=src_t, cb=cb,
                                       pp=pp, cc=cc, th=th, last=last, par=par:
                                       e.matmul(bank[:, :],
                                                gaux[:, src_t + pp * HB + (cb - 8) * 256 + par: src_t + pp * HB + (cb - 7) * 256: 2],
                                                x_sb[:, cc * T + th * TH: cc * T + th * TH + TH],
                                                start=False, stop=last),
                                       inc="pe" if lastu else None)
                st_pe_free[i] = b.cnt["pe"]
                if X == "g":
                    b.wait("s", "pe", b.cnt["pe"])
                    b.wait("s", "dv", ms["g_bank_dv"])
                    for par in range(2):
                        for th in range(2):
                            b.emit("s", lambda e, S, par=par, th=th:
                                   e.activation(qkvt[:, par * 1024 + th * TH: par * 1024 + th * TH + TH],
                                                pbk[2 * par + th][:, :], AF.Silu), inc="ac")
                    ms["silu_ac"] = b.cnt["ac"]
                    if ms["pending_hsum"] is not None:
                        emit_hsum(*ms["pending_hsum"])
                        ms["pending_hsum"] = None
                else:
                    b.wait("v", "pe", b.cnt["pe"])
                    for par in range(2):
                        idx = 2 * (r - rr0) + par
                        for th in range(2):
                            b.emit("v", lambda e, S, par=par, th=th, idx=idx:
                                   e.tensor_mul(at_sb[:, idx * T + th * TH: idx * T + th * TH + TH],
                                                qkvt[:, par * 1024 + th * TH: par * 1024 + th * TH + TH],
                                                pbk[2 * par + th][:, :]),
                                   inc="dv" if (par == 1 and th == 1) else None)
                    ms["h_evac_dv"] = b.cnt["dv"]
                    ms["g_bank_dv"] = b.cnt["dv"]
                    if r == rr1 - 1:
                        emit_hsum(ph, r, ms["h_evac_dv"])
                        emit_hsum_evac(ph)
                    else:
                        ms["pending_hsum"] = (ph, r, ms["h_evac_dv"])

        def dn_group(ph, how, hsub, half, pp, sl=None):
            rr0, rr1 = PH_R[ph]
            nr = rr1 - rr0
            k0, nh = (0, 8) if half == 0 else (8, nr - 8)
            banks = (pbk[0], pbk[1]) if hsub == 0 else (pbk[2], pbk[3])
            for th in range(2):
                bank = banks[th]
                if half == 0:
                    b.emit("t", lambda e, S, bank=bank, hsub=hsub, th=th, nr=nr, sl=sl:
                           e.matmul(bank[:, :],
                                    gaux[0:4 * nr, ZSL[sl] + hsub * 128:ZSL[sl] + (hsub + 1) * 128],
                                    gaux[0:4 * nr, HSUM + th * TH:HSUM + th * TH + TH],
                                    start=True, stop=False))
                for k in range(k0, k0 + nh):
                    for par in range(2):
                        src_t = HI_T if par == 0 else LO_T
                        idx2 = 2 * k + par
                        last = (half == 1 and k == k0 + nh - 1 and par == 1)
                        lastu = (th == 1 and k == k0 + nh - 1 and par == 1)
                        b.emit("t", lambda e, S, bank=bank, src_t=src_t, k=k, k0=k0, hsub=hsub,
                               idx2=idx2, th=th, last=last, pp=pp:
                               e.matmul(bank[:, :],
                                        gaux[:, src_t + pp * HB + (k - k0) * 256 + hsub * 128: src_t + pp * HB + (k - k0) * 256 + hsub * 128 + 128],
                                        at_sb[:, idx2 * T + th * TH: idx2 * T + th * TH + TH],
                                        start=False, stop=last),
                               inc="pe" if lastu else None)

        def emit_dn_evac(ph, how, hsub, guard_pe):
            cc_o = 2 * how + hsub
            banks = (pbk[0], pbk[1]) if hsub == 0 else (pbk[2], pbk[3])
            b.wait("v", "pe", guard_pe)
            b.wait("v", "dsa", b.cnt["dsa"])
            if ph > 0:
                b.wait("v", "ddp", ddp_m[(ph, how, hsub)])
            for th in range(2):
                last = (ph != 2 and th == 1)
                if ph == 0:
                    b.emit("v", lambda e, S, th=th, banks=banks:
                           e.tensor_copy(dpo[:, th * TH: th * TH + TH],
                                         banks[th][:, :]), inc="dv" if last else None)
                else:
                    b.emit("v", lambda e, S, th=th, hsub=hsub, banks=banks:
                           e.tensor_add(dpo[:, th * TH: th * TH + TH],
                                        qkvt[:, hsub * 1024 + th * TH: hsub * 1024 + th * TH + TH],
                                        banks[th][:, :]), inc="dv" if last else None)
            if ph == 2:
                b.drain("v")
                for th in range(2):
                    b.emit("v", lambda e, S, th=th, cc_o=cc_o:
                           e.tensor_add(dpo[:, th * TH: th * TH + TH],
                                        dpo[:, th * TH: th * TH + TH],
                                        x_sb[:, cc_o * T + th * TH: cc_o * T + th * TH + TH]),
                           inc="dv" if th == 1 else None)
            ms["dn_evac_dv"][hsub] = b.cnt["dv"]
            ms["dpp_guard"][hsub] = b.cnt["dv"]

        def emit_dn_store(ph, how, hsub, guard_dv):
            cc_o = 2 * how + hsub
            b.wait("g", "dv", guard_dv)
            dst_d = a2d if ph == 2 else dpd
            b.dma("g", "dsa", lambda e, S, dst_d=dst_d, cc_o=cc_o: e.dma_start(
                out=dst_d[cc_o * 128:(cc_o + 1) * 128, :],
                in_=dpo[:, 0:1024]))

        def grp_dn(i):
            _, ph, _, how, h = units[i]
            pp = i % 2
            sl, zm = zs_mark[("dn", ph, how)]
            if h == 0:
                if ph > 0:
                    for hsub in range(2):
                        b.wait("y", "dv", ms["dpp_guard"][hsub])
                        b.wait("y", "dv", ms["h_evac_dv"])
                        ddp_m[(ph, how, hsub)] = b.dma(
                            "y", "ddp", lambda e, S, how=how, hsub=hsub: e.dma_start(
                                out=qkvt[:, hsub * 1024:(hsub + 1) * 1024],
                                in_=dpd[(2 * how + hsub) * 128:(2 * how + hsub + 1) * 128, :]))
                b.wait("t", "dv", st_deq_dv[i])
                b.wait("t", "dzt", zm)
                b.wait("t", "dv", ms["hsum_done"])
                b.wait("t", "dv", max(ms["dn_evac_dv"][0], ms["h_evac_dv"]))
                dn_group(ph, how, hsub=0, half=0, pp=pp, sl=sl)
                b.wait("t", "dv", st_deq_dv[i + 1])
                dn_group(ph, how, hsub=0, half=1, pp=(i + 1) % 2)
                emit_dn_evac(ph, how, hsub=0, guard_pe=b.cnt["pe"])
                ev_dv = b.cnt["dv"]
                b.wait("t", "dv", max(ms["dn_evac_dv"][1], ms["h_evac_dv"]))
                dn_group(ph, how, hsub=1, half=0, pp=pp, sl=sl)
                zs_free[sl] = b.cnt["pe"]
                st_pe_free[i] = b.cnt["pe"]

                def post():
                    emit_dn_store(ph, how, 0, ev_dv)
                return post
            else:
                b.wait("t", "dv", st_deq_dv[i])
                dn_group(ph, how, hsub=1, half=1, pp=pp)
                st_pe_free[i] = b.cnt["pe"]
                pe_11 = b.cnt["pe"]

                def post():
                    emit_dn_evac(ph, how, hsub=1, guard_pe=pe_11)
                    emit_dn_store(ph, how, 1, b.cnt["dv"])
                    if how == 15:
                        ms["g_bank_dv"] = b.cnt["dv"]
                return post

        deq_unit(0)
        deq_unit(1)
        for i in range(len(units)):
            if units[i][0] == "gu":
                post = grp_gu(i)
            else:
                post = grp_dn(i)
            if i + 2 < len(units):
                deq_unit(i + 2)
            if post is not None:
                post()

        a2_stores = b.cnt["dsa"]
        b.wait("y", "dsa", a2_stores)
        tm_guard = [0, 0]
        sq_fin_guard = 0
        for tc in range(8):
            sl = tc % 2
            b.wait("y", "dv", tm_guard[sl])
            tm_m = b.dma("y", "dtm", lambda e, S, tc=tc, sl=sl: e.dma_start_transpose(
                gaux[:, sl * 4096:(sl + 1) * 4096], a2d[:, tc * 128:(tc + 1) * 128]))
            ro = 8 * sl
            b.wait("v", "dtm", tm_m)
            b.emit("v", lambda e, S, sl=sl, ro=ro: e.tensor_reduce(
                stf[:, ro:ro + 1], gaux[:, sl * 4096:(sl + 1) * 4096], AXX, ALU.add))
            b.wait("s", "dtm", tm_m)
            b.wait("s", "dv", sq_fin_guard)
            b.emit("s", lambda e, S, sl=sl: e.activation(
                gaux[:, 8192:12288], gaux[:, sl * 4096:(sl + 1) * 4096], AF.Square), inc="ac")
            b.wait("v", "ac", b.cnt["ac"])
            b.emit("v", lambda e, S, ro=ro: e.tensor_reduce(
                stf[:, ro + 1:ro + 2], gaux[:, 8192:12288], AXX, ALU.add), inc="dv")
            sq_fin_guard = b.cnt["dv"]
            b.drain("v")
            b.emit("v", lambda e, S, ro=ro: e.tensor_scalar(
                stf[:, ro + 2:ro + 3], stf[:, ro:ro + 1], 1.0 / C, None, ALU.mult))
            b.drain("v")
            b.emit("v", lambda e, S, ro=ro: e.tensor_mul(stf[:, ro + 3:ro + 4],
                                                         stf[:, ro + 2:ro + 3], stf[:, ro + 2:ro + 3]))
            b.drain("v")
            b.emit("v", lambda e, S, ro=ro: e.scalar_tensor_tensor(
                stf[:, ro + 4:ro + 5], stf[:, ro + 1:ro + 2], 1.0 / C, stf[:, ro + 3:ro + 4],
                ALU.mult, ALU.subtract))
            b.drain("v")
            b.emit("v", lambda e, S, ro=ro: e.tensor_scalar(
                stf[:, ro + 4:ro + 5], stf[:, ro + 4:ro + 5], EPS, None, ALU.add), inc="dv")
            b.wait("s", "dv", b.cnt["dv"])
            b.emit("s", lambda e, S, ro=ro: e.activation(stf[:, ro + 5:ro + 6],
                                                         stf[:, ro + 4:ro + 5], AF.Sqrt), inc="ac")
            b.wait("v", "ac", b.cnt["ac"])
            b.emit("v", lambda e, S, ro=ro: e.reciprocal(stf[:, ro + 6:ro + 7],
                                                         stf[:, ro + 5:ro + 6]))
            b.drain("v")
            for qv in range(4):
                fsl = qv % 2
                b.wait("v", "dfo", max(0, b.cnt["dfo"] - 16))
                src_ap = gaux[:, sl * 4096 + qv * 1024: sl * 4096 + (qv + 1) * 1024]
                b.emit("v", lambda e, S, src_ap=src_ap, ro=ro, fsl=fsl:
                       e.tensor_scalar(
                           fout[:, fsl * 1024:(fsl + 1) * 1024].rearrange(
                               "t (jb p par) -> t jb par p", jb=4, p=128, par=2),
                           src_ap.rearrange("t (jb par p) -> t jb par p", jb=4, par=2, p=128),
                           stf[:, ro + 2:ro + 3], stf[:, ro + 6:ro + 7],
                           ALU.subtract, ALU.mult), inc="dv")
                vm = b.cnt["dv"]
                b.wait("y", "dv", vm)
                b.dma("y", "dfo", lambda e, S, tc=tc, qv=qv, fsl=fsl: e.dma_start(
                    out=out_t[tc * 128:(tc + 1) * 128, qv * 1024:(qv + 1) * 1024],
                    in_=fout[:, fsl * 1024:(fsl + 1) * 1024]))
            tm_guard[sl] = b.cnt["dv"]

        b.wait("g", "dfo", b.cnt["dfo"])
        b.wait("g", "fin", b.cnt["fin"])

        with nc.Block() as blk:
            @blk.tensor
            def _(t):
                for fn in b.q["t"]:
                    fn(t, sems)

            @blk.vector
            def _(v):
                for fn in b.q["v"]:
                    fn(v, sems)

            @blk.scalar
            def _(s):
                for fn in b.q["s"]:
                    fn(s, sems)

            @blk.gpsimd
            def _(g):
                for fn in b.q["g"]:
                    fn(g, sems)

            @blk.sync
            def _(y):
                for fn in b.q["y"]:
                    fn(y, sems)

    return nc


def kernel(**inputs):
    shared, per_core = host_prep(inputs)
    nc = build_nc()
    in_maps = [dict(shared, **pc) for pc in per_core]
    res = run_bass_kernel_spmd(nc, in_maps, core_ids=list(range(NCORE)))
    out = np.empty((B, S, C), np.float32)
    for c in range(NCORE):
        i, j = c // 2, c % 2
        out[i, j * T:(j + 1) * T, :] = res.results[c]["out"]
    return out
